# version 02
# baseline (speedup 1.0000x reference)
import math
import numpy as np
import ml_dtypes
from contextlib import ExitStack

import concourse.bass as bass
import concourse.mybir as mybir
from concourse.bass import IndirectOffsetOnAxis
from concourse.bass_utils import run_bass_kernel_spmd

F32 = mybir.dt.float32
BF16 = mybir.dt.bfloat16
U32 = mybir.dt.uint32
AF = mybir.ActivationFunctionType
ALU = mybir.AluOpType
AX = mybir.AxisListType.X

D = 1024
NCORES = 8
T_FULL = 4096

_L1 = math.log(0.1 / 0.9)
_L2 = math.log(0.25 / 0.75)
_L3 = math.log(0.85 / 0.15)
_L4 = math.log(0.95 / 0.05)
SCALE0 = 64.0 ** -0.5
SCALE1 = 128.0 ** -0.5
NEG = -30000.0

CONST_COLS = ["ml1", "ml2", "ml3", "ml4", "scale0", "scale1", "half", "eps", "neg"]
CONST_VALS = {"ml1": -_L1, "ml2": -_L2, "ml3": -_L3, "ml4": -_L4,
              "scale0": SCALE0, "scale1": SCALE1, "half": 0.5, "eps": 1e-5, "neg": NEG}
CIDX = {k: i for i, k in enumerate(CONST_COLS)}



class Op:
    __slots__ = ("idx", "engine", "fn", "after", "dma", "dma_inc", "needs_inc",
                 "chan_idx", "tag")

    def __init__(self, idx, engine, fn, after, dma, tag):
        self.idx = idx
        self.engine = engine
        self.fn = fn
        self.after = after
        self.dma = dma
        self.needs_inc = False
        self.chan_idx = None
        self.tag = tag


ENGS = ("sync", "gpsimd", "vector", "scalar", "tensor")


class Prog:
    def __init__(self, nc, ctx):
        self.nc = nc
        self.ctx = ctx
        self.ops = []
        self._sb_bytes = 0

    def op(self, engine, fn, after=(), dma=False, tag=""):
        a = []
        for d in after:
            if d is None:
                continue
            di = d[1] if isinstance(d, tuple) else d
            assert isinstance(di, int) and di < len(self.ops), f"bad dep {d}"
            a.append(d)
        o = Op(len(self.ops), engine, fn, a, dma, tag)
        self.ops.append(o)
        return o.idx

    def force(self, dep):
        return ("force", dep)

    def emit(self):
        nc = self.nc
        for o in self.ops:
            for d in o.after:
                force = isinstance(d, tuple)
                dep = self.ops[d[1] if force else d]
                if dep.dma or dep.engine != o.engine or force:
                    dep.needs_inc = True
        KD = {"sync": 32, "gpsimd": 32, "vector": 4, "scalar": 4, "tensor": 4}
        self.csem = {e: self.ctx.enter_context(nc.semaphore(f"c_{e}")) for e in ENGS}
        self.dsem = {e: [self.ctx.enter_context(nc.semaphore(f"d_{e}{j}"))
                         for j in range(KD[e])] for e in ENGS}
        ccount = {e: 0 for e in ENGS}
        dcount = {e: 0 for e in ENGS}
        for o in self.ops:
            if o.dma:
                n = dcount[o.engine]
                o.chan_idx = (n % KD[o.engine], 16 * (n // KD[o.engine] + 1))
                dcount[o.engine] += 1
            elif o.needs_inc:
                ccount[o.engine] += 1
                o.chan_idx = ccount[o.engine]
        self.final_counts = dict(c=ccount, d=dcount)
        self.KD = KD

        hiwater = {}

        def emit_engine(eng_name, eng):
            for o in self.ops:
                if o.engine != eng_name:
                    continue
                want = {}
                for d in o.after:
                    force = False
                    if isinstance(d, tuple):
                        force = True
                        d = d[1]
                    dep = self.ops[d]
                    if dep.engine == eng_name and not dep.dma and not force:
                        continue
                    if dep.dma:
                        slot, val = dep.chan_idx
                        key = ("d", dep.engine, slot)
                    else:
                        assert dep.chan_idx is not None, (o.idx, o.tag, d, dep.tag)
                        key = ("c", dep.engine)
                        val = dep.chan_idx
                    want[key] = max(want.get(key, 0), val)
                if o.dma:
                    slot, val = o.chan_idx
                    if val > 16:
                        key = ("d", eng_name, slot)
                        want[key] = max(want.get(key, 0), val - 16)
                for key, val in sorted(want.items()):
                    hw = hiwater.get((eng_name, key), 0)
                    if val > hw:
                        sem = self.dsem[key[1]][key[2]] if key[0] == "d" else self.csem[key[1]]
                        eng.wait_ge(sem, val)
                        hiwater[(eng_name, key)] = val
                inst = o.fn(eng)
                if o.dma:
                    assert inst is not None, (o.idx, o.tag)
                    slot, val = o.chan_idx
                    inst.then_inc(self.dsem[eng_name][slot], 16)
                elif o.needs_inc:
                    if inst is None:
                        inst = eng.nop()
                    inst.then_inc(self.csem[eng_name], 1)

        blk = self.ctx.enter_context(nc.Block())

        @blk.sync
        def _(e):
            emit_engine("sync", e)
            for en in ENGS:
                n = self.final_counts["d"][en]
                for slot in range(min(n, self.KD[en])):
                    nfull = n // self.KD[en] + (1 if slot < n % self.KD[en] else 0)
                    e.wait_ge(self.dsem[en][slot], 16 * nfull)

        @blk.gpsimd
        def _(e):
            emit_engine("gpsimd", e)

        @blk.vector
        def _(e):
            emit_engine("vector", e)

        @blk.scalar
        def _(e):
            emit_engine("scalar", e)

        @blk.tensor
        def _(e):
            emit_engine("tensor", e)


class Ctx:

    def __init__(self, nc, ctx, T):
        self.nc = nc
        self.ctx = ctx
        self.T = T
        self.NT = T // 128
        self.sb = {}

    def sbuf(self, name, shape, dtype):
        return self.ctx.enter_context(self.nc.sbuf_tensor(name, shape, dtype))

    def psum(self, name, shape, dtype):
        return self.ctx.enter_context(self.nc.psum_tensor(name, shape, dtype))

    def cc(self, name):
        return self.sb["consts"][:, CIDX[name]:CIDX[name] + 1]



def _prep(inputs):
    f = lambda a: np.ascontiguousarray(a, dtype=np.float32)
    sh = {}
    sh["Wqkv"] = f(inputs["L0_attn_W_qkv"])
    sh["bqkv_pp"] = f(np.asarray(inputs["L0_attn_b_qkv"])[:2048].reshape(16, 128).T)
    sh["bv_row"] = f(np.asarray(inputs["L0_attn_b_qkv"])[2048:].reshape(1, 1024))
    sh["Wproj"] = f(inputs["L0_attn_W_proj"])
    sh["bproj_row"] = f(np.asarray(inputs["L0_attn_b_proj"]).reshape(1, 1024))
    sh["Wg0"] = f(inputs["L0_ffn_Wg"])
    sh["Wu0"] = f(inputs["L0_ffn_Wu"])
    sh["Wd0"] = f(inputs["L0_ffn_Wd"])
    sh["bg0_pp"] = f(np.asarray(inputs["L0_ffn_bg"]).reshape(32, 128).T)
    sh["bu0_pp"] = f(np.asarray(inputs["L0_ffn_bu"]).reshape(32, 128).T)
    sh["bd0_row"] = f(np.asarray(inputs["L0_ffn_bd"]).reshape(1, 1024))
    sh["Wqp"] = f(inputs["L0_ffn_Wq"])
    sh["keysT"] = f(np.asarray(inputs["L0_ffn_keys"]).transpose(3, 2, 0, 1).reshape(64, 512))
    sh["demb"] = f(inputs["L0_ffn_down_embed"])
    sh["uemb"] = f(inputs["L0_ffn_up_embed"])
    sh["Wq1"] = f(inputs["L1_attn_Wq"])
    sh["Wk1"] = f(inputs["L1_attn_Wk"])
    sh["Wv1"] = f(inputs["L1_attn_Wv"])
    sh["Wo1"] = f(inputs["L1_attn_Wo"])
    sh["WgTs"] = f(np.asarray(inputs["L1_ffn_Wg_s"]).T)
    sh["WuTs"] = f(np.asarray(inputs["L1_ffn_Wu_s"]).T)
    sh["WdTs"] = f(np.asarray(inputs["L1_ffn_Wd_s"]).T)
    sh["WgT"] = f(np.asarray(inputs["L1_ffn_Wg"]).transpose(0, 2, 1))
    sh["WuT"] = f(np.asarray(inputs["L1_ffn_Wu"]).transpose(0, 2, 1))
    sh["WdT"] = f(np.asarray(inputs["L1_ffn_Wd"]).transpose(0, 2, 1))
    sh["WrT"] = f(np.asarray(inputs["L1_ffn_Wr"]).T)
    sh["rb_row"] = f(np.asarray(inputs["L1_ffn_rb"]).reshape(1, 8))
    sh["ident_bf"] = np.eye(128, dtype=np.float32).astype(ml_dtypes.bfloat16)
    sh["ident_f32"] = np.eye(128, dtype=np.float32)
    cvals = np.array([[CONST_VALS[k] for k in CONST_COLS]], dtype=np.float32)
    sh["consts"] = np.tile(cvals, (128, 1))
    x = np.asarray(inputs["L0_attn_x"], dtype=np.float32)
    per_core = [{"x": np.ascontiguousarray(x[b])} for b in range(x.shape[0])]
    return sh, per_core


SHARED_SPECS = {
    "Wqkv": ([1024, 3072], F32), "bqkv_pp": ([128, 16], F32), "bv_row": ([1, 1024], F32),
    "Wproj": ([1024, 1024], F32), "bproj_row": ([1, 1024], F32),
    "Wg0": ([1024, 4096], F32), "Wu0": ([1024, 4096], F32), "Wd0": ([4096, 1024], F32),
    "bg0_pp": ([128, 32], F32), "bu0_pp": ([128, 32], F32), "bd0_row": ([1, 1024], F32),
    "Wqp": ([1024, 512], F32), "keysT": ([64, 512], F32),
    "demb": ([4096, 1024], F32), "uemb": ([4096, 1024], F32),
    "Wq1": ([1024, 1024], F32), "Wk1": ([1024, 512], F32), "Wv1": ([1024, 512], F32),
    "Wo1": ([1024, 1024], F32),
    "WgTs": ([1024, 2048], F32), "WuTs": ([1024, 2048], F32), "WdTs": ([2048, 1024], F32),
    "WgT": ([8, 1024, 2048], F32), "WuT": ([8, 1024, 2048], F32), "WdT": ([8, 2048, 1024], F32),
    "WrT": ([1024, 8], F32), "rb_row": ([1, 8], F32),
    "ident_bf": ([128, 128], BF16), "ident_f32": ([128, 128], F32),
    "consts": ([128, len(CONST_COLS)], F32),
}

def inter_specs(T):
    return {
        "x": ([T, 1024], F32),
        "xn_t": ([T, 1024], BF16),
        "xn_f": ([1024, T], BF16),
        "qk0_f": ([2048, T], BF16),
        "v0_t": ([T, 1024], BF16),
        "sims": ([T, 512], F32),
        "attn0": ([T, 1024], F32),
        "pkm": ([T, 1024], F32),
        "x1": ([T, 1024], F32),
        "xn1_f": ([1024, T], BF16),
        "q1_f": ([1024, T], BF16),
        "k1_f": ([512, T], BF16),
        "v1_t": ([T, 512], BF16),
        "cw_t": ([T, 8], F32),
        "attn1": ([T, 1024], F32),
        "moe": ([T, 1024], F32),
        "dembh": ([4096, 1024], BF16),
        "uembh": ([4096, 1024], BF16),
        "pkm_idx": ([T, 32], U32),
        "pkm_cc": ([T, 32], F32),
        "out": ([T, 1024], F32),
    }



def emit_setup(p, c, d, need_iota=False, need_identf=False):
    c.sb["consts"] = c.sbuf("c_consts", [128, len(CONST_COLS)], F32)
    c.sb["ident"] = c.sbuf("c_ident", [128, 128], BF16)
    h = {}
    h["consts"] = p.op("sync", lambda e: e.dma_start(out=c.sb["consts"][:], in_=d["consts"][:, :]), dma=True, tag="ld_consts")
    h["ident"] = p.op("sync", lambda e: e.dma_start(out=c.sb["ident"][:], in_=d["ident_bf"][:, :]), dma=True, tag="ld_ident")
    if need_identf:
        c.sb["identf"] = c.sbuf("c_identf", [128, 128], F32)
        h["identf"] = p.op("sync", lambda e: e.dma_start(out=c.sb["identf"][:], in_=d["ident_f32"][:, :]), dma=True, tag="ld_identf")
    if need_iota:
        c.sb["iota64u"] = c.sbuf("c_iota64u", [128, 64], U32)
        c.sb["iota64"] = c.sbuf("c_iota64", [128, 64], F32)
        i1 = p.op("gpsimd", lambda e: e.iota(c.sb["iota64u"][:], pattern=[[1, 64]], base=0, channel_multiplier=0), tag="iota")
        h["iota"] = p.op("vector", lambda e: e.tensor_copy(c.sb["iota64"][:], c.sb["iota64u"][:]), after=[i1], tag="iotaf")
    return h


L0PRE_PARTS = 5


def emit_l0pre(p, c, d, h0):
    T, NT = c.T, c.NT
    PARTS = L0PRE_PARTS
    wqkv = c.sbuf("a_wqkv", [128, 8 * 3072], BF16)
    wqp = c.sbuf("a_wqp", [128, 8 * 512], BF16)
    keysb = c.sbuf("a_keys", [128, 512], BF16)
    keysf = c.sbuf("a_keysf", [64, 512], F32)
    bqkv = c.sbuf("a_bqkv", [128, 16], F32)
    bvb = c.sbuf("a_bvb", [128, 1024], F32)
    xn_f = c.sbuf("a_xnf", [128, 8 * T], BF16)
    qp_f = c.sbuf("a_qpf", [128, 4 * T], BF16)
    xt = c.sbuf("a_xt", [128, 2 * 1024], F32)
    sqbuf = c.sbuf("a_sq", [128, 1024], F32)
    stat = c.sbuf("a_stat", [128, 12], F32)
    aggr = c.sbuf("a_aggr", [128, 2], F32)
    mv = c.sbuf("a_mv", [128, 4], F32)
    xnt_sb = c.sbuf("a_xnt", [128, 2 * 1024], BF16)
    ps_tp = [c.psum("a_pstp0", [128, 128], BF16), c.psum("a_pstp1", [128, 128], BF16)]
    ps_mm = c.psum("a_psmm", [128, 4 * 512], F32)
    evac = c.sbuf("a_evac", [128, 4 * 512], BF16)
    sim_sb = c.sbuf("a_sim", [128, 2 * 512], F32)

    wl = []
    for kc in range(8):
        wl.append(p.op("gpsimd", lambda e, kc=kc: e.dma_start(
            out=wqkv[:, kc * 3072:(kc + 1) * 3072],
            in_=d["Wqkv"][kc * 128:(kc + 1) * 128, :]), dma=True, tag="wqkv"))
    for kc in range(8):
        wl.append(p.op("gpsimd", lambda e, kc=kc: e.dma_start(
            out=wqp[:, kc * 512:(kc + 1) * 512],
            in_=d["Wqp"][kc * 128:(kc + 1) * 128, :]), dma=True, tag="wqp"))
    l_keys = p.op("sync", lambda e: e.dma_start(out=keysf[:], in_=d["keysT"][:, :]), dma=True, tag="keys")
    p.op("vector", lambda e: e.tensor_copy(keysb[0:64, :], keysf[:]), after=[l_keys], tag="keyscast")
    cast_keys = p.op("vector", lambda e: e.tensor_copy(keysb[64:128, :], keysf[:]), after=[l_keys], tag="keysdup")
    l_bq = p.op("sync", lambda e: e.dma_start(out=bqkv[:], in_=d["bqkv_pp"][:, :]), dma=True, tag="bqkv")
    l_bv = p.op("sync", lambda e: e.dma_start(out=bvb[:], in_=d["bv_row"][0:1, :].to_broadcast([128, 1024])), dma=True, tag="bv")

    xn_done = []
    tp_done = []
    xld = [None] * NT
    xst = [None] * NT
    for i in range(NT):
        b = (i % 2) * 1024
        dep = [xn_done[i - 2]] if i >= 2 else []
        xld[i] = p.op("sync", lambda e, i=i, b=b: e.dma_start(
            out=xt[:, b:b + 1024], in_=d["x"][i * 128:(i + 1) * 128, :]),
            after=dep, dma=True, tag=f"xld{i}")
        rs1 = p.op("vector", lambda e, b=b: e.reduce_sum(stat[:, 0:1], xt[:, b:b + 1024], axis=AX), after=[xld[i]], tag="lnsum")
        mn = p.op("vector", lambda e: e.tensor_scalar(aggr[:, 0:1], stat[:, 0:1], 1.0 / 1024, None, op0=ALU.mult), after=[p.force(rs1)], tag="lnmean")
        sqv = p.op("scalar", lambda e, b=b: e.square(sqbuf[:], xt[:, b:b + 1024]), after=[xld[i]], tag="lnsq")
        rs2 = p.op("vector", lambda e: e.reduce_sum(stat[:, 1:2], sqbuf[:], axis=AX), after=[sqv], tag="lnsum2")
        ex2 = p.op("vector", lambda e: e.tensor_scalar(aggr[:, 1:2], stat[:, 1:2], 1.0 / 1024, None, op0=ALU.mult), after=[p.force(rs2)], tag="lnex2")
        nv = p.op("vector", lambda e: e.scalar_tensor_tensor(
            out=mv[:, 2:3], in0=aggr[:, 0:1], scalar=aggr[:, 0:1], in1=aggr[:, 1:2],
            op0=ALU.mult, op1=ALU.subtract), after=[p.force(mn), p.force(ex2)], tag="lnnegvar")
        veps = p.op("vector", lambda e: e.tensor_scalar(
            mv[:, 2:3], mv[:, 2:3], -1.0, 1e-5, op0=ALU.mult, op1=ALU.add), after=[p.force(nv)], tag="lnveps")
        sq = p.op("scalar", lambda e: e.sqrt(mv[:, 3:4], mv[:, 2:3]), after=[p.force(veps)], tag="sqrt")
        rc = p.op("vector", lambda e: e.reciprocal(mv[:, 0:1], mv[:, 3:4]), after=[p.force(sq)], tag="recip")
        nm = p.op("vector", lambda e: e.scalar_tensor_tensor(
            out=mv[:, 1:2], in0=aggr[:, 0:1], scalar=-1.0, in1=mv[:, 0:1],
            op0=ALU.mult, op1=ALU.mult), after=[p.force(rc)], tag="negm")
        dep2 = [p.force(rc), p.force(nm)]
        if i >= 2:
            dep2.append(xst[i - 2])
        xn = p.op("scalar", lambda e, b=b: e.activation(
            xnt_sb[:, b:b + 1024], xt[:, b:b + 1024], AF.Identity,
            bias=mv[:, 1:2], scale=mv[:, 0:1]), after=dep2, tag=f"xn{i}")
        xn_done.append(xn)
        xst[i] = p.op("sync", lambda e, i=i, b=b: e.dma_start(
            out=d["xn_t"][i * 128:(i + 1) * 128, :], in_=xnt_sb[:, b:b + 1024]),
            after=[xn], dma=True, tag=f"xnst{i}")
        for kc in range(8 if PARTS >= 1 else 0):
            par = (i * 8 + kc) % 2
            dep3 = [xn, h0["ident"]]
            if i * 8 + kc >= 2:
                dep3.append(tp_done[-2])
            tp = p.op("tensor", lambda e, b=b, kc=kc, par=par: e.transpose(
                ps_tp[par][:], xnt_sb[:, b + kc * 128:b + kc * 128 + 128],
                c.sb["ident"][:]), after=dep3, tag="tp")
            ev = p.op("scalar", lambda e, i=i, kc=kc, par=par: e.copy(
                xn_f[:, kc * T + i * 128: kc * T + i * 128 + 128], ps_tp[par][:]),
                after=[tp], tag="tpev")
            tp_done.append(ev)
    if PARTS < 1:
        return {}
    all_xnf = tp_done[-1]
    xnf_st = []
    for kc in range(8):
        xnf_st.append(p.op("sync", lambda e, kc=kc: e.dma_start(
            out=d["xn_f"][kc * 128:(kc + 1) * 128, :], in_=xn_f[:, kc * T:(kc + 1) * T]),
            after=[all_xnf], dma=True, tag="xnfst"))

    if PARTS < 2:
        return {}
    mm_done = []
    n_ev = 0

    def proj_group(bank, mm_fn, ev_fn, extra_dep=()):
        nonlocal n_ev
        dep = list(extra_dep)
        if n_ev >= 4:
            dep.append(mm_done[n_ev - 4][1])
        mm = mm_fn(dep)
        dep_store = [mm_done[n_ev - 4][2]] if (n_ev >= 4 and mm_done[n_ev - 4][2] is not None) else []
        ev, st = ev_fn(mm, dep_store)
        mm_done.append((mm, ev, st))
        n_ev += 1
        return ev

    for ec in range(16):
        for tc in range(T // 512):
            bank = n_ev % 4

            def mk_mm(dep, ec=ec, tc=tc, bank=bank):
                first = None
                for kc in range(8):
                    o = p.op("tensor", lambda e, kc=kc, ec=ec, tc=tc, bank=bank: e.matmul(
                        ps_mm[:, bank * 512:(bank + 1) * 512],
                        wqkv[:, kc * 3072 + ec * 128: kc * 3072 + ec * 128 + 128],
                        xn_f[:, kc * T + tc * 512: kc * T + tc * 512 + 512],
                        start=(kc == 0), stop=(kc == 7)),
                        after=(dep + [all_xnf] + wl) if kc == 0 else [], tag=f"qkmm")
                return o

            def mk_ev(mm, dep_store, ec=ec, tc=tc, bank=bank):
                ev = p.op("vector", lambda e: e.tensor_scalar(
                    evac[:, bank * 512:(bank + 1) * 512],
                    ps_mm[:, bank * 512:(bank + 1) * 512],
                    bqkv[:, ec:ec + 1], None, op0=ALU.add),
                    after=[mm, l_bq] + dep_store, tag="qkev")
                st = p.op("sync", lambda e: e.dma_start(
                    out=d["qk0_f"][ec * 128:(ec + 1) * 128, tc * 512:(tc + 1) * 512],
                    in_=evac[:, bank * 512:(bank + 1) * 512]), after=[ev], dma=True, tag="qkst")
                return ev, st

            proj_group(bank, mk_mm, mk_ev)

    for i in range(NT if PARTS >= 3 else 0):
        for half in range(2):
            bank = n_ev % 4

            def mk_mm(dep, i=i, half=half, bank=bank):
                for kc in range(8):
                    o = p.op("tensor", lambda e, kc=kc, i=i, half=half, bank=bank: e.matmul(
                        ps_mm[:, bank * 512:(bank + 1) * 512],
                        xn_f[:, kc * T + i * 128: kc * T + i * 128 + 128],
                        wqkv[:, kc * 3072 + 2048 + half * 512: kc * 3072 + 2048 + (half + 1) * 512],
                        start=(kc == 0), stop=(kc == 7)), after=dep if kc == 0 else [], tag="vmm")
                return o

            def mk_ev(mm, dep_store, i=i, half=half, bank=bank):
                ev = p.op("vector", lambda e: e.tensor_tensor(
                    evac[:, bank * 512:(bank + 1) * 512],
                    ps_mm[:, bank * 512:(bank + 1) * 512],
                    bvb[:, half * 512:(half + 1) * 512], op=ALU.add),
                    after=[mm, l_bv] + dep_store, tag="vev")
                st = p.op("sync", lambda e: e.dma_start(
                    out=d["v0_t"][i * 128:(i + 1) * 128, half * 512:(half + 1) * 512],
                    in_=evac[:, bank * 512:(bank + 1) * 512]), after=[ev], dma=True, tag="vst")
                return ev, st

            proj_group(bank, mk_mm, mk_ev)

    if PARTS < 4:
        return {}
    for ec in range(4):
        for tc in range(T // 512):
            bank = n_ev % 4

            def mk_mm(dep, ec=ec, tc=tc, bank=bank):
                for kc in range(8):
                    o = p.op("tensor", lambda e, kc=kc, ec=ec, tc=tc, bank=bank: e.matmul(
                        ps_mm[:, bank * 512:(bank + 1) * 512],
                        wqp[:, kc * 512 + ec * 128: kc * 512 + ec * 128 + 128],
                        xn_f[:, kc * T + tc * 512: kc * T + tc * 512 + 512],
                        start=(kc == 0), stop=(kc == 7)), after=dep if kc == 0 else [], tag="qpmm")
                return o

            def mk_ev(mm, dep_store, ec=ec, tc=tc, bank=bank):
                ev = p.op("vector", lambda e: e.tensor_copy(
                    qp_f[:, ec * T + tc * 512: ec * T + tc * 512 + 512],
                    ps_mm[:, bank * 512:(bank + 1) * 512]), after=[mm], tag="qpev")
                return ev, None

            proj_group(bank, mk_mm, mk_ev)

    qp_ready = mm_done[-1][1]

    if PARTS < 5:
        return {}
    sim_st = []
    for i in range(NT):
        sb = (i % 2) * 512
        evs = []
        for rnd in range(2):
            mms = []
            for g4 in range(4):
                g = rnd * 4 + g4
                pp, hh = divmod(g, 4)
                rows = pp * 256 + hh * 64
                chunk, base = divmod(rows, 128)
                dep = [qp_ready, cast_keys]
                if n_ev >= 4 or rnd or i:
                    if evs:
                        dep.append(evs[-4 + g4] if len(evs) >= 4 else None)
                    elif i:
                        dep.append(sim_st[-1])
                    else:
                        dep.append(mm_done[n_ev - 4][1])
                mms.append(p.op("tensor", lambda e, g=g, i=i, g4=g4, chunk=chunk, base=base: e.matmul(
                    ps_mm[:, g4 * 512: g4 * 512 + 64],
                    qp_f[base:base + 64, chunk * T + i * 128: chunk * T + i * 128 + 128],
                    keysb[base:base + 64, g * 64:(g + 1) * 64], start=True, stop=True),
                    after=[dd for dd in dep if dd is not None], tag="simmm"))
            for g4 in range(4):
                g = rnd * 4 + g4
                evs.append(p.op("scalar", lambda e, g=g, g4=g4, sb=sb: e.copy(
                    sim_sb[:, sb + g * 64: sb + (g + 1) * 64], ps_mm[:, g4 * 512: g4 * 512 + 64]),
                    after=[mms[g4]] + ([sim_st[i - 2]] if (rnd == 0 and g4 == 0 and i >= 2) else []),
                    tag="simev"))
        st = p.op("sync", lambda e, i=i, sb=sb: e.dma_start(
            out=d["sims"][i * 128:(i + 1) * 128, :], in_=sim_sb[:, sb:sb + 512]),
            after=[evs[-1]], dma=True, tag="simst")
        sim_st.append(st)

    return {}


def emit_l0attn(p, c, d, h0):
    T, NT = c.T, c.NT
    k_f = c.sbuf("b_kf", [128, 8 * T], BF16)
    v_t = c.sbuf("b_vt", [128, NT * 1024], BF16)
    wproj = c.sbuf("b_wproj", [128, 8 * 1024], BF16)
    bprojb = c.sbuf("b_bprojb", [128, 1024], F32)
    q_sb = c.sbuf("b_q", [128, 2 * 8 * 128], BF16)
    scores = c.sbuf("b_scores", [128, T], F32)
    sg = c.sbuf("b_sg", [128, 4 * 512], BF16)
    mbuf = c.sbuf("b_m", [128, 3 * 512], BF16)
    pexp = c.sbuf("b_p", [128, T], BF16)
    pt = c.sbuf("b_pt", [128, T], BF16)
    nmx = c.sbuf("b_nmx", [128, 2], F32)
    den = c.sbuf("b_den", [128, 1], F32)
    o_sb = c.sbuf("b_o", [128, 1024], BF16)
    o_f = c.sbuf("b_of", [128, 8 * 128], BF16)
    at_sb = c.sbuf("b_at", [128, 2 * 1024], F32)
    ps_qk = [c.psum("b_psqk0", [128, 512], F32), c.psum("b_psqk1", [128, 512], F32)]
    ps_tp = [c.psum("b_pstp0", [128, 128], BF16), c.psum("b_pstp1", [128, 128], BF16)]
    ps_pv = [c.psum("b_pspv0", [128, 512], F32), c.psum("b_pspv1", [128, 512], F32)]
    ps_pr = c.psum("b_pspr", [128, 512], F32)

    kl = []
    for kc in range(8):
        kl.append(p.op("sync", lambda e, kc=kc: e.dma_start(
            out=k_f[:, kc * T:(kc + 1) * T],
            in_=d["qk0_f"][1024 + kc * 128: 1024 + (kc + 1) * 128, :]), dma=True, tag="kld"))
    vl = []
    for i in range(NT):
        vl.append(p.op("sync", lambda e, i=i: e.dma_start(
            out=v_t[:, i * 1024:(i + 1) * 1024],
            in_=d["v0_t"][i * 128:(i + 1) * 128, :]), dma=True, tag="vld"))
    wl = []
    for kc in range(8):
        wl.append(p.op("gpsimd", lambda e, kc=kc: e.dma_start(
            out=wproj[:, kc * 1024:(kc + 1) * 1024],
            in_=d["Wproj"][kc * 128:(kc + 1) * 128, :]), dma=True, tag="wprojld"))
    l_bp = p.op("sync", lambda e: e.dma_start(
        out=bprojb[:], in_=d["bproj_row"][0:1, :].to_broadcast([128, 1024])), dma=True, tag="bprojld")

    prev = {"qld": [None, None], "stt": [None, None], "exp": None, "ptev": [None, None],
            "oev": [None, None], "atst": [None, None], "sg": None, "qk_last": {}}
    for i in range(NT):
        qb = (i % 2) * 1024
        dep = [prev["qk_last"].get(i - 2)] if i >= 2 else []
        qld = p.op("sync", lambda e, i=i, qb=qb: e.dma_start(
            out=q_sb[:, qb:qb + 1024].rearrange("p (c t) -> p c t", c=8),
            in_=d["qk0_f"][0:1024, i * 128:(i + 1) * 128].rearrange("(c p) t -> p c t", p=128)),
            after=dep + ([] if i > 0 else kl), dma=True, tag=f"qld{i}")
        sps_done = []
        for nk in range(T // 512):
            par = nk % 2
            mmdep = [qld]
            if nk >= 2:
                mmdep.append(sps_done[nk - 2])
            elif i > 0:
                mmdep.append(prev["stt"][par])
            if i == 0 and nk == 0:
                mmdep += kl
            last = None
            for kc in range(8):
                last = p.op("tensor", lambda e, kc=kc, nk=nk, par=par, qb=qb: e.matmul(
                    ps_qk[par][:], q_sb[:, qb + kc * 128: qb + kc * 128 + 128],
                    k_f[:, kc * T + nk * 512: kc * T + nk * 512 + 512],
                    start=(kc == 0), stop=(kc == 7)),
                    after=mmdep if kc == 0 else [], tag="qkmm")
            prev["qk_last"][i] = last
            sgdep = [last, h0["consts"]]
            if prev["sg"] is not None:
                sgdep.append(prev["sg"])
            sgops = []
            for j, ml in enumerate(["ml1", "ml2", "ml3", "ml4"]):
                sgops.append(p.op("scalar", lambda e, j=j, ml=ml, par=par: e.activation(
                    sg[:, j * 512:(j + 1) * 512], ps_qk[par][:], AF.Sign,
                    bias=c.cc(ml), scale=c.cc("scale0")),
                    after=sgdep if j == 0 else [], tag="sg"))
            v1 = p.op("vector", lambda e: e.scalar_tensor_tensor(
                out=mbuf[:, 0:512], in0=sg[:, 512:1024], scalar=3.5, in1=sg[:, 0:512],
                op0=ALU.mult, op1=ALU.add), after=[sgops[-1]], tag="spsv1")
            v2 = p.op("vector", lambda e: e.tensor_add(
                mbuf[:, 512:1024], sg[:, 1024:1536], sg[:, 1536:2048]), tag="spsv2")
            prev["sg"] = v2
            v3 = p.op("vector", lambda e: e.scalar_tensor_tensor(
                out=mbuf[:, 0:512], in0=mbuf[:, 512:1024], scalar=2.5, in1=mbuf[:, 0:512],
                op0=ALU.mult, op1=ALU.add), tag="spsv3")
            v4 = p.op("vector", lambda e: e.tensor_scalar(
                mbuf[:, 1024:1536], mbuf[:, 0:512], 0.1, 1.05, op0=ALU.mult, op1=ALU.add), tag="spsv4")
            stt = p.op("vector", lambda e, nk=nk, par=par: e.scalar_tensor_tensor(
                out=scores[:, nk * 512:(nk + 1) * 512], in0=ps_qk[par][:],
                scalar=c.cc("scale0"), in1=mbuf[:, 1024:1536],
                op0=ALU.mult, op1=ALU.mult), tag="spsstt")
            prev["stt"][par] = stt
            sps_done.append(stt)
        mx = p.op("vector", lambda e: e.reduce_max(
            nmx[:, 0:1], scores[:], axis=AX, negate=True), after=[sps_done[-1]], tag="mx")
        expdep = [p.force(mx)]
        if prev["exp"] is not None:
            expdep.append(prev["exp"])
        ex = p.op("scalar", lambda e: e.activation(
            pexp[:], scores[:], AF.Exp, bias=nmx[:, 0:1], accum_out=den[:]),
            after=expdep, tag="exp")
        rc = p.op("vector", lambda e: e.reciprocal(nmx[:, 1:2], den[:]), after=[ex], tag="rcp")
        tpev = []
        for c32 in range(NT):
            par = c32 % 2
            dep2 = [ex]
            if c32 >= 2:
                dep2.append(tpev[c32 - 2])
            elif prev["ptev"][par] is not None:
                dep2.append(prev["ptev"][par])
            tp = p.op("tensor", lambda e, c32=c32, par=par: e.transpose(
                ps_tp[par][:], pexp[:, c32 * 128:(c32 + 1) * 128], c.sb["ident"][:]),
                after=dep2, tag="ptp")
            ev = p.op("scalar", lambda e, c32=c32, par=par: e.copy(
                pt[:, c32 * 128:(c32 + 1) * 128], ps_tp[par][:]), after=[tp], tag="ptev")
            tpev.append(ev)
        prev["ptev"] = [tpev[-2], tpev[-1]]
        prev["exp"] = tpev[-1]
        for half in range(2):
            dep3 = [tpev[-1]] + (vl if i == 0 else []) + \
                   ([prev["oev"][half]] if prev["oev"][half] is not None else [])
            last = None
            for c32 in range(NT):
                last = p.op("tensor", lambda e, c32=c32, half=half: e.matmul(
                    ps_pv[half][:], pt[:, c32 * 128:(c32 + 1) * 128],
                    v_t[:, c32 * 1024 + half * 512: c32 * 1024 + (half + 1) * 512],
                    start=(c32 == 0), stop=(c32 == NT - 1)),
                    after=dep3 if c32 == 0 else [], tag="pvmm")
            oev = p.op("scalar", lambda e, half=half: e.activation(
                o_sb[:, half * 512:(half + 1) * 512], ps_pv[half][:], AF.Identity,
                scale=nmx[:, 1:2]), after=[last, p.force(rc)], tag="oev")
            prev["oev"][half] = oev
        ofev = []
        for kc in range(8):
            par = kc % 2
            dep4 = [prev["oev"][kc // 4]]
            if kc >= 2:
                dep4.append(ofev[kc - 2])
            else:
                dep4.append(tpev[-2 + kc])
            tp = p.op("tensor", lambda e, kc=kc, par=par: e.transpose(
                ps_tp[par][:], o_sb[:, kc * 128:(kc + 1) * 128], c.sb["ident"][:]),
                after=dep4, tag="otp")
            ev = p.op("scalar", lambda e, kc=kc, par=par: e.copy(
                o_f[:, kc * 128:(kc + 1) * 128], ps_tp[par][:]), after=[tp], tag="otev")
            ofev.append(ev)
        prev["ptev"] = [ofev[-2], ofev[-1]]
        ab = (i % 2) * 1024
        for half in range(2):
            dep5 = [ofev[-1]] + (wl if i == 0 else []) + \
                   ([prev["atst"][half]] if prev["atst"][half] is not None else [])
            if prev.get("pr_ev") is not None:
                dep5.append(prev["pr_ev"])
            last = None
            for kc in range(8):
                last = p.op("tensor", lambda e, kc=kc, half=half: e.matmul(
                    ps_pr[:], o_f[:, kc * 128:(kc + 1) * 128],
                    wproj[:, kc * 1024 + half * 512: kc * 1024 + (half + 1) * 512],
                    start=(kc == 0), stop=(kc == 7)),
                    after=dep5 if kc == 0 else [], tag="prmm")
            ev = p.op("vector", lambda e, half=half, ab=ab: e.tensor_tensor(
                at_sb[:, ab + half * 512: ab + (half + 1) * 512], ps_pr[:],
                bprojb[:, half * 512:(half + 1) * 512], op=ALU.add),
                after=[last, l_bp], tag="prev")
            prev["pr_ev"] = ev
            st = p.op("sync", lambda e, i=i, half=half, ab=ab: e.dma_start(
                out=d["attn0"][i * 128:(i + 1) * 128, half * 512:(half + 1) * 512],
                in_=at_sb[:, ab + half * 512: ab + (half + 1) * 512]),
                after=[ev], dma=True, tag="atst")
            prev["atst"][half] = st


def emit_l0pkm(p, c, d, h0):
    T, NT = c.T, c.NT
    sm = c.sbuf("p_sm", [128, 2 * 512], F32)
    xnt = c.sbuf("p_xnt", [128, 2 * 1024], BF16)
    mx8 = c.sbuf("p_mx8", [128, 64], F32)
    ixv = c.sbuf("p_ixv", [128, 64], F32)
    scr64 = c.sbuf("p_scr64", [128, 64], F32)
    asc = c.sbuf("p_asc", [128, 64], F32)
    aix = c.sbuf("p_aix", [128, 64], F32)
    xx64 = c.sbuf("p_xx64", [128, 8], F32)
    sc8 = c.sbuf("p_sc8", [128, 32], F32)
    idxf = c.sbuf("p_idxf", [128, 32], F32)
    idxu = c.sbuf("p_idxu", [128, 2 * 32], U32)
    nmh = c.sbuf("p_nmh", [128, 4], F32)
    ex8 = c.sbuf("p_ex8", [128, 32], F32)
    zz = c.sbuf("p_z", [128, 4], F32)
    zr = c.sbuf("p_zr", [128, 4], F32)
    sm8 = c.sbuf("p_sm8", [128, 32], F32)
    wsel = c.sbuf("p_wsel", [128, 32], F32)
    sw = c.sbuf("p_sw", [128, 32], F32)
    ccf = c.sbuf("p_cc", [128, 2 * 32], F32)
    deg = c.sbuf("p_deg", [128, 4 * 1024], BF16)
    ueg = c.sbuf("p_ueg", [128, 8 * 1024], BF16)
    dscr = c.sbuf("p_dscr", [128, 1024], BF16)
    acc = c.sbuf("p_acc", [128, 2 * 1024], F32)

    tbl = []
    for t8 in range(8):
        tbl.append(p.op("gpsimd", lambda e, t8=t8: e.dma_start(
            out=d["dembh"][t8 * 512:(t8 + 1) * 512, :],
            in_=d["demb"][t8 * 512:(t8 + 1) * 512, :]), dma=True, tag="dembcast"))
        tbl.append(p.op("gpsimd", lambda e, t8=t8: e.dma_start(
            out=d["uembh"][t8 * 512:(t8 + 1) * 512, :],
            in_=d["uemb"][t8 * 512:(t8 + 1) * 512, :]), dma=True, tag="uembcast"))

    prev = {"dot": {}, "fma": {}, "accst": [None, None], "idxcp": None, "s1ex": {}}
    bcreg = {}

    def bc(e):
        if "r" not in bcreg:
            bcreg["r"] = e.to_reg(4095)
        return bcreg["r"]
    for i in range(NT):
        sb = (i % 2) * 512
        xb = (i % 2) * 1024
        ib = (i % 2) * 32
        smdep = [prev["s1ex"][i - 2]] if i >= 2 else []
        smld = p.op("sync", lambda e, i=i, sb=sb: e.dma_start(
            out=sm[:, sb:sb + 512], in_=d["sims"][i * 128:(i + 1) * 128, :]),
            after=smdep, dma=True, tag="smld")
        xndep = [prev["dot"].get((i - 2, 31))] if i >= 2 else []
        xnld = p.op("sync", lambda e, i=i, xb=xb: e.dma_start(
            out=xnt[:, xb:xb + 1024], in_=d["xn_t"][i * 128:(i + 1) * 128, :]),
            after=xndep, dma=True, tag="xnld")
        mxops = []
        for g in range(8):
            mxops.append(p.op("vector", lambda e, g=g, sb=sb: e.max(
                out=mx8[:, g * 8:(g + 1) * 8], in_=sm[:, sb + g * 64: sb + (g + 1) * 64]),
                after=[smld] if g == 0 else [], tag="s1max"))
        bar1 = mxops[-1]
        exops = []
        for g in range(8):
            for s_ in range(8):
                exops.append(p.op("vector", lambda e, g=g, s_=s_, sb=sb: e.scalar_tensor_tensor(
                    out=scr64[:], in0=sm[:, sb + g * 64: sb + (g + 1) * 64],
                    scalar=mx8[:, g * 8 + s_: g * 8 + s_ + 1], in1=c.sb["iota64"][:],
                    op0=ALU.is_equal, op1=ALU.mult,
                    accum_out=ixv[:, g * 8 + s_: g * 8 + s_ + 1]),
                    after=[p.force(bar1), h0["iota"]] if (g == 0 and s_ == 0) else [], tag="s1ex"))
        bar2 = exops[-1]
        prev["s1ex"][i] = bar2
        s2ops = []
        for h in range(4):
            gx, gy = h, 4 + h
            a1 = p.op("vector", lambda e, gx=gx: e.tensor_scalar(
                xx64[:], ixv[:, gx * 8:(gx + 1) * 8], 64.0, None, op0=ALU.mult),
                after=[p.force(bar2)] if h == 0 else [], tag="xx64")
            for i8 in range(8):
                p.op("vector", lambda e, i8=i8, gx=gx, gy=gy: e.tensor_scalar(
                    asc[:, i8 * 8:(i8 + 1) * 8], mx8[:, gy * 8:(gy + 1) * 8],
                    mx8[:, gx * 8 + i8: gx * 8 + i8 + 1], None, op0=ALU.add), tag="ascb")
                p.op("vector", lambda e, i8=i8, gy=gy: e.tensor_scalar(
                    aix[:, i8 * 8:(i8 + 1) * 8], ixv[:, gy * 8:(gy + 1) * 8],
                    xx64[:, i8:i8 + 1], None, op0=ALU.add), tag="aixb")
            m2 = p.op("vector", lambda e, h=h: e.max(
                out=sc8[:, h * 8:(h + 1) * 8], in_=asc[:]), tag="s2max")
            for s_ in range(8):
                s2ops.append(p.op("vector", lambda e, h=h, s_=s_: e.scalar_tensor_tensor(
                    out=scr64[:], in0=asc[:], scalar=sc8[:, h * 8 + s_: h * 8 + s_ + 1],
                    in1=aix[:], op0=ALU.is_equal, op1=ALU.mult,
                    accum_out=idxf[:, h * 8 + s_: h * 8 + s_ + 1]),
                    after=[p.force(m2)] if s_ == 0 else [], tag="s2ex"))
            nmop = p.op("vector", lambda e, h=h: e.tensor_scalar(
                nmh[:, h:h + 1], sc8[:, h * 8:h * 8 + 1], -1.0, None, op0=ALU.mult), tag="nmh")
        bar3 = nmop
        depi = [prev["fma"].get((i - 2, 31)), p.force(bar3)]
        idxcp = p.op("vector", lambda e, ib=ib: e.tensor_copy(idxu[:, ib:ib + 32], idxf[:]),
                     after=depi, tag="idxu")
        eops = []
        for h in range(4):
            eops.append(p.op("scalar", lambda e, h=h: e.activation(
                ex8[:, h * 8:(h + 1) * 8], sc8[:, h * 8:(h + 1) * 8], AF.Exp,
                bias=nmh[:, h:h + 1], accum_out=zz[:, h:h + 1]),
                after=[bar3], tag="hexp"))
        rcz = p.op("vector", lambda e: e.reciprocal(zr[:], zz[:]), after=[eops[-1]], tag="zrec")
        smops = []
        for h in range(4):
            smops.append(p.op("vector", lambda e, h=h: e.tensor_scalar(
                sm8[:, h * 8:(h + 1) * 8], ex8[:, h * 8:(h + 1) * 8],
                zr[:, h:h + 1], None, op0=ALU.mult),
                after=[p.force(rcz)] if h == 0 else [], tag="smsc"))
        dots = []
        for s_ in range(32):
            gb = (s_ % 4) * 1024
            depg = [idxcp] + tbl if (i == 0 and s_ == 0) else [idxcp]
            if s_ >= 4:
                depg.append(dots[s_ - 4])
            elif prev["dot"].get((i - 1, 28 + s_)) is not None:
                depg.append(prev["dot"][(i - 1, 28 + s_)])
            g1 = p.op("gpsimd", lambda e, s_=s_, gb=gb, ib=ib: e.indirect_dma_start(
                out=deg[:, gb:gb + 1024], out_offset=None, in_=d["dembh"][:, :],
                in_offset=IndirectOffsetOnAxis(idxu[:, ib + s_: ib + s_ + 1], axis=0),
                bounds_check=bc(e), oob_is_err=False),
                after=depg, dma=True, tag="deg")
            dot = p.op("vector", lambda e, s_=s_, gb=gb, xb=xb: e.scalar_tensor_tensor(
                out=dscr[:], in0=deg[:, gb:gb + 1024], scalar=0.0,
                in1=xnt[:, xb:xb + 1024], op0=ALU.bypass, op1=ALU.mult,
                accum_out=wsel[:, s_:s_ + 1]), after=[g1, xnld], tag="dot")
            dots.append(dot)
            prev["dot"][(i, s_)] = dot
        sil = p.op("scalar", lambda e: e.activation(sw[:], wsel[:], AF.Silu),
                   after=[dots[-1]], tag="silu")
        ccop = p.op("vector", lambda e, ib=ib: e.tensor_mul(ccf[:, ib:ib + 32], sw[:], sm8[:]),
                    after=[sil, p.force(smops[-1])] + ([prev["fma"].get((i - 2, 31))] if i >= 2 else []),
                    tag="cc")
        for s_ in range(32):
            ub = (s_ % 8) * 1024
            depg = [idxcp]
            if s_ >= 8:
                depg.append(prev["fma"][(i, s_ - 8)])
            elif prev["fma"].get((i - 1, 24 + s_)) is not None:
                depg.append(prev["fma"][(i - 1, 24 + s_)])
            g2 = p.op("gpsimd", lambda e, s_=s_, ub=ub, ib=ib: e.indirect_dma_start(
                out=ueg[:, ub:ub + 1024], out_offset=None, in_=d["uembh"][:, :],
                in_offset=IndirectOffsetOnAxis(idxu[:, ib + s_: ib + s_ + 1], axis=0),
                bounds_check=bc(e), oob_is_err=False),
                after=depg, dma=True, tag="ueg")
            depf = [g2, p.force(ccop)] if s_ == 0 else [g2]
            if s_ == 0 and prev["accst"][i % 2] is not None:
                depf.append(prev["accst"][i % 2])
            if s_ == 0:
                fma = p.op("vector", lambda e, ub=ub, xb=xb, ib=ib, s_=s_: e.tensor_scalar(
                    acc[:, xb:xb + 1024], ueg[:, ub:ub + 1024],
                    ccf[:, ib + s_: ib + s_ + 1], None, op0=ALU.mult),
                    after=depf, tag="fma0")
            else:
                fma = p.op("vector", lambda e, ub=ub, xb=xb, ib=ib, s_=s_: e.scalar_tensor_tensor(
                    out=acc[:, xb:xb + 1024], in0=ueg[:, ub:ub + 1024],
                    scalar=ccf[:, ib + s_: ib + s_ + 1], in1=acc[:, xb:xb + 1024],
                    op0=ALU.mult, op1=ALU.add), after=depf, tag="fma")
            prev["fma"][(i, s_)] = fma
        st = p.op("sync", lambda e, i=i, xb=xb: e.dma_start(
            out=d["pkm"][i * 128:(i + 1) * 128, :], in_=acc[:, xb:xb + 1024]),
            after=[prev["fma"][(i, 31)]], dma=True, tag="pkmst")
        prev["accst"][i % 2] = st


def emit_l0mlp(p, c, d, h0):
    T, NT = c.T, c.NT
    NTC = T // 512
    wg = c.sbuf("m_wg", [128, 8 * 2048], BF16)
    wu = c.sbuf("m_wu", [128, 8 * 2048], BF16)
    wd = c.sbuf("m_wd", [128, 16 * 1024], BF16)
    bg = c.sbuf("m_bg", [128, 32], F32)
    bu = c.sbuf("m_bu", [128, 32], F32)
    bdb = c.sbuf("m_bdb", [128, 1024], F32)
    xnf = c.sbuf("m_xnf", [128, 2 * 4096], BF16)
    gbuf = c.sbuf("m_g", [128, 512], BF16)
    abuf = c.sbuf("m_a", [128, 16 * 512], BF16)
    rx = c.sbuf("m_rx", [128, 2 * 1024], F32)
    rat = c.sbuf("m_rat", [128, 2 * 1024], F32)
    rpk = c.sbuf("m_rpk", [128, 2 * 1024], F32)
    prebuf = c.sbuf("m_pre", [128, 2 * 1024], F32)
    out_sb = c.sbuf("m_out", [128, 4096], F32)
    ps_gu = [c.psum("m_psg0", [128, 512], F32), c.psum("m_psg1", [128, 512], F32)]
    ps_dn = [c.psum("m_psd0", [128, 512], F32), c.psum("m_psd1", [128, 512], F32)]

    l_bg = p.op("sync", lambda e: e.dma_start(out=bg[:], in_=d["bg0_pp"][:, :]), dma=True, tag="bg")
    l_bu = p.op("sync", lambda e: e.dma_start(out=bu[:], in_=d["bu0_pp"][:, :]), dma=True, tag="bu")
    l_bd = p.op("sync", lambda e: e.dma_start(out=bdb[:], in_=d["bd0_row"][0:1, :].to_broadcast([128, 1024])), dma=True, tag="bd")

    pre_st = {}
    pre_v = {}
    for i in range(NT):
        b = (i % 2) * 1024
        dep = [pre_v.get(i - 2)]
        l1 = p.op("sync", lambda e, i=i, b=b: e.dma_start(
            out=rx[:, b:b + 1024], in_=d["x"][i * 128:(i + 1) * 128, :]), after=dep, dma=True, tag="rx")
        l2 = p.op("sync", lambda e, i=i, b=b: e.dma_start(
            out=rat[:, b:b + 1024], in_=d["attn0"][i * 128:(i + 1) * 128, :]), after=dep, dma=True, tag="rat")
        l3 = p.op("sync", lambda e, i=i, b=b: e.dma_start(
            out=rpk[:, b:b + 1024], in_=d["pkm"][i * 128:(i + 1) * 128, :]), after=dep, dma=True, tag="rpk")
        sdep = [pre_st.get(i - 2)]
        v1 = p.op("vector", lambda e, b=b: e.tensor_add(
            prebuf[:, b:b + 1024], rx[:, b:b + 1024], rat[:, b:b + 1024]),
            after=[l1, l2] + sdep, tag="pre1")
        p.op("vector", lambda e, b=b: e.tensor_add(
            prebuf[:, b:b + 1024], prebuf[:, b:b + 1024], rpk[:, b:b + 1024]), after=[l3], tag="pre2")
        v3 = p.op("vector", lambda e, b=b: e.tensor_add(
            prebuf[:, b:b + 1024], prebuf[:, b:b + 1024], bdb[:]), after=[l_bd], tag="pre3")
        pre_v[i] = v3
        pre_st[i] = p.op("sync", lambda e, i=i, b=b: e.dma_start(
            out=d["x1"][i * 128:(i + 1) * 128, :], in_=prebuf[:, b:b + 1024]),
            after=[v3], dma=True, tag="prest")

    a_done_hist = {}
    dn_seq_hist = {}
    st_hist = {}
    for half in range(2):
        wdep = [dn_seq_hist[(half - 1, NTC - 1)][-1]] if half else []
        wl = []
        for kc in range(8):
            wl.append(p.op("gpsimd", lambda e, kc=kc, half=half: e.dma_start(
                out=wg[:, kc * 2048:(kc + 1) * 2048],
                in_=d["Wg0"][kc * 128:(kc + 1) * 128, half * 2048:(half + 1) * 2048]),
                after=wdep, dma=True, tag="wg"))
            wl.append(p.op("gpsimd", lambda e, kc=kc, half=half: e.dma_start(
                out=wu[:, kc * 2048:(kc + 1) * 2048],
                in_=d["Wu0"][kc * 128:(kc + 1) * 128, half * 2048:(half + 1) * 2048]),
                after=wdep, dma=True, tag="wu"))
        for ic in range(16):
            wl.append(p.op("gpsimd", lambda e, ic=ic, half=half: e.dma_start(
                out=wd[:, ic * 1024:(ic + 1) * 1024],
                in_=d["Wd0"][half * 2048 + ic * 128: half * 2048 + ic * 128 + 128, :]),
                after=wdep, dma=True, tag="wd"))
        for tc in range(NTC):
            xb = (tc % 2) * 4096
            hc, tcp = (half, tc - 2) if tc >= 2 else (half - 1, NTC - 2 + tc)
            xnd = [a_done_hist.get((hc, tcp))]
            xnld = p.op("sync", lambda e, tc=tc, xb=xb: e.dma_start(
                out=xnf[:, xb:xb + 4096].rearrange("p (c t) -> p c t", c=8),
                in_=d["xn_f"][:, tc * 512:(tc + 1) * 512].rearrange("(c p) t -> p c t", p=128)),
                after=xnd, dma=True, tag="xnld")
            aev_list = []
            for ic in range(16):
                par = ic % 2
                if ic >= 2:
                    guev = aev_list[ic - 2]
                else:
                    guev = a_done_hist.get((half, tc - 1)) or (a_done_hist.get((half - 1, NTC - 1)) if half else None)
                mmdep = [xnld] + (wl if (tc == 0 and ic == 0) else []) + ([guev] if guev else [])
                adep = []
                if ic == 0:
                    pk = (half, tc - 1) if tc else (half - 1, NTC - 1)
                    if pk in dn_seq_hist:
                        adep.append(dn_seq_hist[pk][-1])
                last = None
                for kc in range(8):
                    last = p.op("tensor", lambda e, kc=kc, ic=ic, par=par, xb=xb: e.matmul(
                        ps_gu[par][:], wg[:, kc * 2048 + ic * 128: kc * 2048 + ic * 128 + 128],
                        xnf[:, xb + kc * 512: xb + kc * 512 + 512],
                        start=(kc == 0), stop=(kc == 7)),
                        after=(mmdep + adep) if kc == 0 else [], tag="gmm")
                gev = p.op("scalar", lambda e, ic=ic, par=par, half=half: e.activation(
                    gbuf[:], ps_gu[par][:], AF.Silu, bias=bg[:, half * 16 + ic: half * 16 + ic + 1]),
                    after=[last, l_bg], tag="gev")
                for kc in range(8):
                    last = p.op("tensor", lambda e, kc=kc, ic=ic, par=par, xb=xb: e.matmul(
                        ps_gu[par][:], wu[:, kc * 2048 + ic * 128: kc * 2048 + ic * 128 + 128],
                        xnf[:, xb + kc * 512: xb + kc * 512 + 512],
                        start=(kc == 0), stop=(kc == 7)),
                        after=[gev] if kc == 0 else [], tag="umm")
                aev = p.op("vector", lambda e, ic=ic, par=par, half=half: e.scalar_tensor_tensor(
                    out=abuf[:, ic * 512:(ic + 1) * 512], in0=ps_gu[par][:],
                    scalar=bu[:, half * 16 + ic: half * 16 + ic + 1], in1=gbuf[:],
                    op0=ALU.add, op1=ALU.mult),
                    after=[last, l_bu, p.force(gev)] + adep, tag="aev")
                aev_list.append(aev)
            a_done = aev_list[-1]
            a_done_hist[(half, tc)] = a_done
            dn_seq = []
            for piece in range(8):
                tt, nh = divmod(piece, 2)
                par = piece % 2
                dndep = [a_done]
                if piece >= 2:
                    dndep.append(dn_seq[piece - 2])
                else:
                    pk = (half, tc - 1) if tc else (half - 1, NTC - 1)
                    if pk in dn_seq_hist:
                        dndep.append(dn_seq_hist[pk][piece - 2])
                last = None
                for ic in range(16):
                    last = p.op("tensor", lambda e, ic=ic, tt=tt, nh=nh, par=par: e.matmul(
                        ps_dn[par][:], abuf[:, ic * 512 + tt * 128: ic * 512 + tt * 128 + 128],
                        wd[:, ic * 1024 + nh * 512: ic * 1024 + (nh + 1) * 512],
                        start=(ic == 0), stop=(ic == 15)),
                        after=dndep if ic == 0 else [], tag="dmm")
                ob = tt * 1024 + nh * 512
                evdep = [last]
                if piece == 0 and st_hist.get((half, tc - 1)) is not None:
                    evdep.append(st_hist[(half, tc - 1)])
                if piece == 0 and half == 1 and tc == 0 and st_hist.get((0, NTC - 1)) is not None:
                    evdep.append(st_hist[(0, NTC - 1)])
                ev = p.op("vector", lambda e, par=par, ob=ob: e.tensor_copy(
                    out_sb[:, ob:ob + 512], ps_dn[par][:]), after=evdep, tag="dev")
                dn_seq.append(ev)
            dn_seq_hist[(half, tc)] = dn_seq
            sdep = [dn_seq[-1]] + [pre_st[tc * 4 + j] for j in range(4)]
            st = p.op("gpsimd", lambda e, tc=tc: e.dma_start(
                out=d["x1"][tc * 512:(tc + 1) * 512, :].rearrange("(a p) q -> p a q", p=128),
                in_=out_sb[:].rearrange("p (a q) -> p a q", a=4),
                accum_op=ALU.add), after=sdep, dma=True, tag="x1acc")
            st_hist[(half, tc)] = st


def emit_l1pre(p, c, d, h0):
    T, NT = c.T, c.NT
    wq = c.sbuf("e_wq", [128, 8 * 1024], BF16)
    wk = c.sbuf("e_wk", [128, 8 * 512], BF16)
    wv = c.sbuf("e_wv", [128, 8 * 512], BF16)
    wr = c.sbuf("e_wr", [128, 8 * 8], BF16)
    rbb = c.sbuf("e_rbb", [128, 8], F32)
    xn_f = c.sbuf("e_xnf", [128, 8 * T], BF16)
    x1t = c.sbuf("e_x1t", [128, 2 * 1024], F32)
    sqbuf = c.sbuf("e_sq", [128, 1024], F32)
    stat = c.sbuf("e_stat", [128, 12], F32)
    aggr = c.sbuf("e_aggr", [128, 2], F32)
    mv = c.sbuf("e_mv", [128, 4], F32)
    xnt_sb = c.sbuf("e_xnt", [128, 2 * 1024], BF16)
    iot = c.sbuf("e_iot", [128, T], U32)
    cosf = c.sbuf("e_cos", [128, T], F32)
    sinm = c.sbuf("e_sinm", [128, T], F32)
    rscr = c.sbuf("e_rscr", [128, T], F32)
    ivf = c.sbuf("e_ivf", [128, 4], F32)
    rope_a = c.sbuf("e_ra", [128, 512], BF16)
    rope_b = c.sbuf("e_rb", [128, 512], BF16)
    evac = c.sbuf("e_evac", [128, 4 * 512], BF16)
    logits = c.sbuf("e_log", [128, 8], F32)
    probs = c.sbuf("e_pr", [128, 8], F32)
    top2 = c.sbuf("e_t2", [128, 8], F32)
    cw_sb = c.sbuf("e_cw", [128, 2 * 8], F32)
    ps_tp = [c.psum("e_pstp0", [128, 128], BF16), c.psum("e_pstp1", [128, 128], BF16)]
    ps_mm = c.psum("e_psmm", [128, 4 * 512], F32)
    ps_r = c.psum("e_psr", [128, 128], F32)

    wl = []
    for kc in range(8):
        wl.append(p.op("gpsimd", lambda e, kc=kc: e.dma_start(
            out=wq[:, kc * 1024:(kc + 1) * 1024],
            in_=d["Wq1"][kc * 128:(kc + 1) * 128, :]), dma=True, tag="wq1"))
        wl.append(p.op("gpsimd", lambda e, kc=kc: e.dma_start(
            out=wk[:, kc * 512:(kc + 1) * 512],
            in_=d["Wk1"][kc * 128:(kc + 1) * 128, :]), dma=True, tag="wk1"))
        wl.append(p.op("gpsimd", lambda e, kc=kc: e.dma_start(
            out=wv[:, kc * 512:(kc + 1) * 512],
            in_=d["Wv1"][kc * 128:(kc + 1) * 128, :]), dma=True, tag="wv1"))
        wl.append(p.op("gpsimd", lambda e, kc=kc: e.dma_start(
            out=wr[:, kc * 8:(kc + 1) * 8],
            in_=d["WrT"][kc * 128:(kc + 1) * 128, :]), dma=True, tag="wr1"))
    l_rb = p.op("sync", lambda e: e.dma_start(
        out=rbb[:], in_=d["rb_row"][0:1, :].to_broadcast([128, 8])), dma=True, tag="rb")

    g1 = p.op("gpsimd", lambda e: e.iota(iot[:, 0:1], pattern=[[0, 1]], base=0,
                                         channel_multiplier=1), tag="iotp")
    g2 = None
    v1 = p.op("vector", lambda e: e.tensor_copy(ivf[:, 0:1], iot[:, 0:1]), after=[g1], tag="ivc")
    v2a = p.op("vector", lambda e: e.tensor_scalar(
        ivf[:, 1:2], ivf[:, 0:1], 64.0, -64.0, op0=ALU.is_ge, op1=ALU.mult),
        after=[p.force(v1)], tag="ivge")
    v2 = p.op("vector", lambda e: e.tensor_add(
        ivf[:, 1:2], ivf[:, 1:2], ivf[:, 0:1]), after=[p.force(v2a)], tag="ivmod")
    v3 = p.op("vector", lambda e: e.tensor_scalar(
        ivf[:, 1:2], ivf[:, 1:2], -math.log(10000.0) / 64.0, None, op0=ALU.mult),
        after=[p.force(v2)], tag="ivsc")
    e1 = p.op("scalar", lambda e: e.activation(ivf[:, 0:1], ivf[:, 1:2], AF.Exp),
              after=[v3], tag="ivexp")
    g2 = p.op("gpsimd", lambda e: e.iota(iot[:], pattern=[[1, T]], base=0,
                                          channel_multiplier=0), after=[v1], tag="iott")
    v4 = p.op("vector", lambda e: e.tensor_copy(rscr[:], iot[:]), after=[g2], tag="tf")
    v5 = p.op("vector", lambda e: e.tensor_scalar(
        rscr[:], rscr[:], ivf[:, 0:1], None, op0=ALU.mult), after=[p.force(e1)], tag="xmul")
    v6 = p.op("vector", lambda e: e.tensor_scalar(
        cosf[:], rscr[:], 1.0 / (2 * math.pi), 0.5, op0=ALU.mult, op1=ALU.add), tag="ydiv")
    v7 = p.op("vector", lambda e: e.tensor_copy(iot[:], cosf[:]), tag="ytrunc")
    v8 = p.op("vector", lambda e: e.tensor_copy(cosf[:], iot[:]), tag="yback")
    v9 = p.op("vector", lambda e: e.scalar_tensor_tensor(
        out=rscr[:], in0=cosf[:], scalar=-2 * math.pi, in1=rscr[:],
        op0=ALU.mult, op1=ALU.add), tag="rred")
    sn = p.op("scalar", lambda e: e.activation(sinm[:], rscr[:], AF.Sin), after=[v9], tag="sin")
    sgn = p.op("gpsimd", lambda e: e.memset(ivf[:, 2:3], 1.0), tag="sgn1")
    sgn2 = p.op("gpsimd", lambda e: e.memset(ivf[0:64, 2:3], -1.0), after=[sgn], tag="sgn2")
    sm = p.op("vector", lambda e: e.tensor_scalar(
        sinm[:], sinm[:], ivf[:, 2:3], None, op0=ALU.mult), after=[sn, sgn2], tag="sinm")
    ch = p.op("scalar", lambda e: e.activation(cosf[:], rscr[:], AF.Sin, scale=c.cc("half")),
              after=[v9, h0["consts"], p.force(v8)], tag="sinh")
    ch2 = p.op("scalar", lambda e: e.square(cosf[:], cosf[:]), tag="sq")
    ch3 = p.op("vector", lambda e: e.tensor_scalar(
        cosf[:], cosf[:], -2.0, 1.0, op0=ALU.mult, op1=ALU.add), after=[ch2], tag="cosf")
    rope_tabs = [sm, ch3]

    xn_done = []
    tp_done = []
    for i in range(NT):
        b = (i % 2) * 1024
        dep = [xn_done[i - 2]] if i >= 2 else []
        xld = p.op("sync", lambda e, i=i, b=b: e.dma_start(
            out=x1t[:, b:b + 1024], in_=d["x1"][i * 128:(i + 1) * 128, :]),
            after=dep, dma=True, tag=f"x1ld{i}")
        rs1 = p.op("vector", lambda e, b=b: e.reduce_sum(stat[:, 0:1], x1t[:, b:b + 1024], axis=AX), after=[xld], tag="lnsum")
        mn = p.op("vector", lambda e: e.tensor_scalar(aggr[:, 0:1], stat[:, 0:1], 1.0 / 1024, None, op0=ALU.mult), after=[p.force(rs1)], tag="lnmean")
        sqv = p.op("scalar", lambda e, b=b: e.square(sqbuf[:], x1t[:, b:b + 1024]), after=[xld], tag="lnsq")
        rs2 = p.op("vector", lambda e: e.reduce_sum(stat[:, 1:2], sqbuf[:], axis=AX), after=[sqv], tag="lnsum2")
        ex2 = p.op("vector", lambda e: e.tensor_scalar(aggr[:, 1:2], stat[:, 1:2], 1.0 / 1024, None, op0=ALU.mult), after=[p.force(rs2)], tag="lnex2")
        nv = p.op("vector", lambda e: e.scalar_tensor_tensor(
            out=mv[:, 2:3], in0=aggr[:, 0:1], scalar=aggr[:, 0:1], in1=aggr[:, 1:2],
            op0=ALU.mult, op1=ALU.subtract), after=[p.force(mn), p.force(ex2)], tag="lnnegvar")
        veps = p.op("vector", lambda e: e.tensor_scalar(
            mv[:, 2:3], mv[:, 2:3], -1.0, 1e-5, op0=ALU.mult, op1=ALU.add), after=[p.force(nv)], tag="lnveps")
        sq = p.op("scalar", lambda e: e.sqrt(mv[:, 3:4], mv[:, 2:3]), after=[p.force(veps)], tag="sqrt")
        rc = p.op("vector", lambda e: e.reciprocal(mv[:, 0:1], mv[:, 3:4]), after=[p.force(sq)], tag="recip")
        nm = p.op("vector", lambda e: e.scalar_tensor_tensor(
            out=mv[:, 1:2], in0=aggr[:, 0:1], scalar=-1.0, in1=mv[:, 0:1],
            op0=ALU.mult, op1=ALU.mult), after=[p.force(rc)], tag="negm")
        dep2 = [p.force(rc), p.force(nm)]
        if i >= 2:
            dep2.append(tp_done[(i - 2) * 8 + 7])
        xn = p.op("scalar", lambda e, b=b: e.activation(
            xnt_sb[:, b:b + 1024], x1t[:, b:b + 1024], AF.Identity,
            bias=mv[:, 1:2], scale=mv[:, 0:1]), after=dep2, tag=f"xn1_{i}")
        xn_done.append(xn)
        for kc in range(8):
            par = (i * 8 + kc) % 2
            dep3 = [xn, h0["ident"]]
            if i * 8 + kc >= 2:
                dep3.append(tp_done[-2])
            tp = p.op("tensor", lambda e, b=b, kc=kc, par=par: e.transpose(
                ps_tp[par][:], xnt_sb[:, b + kc * 128:b + kc * 128 + 128],
                c.sb["ident"][:]), after=dep3, tag="tp1")
            ev = p.op("scalar", lambda e, i=i, kc=kc, par=par: e.copy(
                xn_f[:, kc * T + i * 128: kc * T + i * 128 + 128], ps_tp[par][:]),
                after=[tp], tag="tpev1")
            tp_done.append(ev)
    all_xnf = tp_done[-1]
    for kc in range(8):
        p.op("sync", lambda e, kc=kc: e.dma_start(
            out=d["xn1_f"][kc * 128:(kc + 1) * 128, :], in_=xn_f[:, kc * T:(kc + 1) * T]),
            after=[all_xnf], dma=True, tag="xn1fst")

    mm_done = []
    n_ev = 0

    def head_proj(w_sb, wcols, nheads, dname):
        nonlocal n_ev
        for hd in range(nheads):
            for tc in range(T // 512):
                bank = n_ev % 4
                dep = [all_xnf] + (wl if n_ev == 0 else [])
                if n_ev >= 4:
                    dep.append(mm_done[n_ev - 4])
                last = None
                for kc in range(8):
                    last = p.op("tensor", lambda e, kc=kc, hd=hd, tc=tc, bank=bank, w_sb=w_sb, wcols=wcols: e.matmul(
                        ps_mm[:, bank * 512:(bank + 1) * 512],
                        w_sb[:, kc * wcols + hd * 128: kc * wcols + hd * 128 + 128],
                        xn_f[:, kc * T + tc * 512: kc * T + tc * 512 + 512],
                        start=(kc == 0), stop=(kc == 7)),
                        after=dep if kc == 0 else [], tag="hq")
                ev0 = p.op("scalar", lambda e, bank=bank: e.copy(
                    rope_a[:], ps_mm[:, bank * 512:(bank + 1) * 512]),
                    after=[last] + ([mm_done[-1]] if mm_done else []), tag="rpev")
                r1 = p.op("vector", lambda e: e.tensor_copy(rope_b[0:64, :], rope_a[64:128, :]),
                          after=[ev0], tag="rot1")
                r2 = p.op("vector", lambda e: e.tensor_copy(rope_b[64:128, :], rope_a[0:64, :]), tag="rot2")
                m1 = p.op("vector", lambda e, tc=tc: e.tensor_mul(
                    rope_a[:], rope_a[:], cosf[:, tc * 512:(tc + 1) * 512]),
                    after=rope_tabs + [r2], tag="rm1")
                m2 = p.op("vector", lambda e, tc=tc: e.tensor_mul(
                    rope_b[:], rope_b[:], sinm[:, tc * 512:(tc + 1) * 512]), tag="rm2")
                m3 = p.op("vector", lambda e: e.tensor_add(rope_a[:], rope_a[:], rope_b[:]), tag="rm3")
                st = p.op("sync", lambda e, hd=hd, tc=tc, dname=dname: e.dma_start(
                    out=d[dname][hd * 128:(hd + 1) * 128, tc * 512:(tc + 1) * 512],
                    in_=rope_a[:]), after=[m3], dma=True, tag="rpst")
                mm_done.append(st)
                n_ev += 1

    head_proj(wq, 1024, 8, "q1_f")
    head_proj(wk, 512, 4, "k1_f")

    for i in range(NT):
        bank = n_ev % 4
        dep = [all_xnf]
        if n_ev >= 4:
            dep.append(mm_done[n_ev - 4])
        last = None
        for kc in range(8):
            last = p.op("tensor", lambda e, kc=kc, i=i, bank=bank: e.matmul(
                ps_mm[:, bank * 512:(bank + 1) * 512],
                xn_f[:, kc * T + i * 128: kc * T + i * 128 + 128],
                wv[:, kc * 512:(kc + 1) * 512],
                start=(kc == 0), stop=(kc == 7)), after=dep if kc == 0 else [], tag="vmm1")
        eb = (n_ev % 4) * 512
        ev = p.op("vector", lambda e, bank=bank, eb=eb: e.tensor_copy(
            evac[:, eb:eb + 512], ps_mm[:, bank * 512:(bank + 1) * 512]), after=[last], tag="vev1")
        st = p.op("sync", lambda e, i=i, eb=eb: e.dma_start(
            out=d["v1_t"][i * 128:(i + 1) * 128, :], in_=evac[:, eb:eb + 512]),
            after=[ev], dma=True, tag="vst1")
        mm_done.append(st)
        n_ev += 1

    prev_cw = [None, None]
    prev_ev = None
    for i in range(NT):
        dep = [all_xnf, l_rb] + ([prev_cw[i % 2]] if prev_cw[i % 2] else []) \
            + ([prev_ev] if prev_ev else [])
        last = None
        for kc in range(8):
            last = p.op("tensor", lambda e, kc=kc, i=i: e.matmul(
                ps_r[:, 0:8], xn_f[:, kc * T + i * 128: kc * T + i * 128 + 128],
                wr[:, kc * 8:(kc + 1) * 8],
                start=(kc == 0), stop=(kc == 7)), after=dep if kc == 0 else [], tag="rmm")
        lg = p.op("vector", lambda e: e.tensor_add(logits[:], ps_r[:, 0:8], rbb[:]),
                  after=[last], tag="lg")
        sg = p.op("scalar", lambda e: e.activation(probs[:], logits[:], AF.Sigmoid),
                  after=[lg], tag="sig")
        t2 = p.op("vector", lambda e: e.max(out=top2[:], in_=probs[:]), after=[sg], tag="top2")
        cb = (i % 2) * 8
        cw = p.op("vector", lambda e, cb=cb: e.scalar_tensor_tensor(
            out=cw_sb[:, cb:cb + 8], in0=probs[:], scalar=top2[:, 1:2], in1=probs[:],
            op0=ALU.is_ge, op1=ALU.mult), after=[p.force(t2)], tag="cw")
        prev_ev = cw
        st = p.op("sync", lambda e, i=i, cb=cb: e.dma_start(
            out=d["cw_t"][i * 128:(i + 1) * 128, :], in_=cw_sb[:, cb:cb + 8]),
            after=[cw], dma=True, tag="cwst")
        prev_cw[i % 2] = st


def emit_l1attn(p, c, d, h0):
    T, NT = c.T, c.NT
    k_f = c.sbuf("f_kf", [128, 4 * T], BF16)
    v_sb = c.sbuf("f_v", [128, NT * 512], BF16)
    wo = c.sbuf("f_wo", [128, 8 * 1024], BF16)
    q_sb = c.sbuf("f_q", [128, 2 * 128], BF16)
    scores = c.sbuf("f_sc", [128, T], F32)
    trimask = c.sbuf("f_tri", [128, 128], F32)
    iotp = c.sbuf("f_iotp", [128, 128], U32)
    iotf = c.sbuf("f_iotf", [128, 129], F32)
    pexp = c.sbuf("f_p", [128, T], BF16)
    pt = c.sbuf("f_pt", [128, T], BF16)
    nmx = c.sbuf("f_nmx", [128, 2], F32)
    den = c.sbuf("f_den", [128, 1], F32)
    o_sb = c.sbuf("f_o", [128, 128], BF16)
    o_f = c.sbuf("f_of", [128, 8 * 128], BF16)
    at_sb = c.sbuf("f_at", [128, 2 * 1024], F32)
    ps_qk = [c.psum("f_psqk0", [128, 512], F32), c.psum("f_psqk1", [128, 512], F32)]
    ps_tp = [c.psum("f_pstp0", [128, 128], BF16), c.psum("f_pstp1", [128, 128], BF16)]
    ps_pv = c.psum("f_pspv", [128, 128], F32)
    ps_pr = c.psum("f_pspr", [128, 512], F32)

    kl = []
    for kc in range(4):
        kl.append(p.op("sync", lambda e, kc=kc: e.dma_start(
            out=k_f[:, kc * T:(kc + 1) * T],
            in_=d["k1_f"][kc * 128:(kc + 1) * 128, :]), dma=True, tag="kld1"))
    vl = []
    for i in range(NT):
        vl.append(p.op("sync", lambda e, i=i: e.dma_start(
            out=v_sb[:, i * 512:(i + 1) * 512],
            in_=d["v1_t"][i * 128:(i + 1) * 128, :]), dma=True, tag="vld1"))
    wl = []
    for kc in range(8):
        wl.append(p.op("gpsimd", lambda e, kc=kc: e.dma_start(
            out=wo[:, kc * 1024:(kc + 1) * 1024],
            in_=d["Wo1"][kc * 128:(kc + 1) * 128, :]), dma=True, tag="wold"))
    g1 = p.op("gpsimd", lambda e: e.iota(iotp[:], pattern=[[1, 128]], base=0,
                                         channel_multiplier=0), tag="fiota")
    m1 = p.op("vector", lambda e: e.tensor_copy(iotf[:, 0:128], iotp[:]), after=[g1], tag="fiotf")
    g2 = p.op("gpsimd", lambda e: e.iota(iotp[:, 0:1], pattern=[[0, 1]], base=0,
                                         channel_multiplier=1), after=[m1], tag="fiotp")
    m2 = p.op("vector", lambda e: e.tensor_copy(iotf[:, 128:129], iotp[:, 0:1]), after=[g2], tag="fiotpf")
    m3 = p.op("vector", lambda e: e.tensor_scalar(
        trimask[:], iotf[:, 0:128], iotf[:, 128:129], NEG, op0=ALU.is_gt, op1=ALU.mult),
        after=[p.force(m2)], tag="ftri")

    prev = {"qld": {}, "stt": [None, None], "exp": None, "ptev": [None, None],
            "oev": None, "otev": {}, "atst": [None, None], "pr_ev": None, "qk_last": {}}
    for i in range(NT):
        nkeys = (i + 1) * 128
        nchunks = (nkeys + 511) // 512
        for h in range(8):
            kv = h // 2
            qb = ((i * 8 + h) % 2) * 128
            qdep = [prev["qk_last"].get((i * 8 + h) - 2)]
            if i == 0 and h == 0:
                qdep += kl
            qld = p.op("sync", lambda e, i=i, h=h, qb=qb: e.dma_start(
                out=q_sb[:, qb:qb + 128],
                in_=d["q1_f"][h * 128:(h + 1) * 128, i * 128:(i + 1) * 128]),
                after=qdep, dma=True, tag="qld1")
            ev_done = []
            for nk in range(nchunks):
                par = nk % 2
                ncols = min(512, nkeys - nk * 512)
                mmdep = [qld]
                if nk >= 2:
                    mmdep.append(ev_done[nk - 2])
                else:
                    mmdep.append(prev["stt"][par])
                last = p.op("tensor", lambda e, nk=nk, par=par, kv=kv, qb=qb, ncols=ncols: e.matmul(
                    ps_qk[par][:, 0:ncols], q_sb[:, qb:qb + 128],
                    k_f[:, kv * T + nk * 512: kv * T + nk * 512 + ncols],
                    start=True, stop=True), after=mmdep, tag="qkmm1")
                prev["qk_last"][i * 8 + h] = last
                ev = p.op("scalar", lambda e, nk=nk, par=par, ncols=ncols: e.activation(
                    scores[:, nk * 512: nk * 512 + ncols], ps_qk[par][:, 0:ncols],
                    AF.Identity, scale=c.cc("scale1")),
                    after=[last, h0["consts"]], tag="scev")
                ev_done.append(ev)
                prev["stt"][par] = ev
            mk = p.op("vector", lambda e, nkeys=nkeys: e.tensor_add(
                scores[:, nkeys - 128: nkeys], scores[:, nkeys - 128: nkeys], trimask[:]),
                after=[ev_done[-1], m3], tag="mask")
            mx = p.op("vector", lambda e, nkeys=nkeys: e.reduce_max(
                nmx[:, 0:1], scores[:, 0:nkeys], axis=AX, negate=True), after=[mk], tag="mx1")
            expdep = [p.force(mx)]
            if prev["exp"] is not None:
                expdep.append(prev["exp"])
            ex = p.op("scalar", lambda e, nkeys=nkeys: e.activation(
                pexp[:, 0:nkeys], scores[:, 0:nkeys], AF.Exp, bias=nmx[:, 0:1],
                accum_out=den[:]), after=expdep, tag="exp1")
            rc = p.op("vector", lambda e: e.reciprocal(nmx[:, 1:2], den[:]), after=[ex], tag="rcp1")
            tpev = []
            for c32 in range(i + 1):
                par = c32 % 2
                dep2 = [ex]
                if c32 >= 2:
                    dep2.append(tpev[c32 - 2])
                elif prev["ptev"][par] is not None:
                    dep2.append(prev["ptev"][par])
                tp = p.op("tensor", lambda e, c32=c32, par=par: e.transpose(
                    ps_tp[par][:], pexp[:, c32 * 128:(c32 + 1) * 128], c.sb["ident"][:]),
                    after=dep2, tag="ptp1")
                ev = p.op("scalar", lambda e, c32=c32, par=par: e.copy(
                    pt[:, c32 * 128:(c32 + 1) * 128], ps_tp[par][:]), after=[tp], tag="ptev1")
                tpev.append(ev)
            if len(tpev) >= 2:
                prev["ptev"] = [tpev[-2], tpev[-1]]
            else:
                prev["ptev"] = [prev["ptev"][1] if prev["ptev"][1] else tpev[-1], tpev[-1]]
            prev["exp"] = tpev[-1]
            dep3 = [tpev[-1]] + (vl if (i == 0 and h == 0) else []) + \
                   ([prev["oev"]] if prev["oev"] is not None else [])
            last = None
            for c32 in range(i + 1):
                last = p.op("tensor", lambda e, c32=c32, kv=kv, i=i: e.matmul(
                    ps_pv[:], pt[:, c32 * 128:(c32 + 1) * 128],
                    v_sb[:, c32 * 512 + kv * 128: c32 * 512 + kv * 128 + 128],
                    start=(c32 == 0), stop=(c32 == i)),
                    after=dep3 if c32 == 0 else [], tag="pvmm1")
            oev = p.op("scalar", lambda e: e.activation(
                o_sb[:], ps_pv[:], AF.Identity, scale=nmx[:, 1:2]),
                after=[last, p.force(rc)], tag="oev1")
            prev["oev"] = oev
            par = h % 2
            dep4 = [oev]
            if prev["otev"].get(h - 2) is not None:
                dep4.append(prev["otev"][h - 2])
            elif h < 2 and len(tpev) >= 2:
                dep4.append(tpev[-1])
            otp = p.op("tensor", lambda e, par=par: e.transpose(
                ps_tp[par][:], o_sb[:], c.sb["ident"][:]),
                after=dep4 + [tpev[-2] if len(tpev) >= 2 else tpev[-1]], tag="otp1")
            otev = p.op("scalar", lambda e, h=h, par=par: e.copy(
                o_f[:, h * 128:(h + 1) * 128], ps_tp[par][:]), after=[otp], tag="otev1")
            prev["otev"][h] = otev
            prev["ptev"] = [otev, otev]
        ab = (i % 2) * 1024
        all_o = prev["otev"][7]
        for half in range(2):
            dep5 = [all_o] + (wl if i == 0 else []) + \
                   ([prev["atst"][half]] if prev["atst"][half] is not None else []) + \
                   ([prev["pr_ev"]] if prev["pr_ev"] is not None else [])
            last = None
            for kc in range(8):
                last = p.op("tensor", lambda e, kc=kc, half=half: e.matmul(
                    ps_pr[:], o_f[:, kc * 128:(kc + 1) * 128],
                    wo[:, kc * 1024 + half * 512: kc * 1024 + (half + 1) * 512],
                    start=(kc == 0), stop=(kc == 7)),
                    after=dep5 if kc == 0 else [], tag="prmm1")
            ev = p.op("vector", lambda e, half=half, ab=ab: e.tensor_copy(
                at_sb[:, ab + half * 512: ab + (half + 1) * 512], ps_pr[:]),
                after=[last], tag="prev1")
            prev["pr_ev"] = ev
            st = p.op("sync", lambda e, i=i, half=half, ab=ab: e.dma_start(
                out=d["attn1"][i * 128:(i + 1) * 128, half * 512:(half + 1) * 512],
                in_=at_sb[:, ab + half * 512: ab + (half + 1) * 512]),
                after=[ev], dma=True, tag="atst1")
            prev["atst"][half] = st
        prev["otev"] = {-2: prev["pr_ev"], -1: prev["pr_ev"]}


def emit_l1moe(p, c, d, h0):
    T, NT = c.T, c.NT
    NTC = T // 512
    wgt = c.sbuf("g_wg", [128, 8 * 2048], BF16)
    wut = c.sbuf("g_wu", [128, 8 * 2048], BF16)
    wdt = c.sbuf("g_wd", [128, 16 * 1024], BF16)
    xnf = c.sbuf("g_xnf", [128, 2 * 4096], BF16)
    gbuf = c.sbuf("g_g", [128, 512], BF16)
    abuf = c.sbuf("g_a", [128, 16 * 512], BF16)
    cwc = c.sbuf("g_cw", [128, 8 * (c.T // 128)], F32)
    out_sb = c.sbuf("g_out", [128, 4096], F32)
    rx1 = c.sbuf("g_rx1", [128, 2 * 1024], F32)
    rat = c.sbuf("g_rat", [128, 2 * 1024], F32)
    rmo = c.sbuf("g_rmo", [128, 2 * 1024], F32)
    fout = c.sbuf("g_fout", [128, 2 * 1024], F32)
    ps_gu = [c.psum("g_psg0", [128, 512], F32), c.psum("g_psg1", [128, 512], F32)]
    ps_dn = [c.psum("g_psd0", [128, 512], F32), c.psum("g_psd1", [128, 512], F32)]

    l_cw = p.op("sync", lambda e: e.dma_start(
        out=cwc[:].rearrange("p (n q) -> p n q", q=8),
        in_=d["cw_t"][:, :].rearrange("(n p) q -> p n q", p=128)), dma=True, tag="cwld")

    a_done_hist = {}
    dn_seq_hist = {}
    st_hist = {}
    xn_hist = {}
    for ex in range(9):
        routed = ex > 0
        wsrc = (("WgTs", "WuTs", "WdTs") if not routed else ("WgT", "WuT", "WdT"))
        wdep = [dn_seq_hist[(ex - 1, NTC - 1)][-1]] if ex else []
        wl = []
        for kc in range(8):
            wl.append(p.op("gpsimd", lambda e, kc=kc, ex=ex, w=wsrc[0]: e.dma_start(
                out=wgt[:, kc * 2048:(kc + 1) * 2048],
                in_=(d[w][kc * 128:(kc + 1) * 128, :] if ex == 0 else
                     d[w][ex - 1, kc * 128:(kc + 1) * 128, :])),
                after=wdep, dma=True, tag="gwg"))
            wl.append(p.op("gpsimd", lambda e, kc=kc, ex=ex, w=wsrc[1]: e.dma_start(
                out=wut[:, kc * 2048:(kc + 1) * 2048],
                in_=(d[w][kc * 128:(kc + 1) * 128, :] if ex == 0 else
                     d[w][ex - 1, kc * 128:(kc + 1) * 128, :])),
                after=wdep, dma=True, tag="gwu"))
        for ic in range(16):
            wl.append(p.op("gpsimd", lambda e, ic=ic, ex=ex, w=wsrc[2]: e.dma_start(
                out=wdt[:, ic * 1024:(ic + 1) * 1024],
                in_=(d[w][ic * 128:(ic + 1) * 128, :] if ex == 0 else
                     d[w][ex - 1, ic * 128:(ic + 1) * 128, :])),
                after=wdep, dma=True, tag="gwd"))
        for tc in range(NTC):
            xb = (tc % 2) * 4096
            hc, tcp = (ex, tc - 2) if tc >= 2 else (ex - 1, NTC - 2 + tc)
            xnd = [xn_hist.get((hc, tcp))]
            xnld = p.op("sync", lambda e, tc=tc, xb=xb: e.dma_start(
                out=xnf[:, xb:xb + 4096].rearrange("p (c t) -> p c t", c=8),
                in_=d["xn1_f"][:, tc * 512:(tc + 1) * 512].rearrange("(c p) t -> p c t", p=128)),
                after=xnd, dma=True, tag="gxnld")
            aev_list = []
            for ic in range(16):
                par = ic % 2
                if ic >= 2:
                    guev = aev_list[ic - 2]
                else:
                    guev = a_done_hist.get((ex, tc - 1)) or (a_done_hist.get((ex - 1, NTC - 1)) if ex else None)
                mmdep = [xnld] + (wl if (tc == 0 and ic == 0) else []) + ([guev] if guev else [])
                adep = []
                if ic == 0:
                    pk = (ex, tc - 1) if tc else (ex - 1, NTC - 1)
                    if pk in dn_seq_hist:
                        adep.append(dn_seq_hist[pk][-1])
                last = None
                for kc in range(8):
                    last = p.op("tensor", lambda e, kc=kc, ic=ic, par=par, xb=xb: e.matmul(
                        ps_gu[par][:], wgt[:, kc * 2048 + ic * 128: kc * 2048 + ic * 128 + 128],
                        xnf[:, xb + kc * 512: xb + kc * 512 + 512],
                        start=(kc == 0), stop=(kc == 7)),
                        after=(mmdep + adep) if kc == 0 else [], tag="ggmm")
                gev = p.op("scalar", lambda e, par=par: e.activation(
                    gbuf[:], ps_gu[par][:], AF.Silu), after=[last], tag="ggev")
                for kc in range(8):
                    last = p.op("tensor", lambda e, kc=kc, ic=ic, par=par, xb=xb: e.matmul(
                        ps_gu[par][:], wut[:, kc * 2048 + ic * 128: kc * 2048 + ic * 128 + 128],
                        xnf[:, xb + kc * 512: xb + kc * 512 + 512],
                        start=(kc == 0), stop=(kc == 7)),
                        after=[gev] if kc == 0 else [], tag="gumm")
                aev = p.op("vector", lambda e, ic=ic, par=par: e.scalar_tensor_tensor(
                    out=abuf[:, ic * 512:(ic + 1) * 512], in0=ps_gu[par][:],
                    scalar=0.0, in1=gbuf[:], op0=ALU.bypass, op1=ALU.mult),
                    after=[last, p.force(gev)] + adep, tag="gaev")
                aev_list.append(aev)
            a_done = aev_list[-1]
            a_done_hist[(ex, tc)] = a_done
            xn_hist[(ex, tc)] = a_done
            dn_seq = []
            for piece in range(8):
                tt, nh = divmod(piece, 2)
                par = piece % 2
                dndep = [a_done]
                if piece >= 2:
                    dndep.append(dn_seq[piece - 2])
                else:
                    pk = (ex, tc - 1) if tc else (ex - 1, NTC - 1)
                    if pk in dn_seq_hist:
                        dndep.append(dn_seq_hist[pk][piece - 2])
                last = None
                for ic in range(16):
                    last = p.op("tensor", lambda e, ic=ic, tt=tt, nh=nh, par=par: e.matmul(
                        ps_dn[par][:], abuf[:, ic * 512 + tt * 128: ic * 512 + tt * 128 + 128],
                        wdt[:, ic * 1024 + nh * 512: ic * 1024 + (nh + 1) * 512],
                        start=(ic == 0), stop=(ic == 15)),
                        after=dndep if ic == 0 else [], tag="gdmm")
                ob = tt * 1024 + nh * 512
                evdep = [last]
                if piece == 0 and st_hist.get((ex, tc - 1)) is not None:
                    evdep.append(st_hist[(ex, tc - 1)])
                if piece == 0 and tc == 0 and ex and st_hist.get((ex - 1, NTC - 1)) is not None:
                    evdep.append(st_hist[(ex - 1, NTC - 1)])
                if routed:
                    col = (tc * 4 + tt) * 8 + (ex - 1)
                    ev = p.op("vector", lambda e, par=par, ob=ob, col=col: e.tensor_scalar(
                        out_sb[:, ob:ob + 512], ps_dn[par][:], cwc[:, col:col + 1],
                        None, op0=ALU.mult), after=evdep + [l_cw], tag="gdev")
                else:
                    ev = p.op("vector", lambda e, par=par, ob=ob: e.tensor_copy(
                        out_sb[:, ob:ob + 512], ps_dn[par][:]), after=evdep, tag="gdev0")
                dn_seq.append(ev)
            dn_seq_hist[(ex, tc)] = dn_seq
            sdep = [dn_seq[-1]]
            if ex == 0:
                st = p.op("sync", lambda e, tc=tc: e.dma_start(
                    out=d["moe"][tc * 512:(tc + 1) * 512, :].rearrange("(a p) q -> p a q", p=128),
                    in_=out_sb[:].rearrange("p (a q) -> p a q", a=4)),
                    after=sdep, dma=True, tag="moest")
            else:
                st = p.op("gpsimd", lambda e, tc=tc: e.dma_start(
                    out=d["moe"][tc * 512:(tc + 1) * 512, :].rearrange("(a p) q -> p a q", p=128),
                    in_=out_sb[:].rearrange("p (a q) -> p a q", a=4),
                    accum_op=ALU.add), after=sdep + [st_hist[(0, tc)]], dma=True, tag="moeacc")
            st_hist[(ex, tc)] = st

    fin_v = {}
    fin_st = {}
    for i in range(NT):
        b = (i % 2) * 1024
        dep = [fin_v.get(i - 2), st_hist[(8, i // 4)]]
        l1 = p.op("sync", lambda e, i=i, b=b: e.dma_start(
            out=rx1[:, b:b + 1024], in_=d["x1"][i * 128:(i + 1) * 128, :]), after=dep, dma=True, tag="fx1")
        l2 = p.op("sync", lambda e, i=i, b=b: e.dma_start(
            out=rat[:, b:b + 1024], in_=d["attn1"][i * 128:(i + 1) * 128, :]), after=dep, dma=True, tag="fat")
        l3 = p.op("sync", lambda e, i=i, b=b: e.dma_start(
            out=rmo[:, b:b + 1024], in_=d["moe"][i * 128:(i + 1) * 128, :]), after=dep, dma=True, tag="fmo")
        sdep = [fin_st.get(i - 2)]
        v1 = p.op("vector", lambda e, b=b: e.tensor_add(
            fout[:, b:b + 1024], rx1[:, b:b + 1024], rat[:, b:b + 1024]),
            after=[l1, l2] + sdep, tag="fadd1")
        v2 = p.op("vector", lambda e, b=b: e.tensor_add(
            fout[:, b:b + 1024], fout[:, b:b + 1024], rmo[:, b:b + 1024]), after=[l3], tag="fadd2")
        fin_v[i] = v2
        fin_st[i] = p.op("sync", lambda e, i=i, b=b: e.dma_start(
            out=d["out"][i * 128:(i + 1) * 128, :], in_=fout[:, b:b + 1024]),
            after=[v2], dma=True, tag="fst")



def build_one(T, stages, dram_io):
    nc = bass.Bass("TRN2", target_bir_lowering=False, debug=False,
                   detect_race_conditions=False)
    specs = dict(SHARED_SPECS)
    specs.update(inter_specs(T))
    dram = {}
    for name, kind in dram_io.items():
        shape, dt = specs[name]
        k = {"in": "ExternalInput", "out": "ExternalOutput", "internal": "Internal"}[kind]
        dram[name] = nc.dram_tensor(name, shape, dt, kind=k).ap()
    with ExitStack() as ctx:
        c = Ctx(nc, ctx, T)
        p = Prog(nc, ctx)
        for st in stages:
            st(p, c, dram)
        p.emit()
    return nc


def launch_l0pre(T):
    def stages(p, c, d):
        h0 = emit_setup(p, c, d)
        emit_l0pre(p, c, d, h0)
    io = {k: "in" for k in ["x", "Wqkv", "bqkv_pp", "bv_row", "Wqp", "keysT", "ident_bf", "consts"]}
    io.update({k: "out" for k in ["xn_t", "xn_f", "qk0_f", "v0_t", "sims"]})
    return build_one(T, [stages], io)


def launch_l0attn(T):
    def stages(p, c, d):
        h0 = emit_setup(p, c, d)
        emit_l0attn(p, c, d, h0)
    io = {k: "in" for k in ["qk0_f", "v0_t", "Wproj", "bproj_row", "ident_bf", "consts"]}
    io.update({k: "out" for k in ["attn0"]})
    return build_one(T, [stages], io)


def launch_l0mlp(T):
    def stages(p, c, d):
        h0 = emit_setup(p, c, d)
        emit_l0mlp(p, c, d, h0)
    io = {k: "in" for k in ["xn_f", "x", "attn0", "pkm", "Wg0", "Wu0", "Wd0",
                            "bg0_pp", "bu0_pp", "bd0_row", "ident_bf", "consts"]}
    io.update({k: "out" for k in ["x1"]})
    return build_one(T, [stages], io)


def launch_l1pre(T):
    def stages(p, c, d):
        h0 = emit_setup(p, c, d, need_identf=True)
        emit_l1pre(p, c, d, h0)
    io = {k: "in" for k in ["x1", "Wq1", "Wk1", "Wv1", "WrT", "rb_row",
                            "ident_bf", "ident_f32", "consts"]}
    io.update({k: "out" for k in ["xn1_f", "q1_f", "k1_f", "v1_t", "cw_t"]})
    return build_one(T, [stages], io)


def launch_l1moe(T):
    def stages(p, c, d):
        h0 = emit_setup(p, c, d)
        emit_l1moe(p, c, d, h0)
    io = {k: "in" for k in ["xn1_f", "cw_t", "x1", "attn1", "WgTs", "WuTs", "WdTs",
                            "WgT", "WuT", "WdT", "ident_bf", "consts"]}
    io.update({k: "out" for k in ["out"]})
    io.update({k: "internal" for k in ["moe"]})
    return build_one(T, [stages], io)


def launch_l1attn(T):
    def stages(p, c, d):
        h0 = emit_setup(p, c, d)
        emit_l1attn(p, c, d, h0)
    io = {k: "in" for k in ["q1_f", "k1_f", "v1_t", "Wo1", "ident_bf", "consts"]}
    io.update({k: "out" for k in ["attn1"]})
    return build_one(T, [stages], io)


def launch_l0pkm(T):
    def stages(p, c, d):
        h0 = emit_setup(p, c, d, need_iota=True)
        emit_l0pkm(p, c, d, h0)
    io = {k: "in" for k in ["sims", "xn_t", "demb", "uemb", "ident_bf", "consts"]}
    io.update({k: "out" for k in ["pkm"]})
    io.update({k: "internal" for k in ["dembh", "uembh"]})
    return build_one(T, [stages], io)


LAUNCHES = [
    ("l0pre", launch_l0pre,
     ["x", "Wqkv", "bqkv_pp", "bv_row", "Wqp", "keysT", "ident_bf", "consts"],
     ["xn_t", "xn_f", "qk0_f", "v0_t", "sims"]),
    ("l0attn", launch_l0attn,
     ["qk0_f", "v0_t", "Wproj", "bproj_row", "ident_bf", "consts"],
     ["attn0"]),
    ("l0pkm", launch_l0pkm,
     ["sims", "xn_t", "demb", "uemb", "ident_bf", "consts"],
     ["pkm"]),
    ("l0mlp", launch_l0mlp,
     ["xn_f", "x", "attn0", "pkm", "Wg0", "Wu0", "Wd0", "bg0_pp", "bu0_pp",
      "bd0_row", "ident_bf", "consts"],
     ["x1"]),
    ("l1pre", launch_l1pre,
     ["x1", "Wq1", "Wk1", "Wv1", "WrT", "rb_row", "ident_bf", "ident_f32", "consts"],
     ["xn1_f", "q1_f", "k1_f", "v1_t", "cw_t"]),
    ("l1attn", launch_l1attn,
     ["q1_f", "k1_f", "v1_t", "Wo1", "ident_bf", "consts"],
     ["attn1"]),
    ("l1moe", launch_l1moe,
     ["xn1_f", "cw_t", "x1", "attn1", "WgTs", "WuTs", "WdTs", "WgT", "WuT",
      "WdT", "ident_bf", "consts"],
     ["out"]),
]

_NC_CACHE = {}


def kernel(**inputs):
    sh, per_core = _prep(inputs)
    T = per_core[0]["x"].shape[0]
    ncores = len(per_core)
    state = [dict(pc) for pc in per_core]
    for name, builder, ins, outs in LAUNCHES:
        key = (name, T)
        if key not in _NC_CACHE:
            _NC_CACHE[key] = builder(T)
        nc = _NC_CACHE[key]
        in_maps = []
        for b in range(ncores):
            m = {}
            for k in ins:
                m[k] = state[b][k] if k in state[b] else sh[k]
            in_maps.append(m)
        res = run_bass_kernel_spmd(nc, in_maps, core_ids=list(range(ncores)))
        for b in range(ncores):
            for k in outs:
                state[b][k] = res.results[b][k]
    return np.stack([np.asarray(state[b]["out"], dtype=np.float32) for b in range(ncores)], 0)


# version 06 (final)
# speedup vs baseline: 1.0426x; 1.0426x over previous
import math
import numpy as np
import ml_dtypes
from contextlib import ExitStack

import concourse.bass as bass
import concourse.mybir as mybir
from concourse.bass import IndirectOffsetOnAxis
from concourse.bass_utils import run_bass_kernel_spmd

F32 = mybir.dt.float32
BF16 = mybir.dt.bfloat16
U32 = mybir.dt.uint32
AF = mybir.ActivationFunctionType
ALU = mybir.AluOpType
AX = mybir.AxisListType.X

D = 1024
NCORES = 8
T_FULL = 4096

_L1 = math.log(0.1 / 0.9)
_L2 = math.log(0.25 / 0.75)
_L3 = math.log(0.85 / 0.15)
_L4 = math.log(0.95 / 0.05)
SCALE0 = 64.0 ** -0.5
SCALE1 = 128.0 ** -0.5
NEG = -30000.0

CONST_COLS = ["ml1", "ml2", "ml3", "ml4", "scale0", "scale1", "half", "eps", "neg"]
CONST_VALS = {"ml1": -_L1, "ml2": -_L2, "ml3": -_L3, "ml4": -_L4,
              "scale0": SCALE0, "scale1": SCALE1, "half": 0.5, "eps": 1e-5, "neg": NEG}
CIDX = {k: i for i, k in enumerate(CONST_COLS)}



class Op:
    __slots__ = ("idx", "engine", "fn", "after", "dma", "dma_inc", "needs_inc",
                 "chan_idx", "tag")

    def __init__(self, idx, engine, fn, after, dma, tag):
        self.idx = idx
        self.engine = engine
        self.fn = fn
        self.after = after
        self.dma = dma
        self.needs_inc = False
        self.chan_idx = None
        self.tag = tag


ENGS = ("sync", "gpsimd", "vector", "scalar", "tensor")


class Prog:
    def __init__(self, nc, ctx):
        self.nc = nc
        self.ctx = ctx
        self.ops = []
        self._sb_bytes = 0
        self._pending_barrier = None
        self._barrier_seen = set()
        self._drains = {}
        self._drain_req = []
        self._bar_scratch = ctx.enter_context(
            nc.sbuf_tensor("bar_scratch", [128, 8], mybir.dt.float32))

    def op(self, engine, fn, after=(), dma=False, tag=""):
        a = []
        for d in after:
            if d is None:
                continue
            di = d[1] if isinstance(d, tuple) else d
            assert isinstance(di, int) and di < len(self.ops), f"bad dep {d}"
            a.append(d)
        if getattr(self, "_pending_barrier", None) and engine not in self._barrier_seen:
            a.extend(self._pending_barrier)
            self._barrier_seen.add(engine)
        o = Op(len(self.ops), engine, fn, a, dma, tag)
        self.ops.append(o)
        return o.idx

    def barrier(self):
        dcount = {e: 0 for e in ENGS}
        for o in self.ops:
            if o.dma:
                dcount[o.engine] += 1
        nops = []
        for en in ENGS:
            n = dcount[en]

            def fn(e, en=en, n=n):
                KD = self.KD[en]
                for slot in range(min(n, KD)):
                    nfull = n // KD + (1 if slot < n % KD else 0)
                    key = ("d", en, slot)
                    if self._hiwater.get((en, key), 0) < 16 * nfull:
                        e.wait_ge(self.dsem[en][slot], 16 * nfull)
                        self._hiwater[(en, key)] = 16 * nfull
                if en == "vector":
                    return e.engine_nop()
                if en == "scalar":
                    return e.memzero(self._bar_scratch[:, 0:2])
                if en == "gpsimd":
                    return e.memset(self._bar_scratch[:, 2:4], 0.0)
                return e.nop()

            nops.append(self.op(en, fn, tag=f"bar_{en}"))
        self._pending_barrier = nops
        self._barrier_seen = set()
        return nops

    def force(self, dep):
        return ("force", dep)

    def emit(self):
        nc = self.nc
        for o in self.ops:
            for d in o.after:
                force = isinstance(d, tuple)
                dep = self.ops[d[1] if force else d]
                if dep.dma or dep.engine != o.engine or force:
                    dep.needs_inc = True
        KD = {"sync": 32, "gpsimd": 32, "vector": 4, "scalar": 4, "tensor": 4}
        self.csem = {e: self.ctx.enter_context(nc.semaphore(f"c_{e}")) for e in ENGS}
        self.dsem = {e: [self.ctx.enter_context(nc.semaphore(f"d_{e}{j}"))
                         for j in range(KD[e])] for e in ENGS}
        ccount = {e: 0 for e in ENGS}
        dcount = {e: 0 for e in ENGS}
        for o in self.ops:
            if o.dma:
                n = dcount[o.engine]
                o.chan_idx = (n % KD[o.engine], 16 * (n // KD[o.engine] + 1))
                dcount[o.engine] += 1
            elif o.needs_inc:
                ccount[o.engine] += 1
                o.chan_idx = ccount[o.engine]
        self.final_counts = dict(c=ccount, d=dcount)
        self.KD = KD

        hiwater = self._hiwater = {}

        def emit_engine(eng_name, eng):
            for o in self.ops:
                if o.engine != eng_name:
                    continue
                want = {}
                for d in o.after:
                    force = False
                    if isinstance(d, tuple):
                        force = True
                        d = d[1]
                    dep = self.ops[d]
                    if dep.engine == eng_name and not dep.dma and not force:
                        continue
                    if dep.dma:
                        slot, val = dep.chan_idx
                        key = ("d", dep.engine, slot)
                    else:
                        assert dep.chan_idx is not None, (o.idx, o.tag, d, dep.tag)
                        key = ("c", dep.engine)
                        val = dep.chan_idx
                    want[key] = max(want.get(key, 0), val)
                if o.dma:
                    slot, val = o.chan_idx
                    if val > 16:
                        key = ("d", eng_name, slot)
                        want[key] = max(want.get(key, 0), val - 16)
                for key, val in sorted(want.items()):
                    hw = hiwater.get((eng_name, key), 0)
                    if val > hw:
                        sem = self.dsem[key[1]][key[2]] if key[0] == "d" else self.csem[key[1]]
                        eng.wait_ge(sem, val)
                        hiwater[(eng_name, key)] = val
                inst = o.fn(eng)
                if o.dma:
                    assert inst is not None, (o.idx, o.tag)
                    slot, val = o.chan_idx
                    inst.then_inc(self.dsem[eng_name][slot], 16)
                elif o.needs_inc:
                    if inst is None:
                        inst = eng.nop()
                    inst.then_inc(self.csem[eng_name], 1)

        blk = self.ctx.enter_context(nc.Block())

        @blk.sync
        def _(e):
            emit_engine("sync", e)
            for en in ENGS:
                n = self.final_counts["d"][en]
                for slot in range(min(n, self.KD[en])):
                    nfull = n // self.KD[en] + (1 if slot < n % self.KD[en] else 0)
                    e.wait_ge(self.dsem[en][slot], 16 * nfull)

        @blk.gpsimd
        def _(e):
            emit_engine("gpsimd", e)

        @blk.vector
        def _(e):
            emit_engine("vector", e)

        @blk.scalar
        def _(e):
            emit_engine("scalar", e)

        @blk.tensor
        def _(e):
            emit_engine("tensor", e)


class Ctx:

    def __init__(self, nc, ctx, T):
        self.nc = nc
        self.ctx = ctx
        self.T = T
        self.NT = T // 128
        self.sb = {}
        self._scopes = []

    def push(self):
        self._scopes.append(ExitStack())

    def pop(self):
        self._scopes.pop().close()

    def _stack(self):
        return self._scopes[-1] if self._scopes else self.ctx

    def sbuf(self, name, shape, dtype):
        return self._stack().enter_context(self.nc.sbuf_tensor(name, shape, dtype))

    def psum(self, name, shape, dtype):
        return self._stack().enter_context(self.nc.psum_tensor(name, shape, dtype))

    def cc(self, name):
        return self.sb["consts"][:, CIDX[name]:CIDX[name] + 1]



def _prep(inputs):
    f = lambda a: np.ascontiguousarray(a, dtype=np.float32)
    sh = {}
    sh["Wqkv"] = f(inputs["L0_attn_W_qkv"])
    sh["bqkv_pp"] = f(np.asarray(inputs["L0_attn_b_qkv"])[:2048].reshape(16, 128).T)
    sh["bv_row"] = f(np.asarray(inputs["L0_attn_b_qkv"])[2048:].reshape(1, 1024))
    sh["Wproj"] = f(inputs["L0_attn_W_proj"])
    sh["bproj_row"] = f(np.asarray(inputs["L0_attn_b_proj"]).reshape(1, 1024))
    sh["Wg0"] = f(inputs["L0_ffn_Wg"])
    sh["Wu0"] = f(inputs["L0_ffn_Wu"])
    sh["Wd0"] = f(inputs["L0_ffn_Wd"])
    sh["bg0_pp"] = f(np.asarray(inputs["L0_ffn_bg"]).reshape(32, 128).T)
    sh["bu0_pp"] = f(np.asarray(inputs["L0_ffn_bu"]).reshape(32, 128).T)
    sh["bd0_row"] = f(np.asarray(inputs["L0_ffn_bd"]).reshape(1, 1024))
    sh["Wqp"] = f(inputs["L0_ffn_Wq"])
    sh["keysT"] = f(np.asarray(inputs["L0_ffn_keys"]).transpose(3, 2, 0, 1).reshape(64, 512))
    sh["demb"] = f(inputs["L0_ffn_down_embed"])
    sh["uemb"] = f(inputs["L0_ffn_up_embed"])
    sh["Wq1"] = f(inputs["L1_attn_Wq"])
    sh["Wk1"] = f(inputs["L1_attn_Wk"])
    sh["Wv1"] = f(inputs["L1_attn_Wv"])
    sh["Wo1"] = f(inputs["L1_attn_Wo"])
    sh["WgTs"] = f(np.asarray(inputs["L1_ffn_Wg_s"]).T)
    sh["WuTs"] = f(np.asarray(inputs["L1_ffn_Wu_s"]).T)
    sh["WdTs"] = f(np.asarray(inputs["L1_ffn_Wd_s"]).T)
    sh["WgT"] = f(np.asarray(inputs["L1_ffn_Wg"]).transpose(0, 2, 1))
    sh["WuT"] = f(np.asarray(inputs["L1_ffn_Wu"]).transpose(0, 2, 1))
    sh["WdT"] = f(np.asarray(inputs["L1_ffn_Wd"]).transpose(0, 2, 1))
    sh["WrT"] = f(np.asarray(inputs["L1_ffn_Wr"]).T)
    sh["rb_row"] = f(np.asarray(inputs["L1_ffn_rb"]).reshape(1, 8))
    sh["ident_bf"] = np.eye(128, dtype=np.float32).astype(ml_dtypes.bfloat16)
    sh["ident_f32"] = np.eye(128, dtype=np.float32)
    cvals = np.array([[CONST_VALS[k] for k in CONST_COLS]], dtype=np.float32)
    sh["consts"] = np.tile(cvals, (128, 1))
    x = np.asarray(inputs["L0_attn_x"], dtype=np.float32)
    per_core = [{"x": np.ascontiguousarray(x[b])} for b in range(x.shape[0])]
    return sh, per_core


SHARED_SPECS = {
    "Wqkv": ([1024, 3072], F32), "bqkv_pp": ([128, 16], F32), "bv_row": ([1, 1024], F32),
    "Wproj": ([1024, 1024], F32), "bproj_row": ([1, 1024], F32),
    "Wg0": ([1024, 4096], F32), "Wu0": ([1024, 4096], F32), "Wd0": ([4096, 1024], F32),
    "bg0_pp": ([128, 32], F32), "bu0_pp": ([128, 32], F32), "bd0_row": ([1, 1024], F32),
    "Wqp": ([1024, 512], F32), "keysT": ([64, 512], F32),
    "demb": ([4096, 1024], F32), "uemb": ([4096, 1024], F32),
    "Wq1": ([1024, 1024], F32), "Wk1": ([1024, 512], F32), "Wv1": ([1024, 512], F32),
    "Wo1": ([1024, 1024], F32),
    "WgTs": ([1024, 2048], F32), "WuTs": ([1024, 2048], F32), "WdTs": ([2048, 1024], F32),
    "WgT": ([8, 1024, 2048], F32), "WuT": ([8, 1024, 2048], F32), "WdT": ([8, 2048, 1024], F32),
    "WrT": ([1024, 8], F32), "rb_row": ([1, 8], F32),
    "ident_bf": ([128, 128], BF16), "ident_f32": ([128, 128], F32),
    "consts": ([128, len(CONST_COLS)], F32),
}

def inter_specs(T):
    return {
        "x": ([T, 1024], F32),
        "xn_t": ([T, 1024], BF16),
        "xn_f": ([1024, T], BF16),
        "qk0_f": ([2048, T], BF16),
        "v0_t": ([T, 1024], BF16),
        "sims": ([T, 512], F32),
        "attn0": ([T, 1024], F32),
        "pkm": ([T, 1024], F32),
        "x1": ([T, 1024], F32),
        "xn1_f": ([1024, T], BF16),
        "q1_f": ([1024, T], BF16),
        "k1_f": ([512, T], BF16),
        "v1_t": ([T, 512], BF16),
        "cw_t": ([T, 8], F32),
        "attn1": ([T, 1024], F32),
        "moe": ([T, 1024], F32),
        "dembh": ([4096, 1024], BF16),
        "uembh": ([4096, 1024], BF16),
        "pkm_idx": ([T, 32], U32),
        "pkm_cc": ([T, 32], F32),
        "out": ([T, 1024], F32),
    }



def emit_setup(p, c, d, need_iota=False, need_identf=False):
    c.sb["consts"] = c.sbuf("c_consts", [128, len(CONST_COLS)], F32)
    c.sb["ident"] = c.sbuf("c_ident", [128, 128], BF16)
    h = {}
    h["consts"] = p.op("sync", lambda e: e.dma_start(out=c.sb["consts"][:], in_=d["consts"][:, :]), dma=True, tag="ld_consts")
    h["ident"] = p.op("sync", lambda e: e.dma_start(out=c.sb["ident"][:], in_=d["ident_bf"][:, :]), dma=True, tag="ld_ident")
    if need_identf:
        c.sb["identf"] = c.sbuf("c_identf", [128, 128], F32)
        h["identf"] = p.op("sync", lambda e: e.dma_start(out=c.sb["identf"][:], in_=d["ident_f32"][:, :]), dma=True, tag="ld_identf")
    if need_iota:
        c.sb["iota64u"] = c.sbuf("c_iota64u", [128, 64], U32)
        c.sb["iota64"] = c.sbuf("c_iota64", [128, 64], F32)
        i1 = p.op("gpsimd", lambda e: e.iota(c.sb["iota64u"][:], pattern=[[1, 64]], base=0, channel_multiplier=0), tag="iota")
        h["iota"] = p.op("vector", lambda e: e.tensor_copy(c.sb["iota64"][:], c.sb["iota64u"][:]), after=[i1], tag="iotaf")
    return h


L0PRE_PARTS = 5


def emit_l0pre(p, c, d, h0):
    T, NT = c.T, c.NT
    PARTS = L0PRE_PARTS
    wqkv = c.sbuf("a_wqkv", [128, 8 * 3072], BF16)
    wqp = c.sbuf("a_wqp", [128, 8 * 512], BF16)
    keysb = c.sbuf("a_keys", [128, 512], BF16)
    keysf = c.sbuf("a_keysf", [64, 512], F32)
    bqkv = c.sbuf("a_bqkv", [128, 16], F32)
    bvb = c.sbuf("a_bvb", [128, 1024], F32)
    xn_f = c.sbuf("a_xnf", [128, 8 * T], BF16)
    qp_f = c.sbuf("a_qpf", [128, 4 * T], BF16)
    xt = c.sbuf("a_xt", [128, 2 * 1024], F32)
    sqbuf = c.sbuf("a_sq", [128, 1024], F32)
    stat = c.sbuf("a_stat", [128, 12], F32)
    aggr = c.sbuf("a_aggr", [128, 2], F32)
    mv = c.sbuf("a_mv", [128, 4], F32)
    xnt_sb = c.sbuf("a_xnt", [128, 2 * 1024], BF16)
    ps_tp = [c.psum("a_pstp0", [128, 128], BF16), c.psum("a_pstp1", [128, 128], BF16)]
    ps_mm = c.psum("a_psmm", [128, 4 * 512], F32)
    evac = c.sbuf("a_evac", [128, 4 * 512], BF16)
    sim_sb = c.sbuf("a_sim", [128, 2 * 512], F32)

    wl = []
    for kc in range(8):
        wl.append(p.op("gpsimd", lambda e, kc=kc: e.dma_start(
            out=wqkv[:, kc * 3072:(kc + 1) * 3072],
            in_=d["Wqkv"][kc * 128:(kc + 1) * 128, :]), dma=True, tag="wqkv"))
    for kc in range(8):
        wl.append(p.op("gpsimd", lambda e, kc=kc: e.dma_start(
            out=wqp[:, kc * 512:(kc + 1) * 512],
            in_=d["Wqp"][kc * 128:(kc + 1) * 128, :]), dma=True, tag="wqp"))
    l_keys = p.op("sync", lambda e: e.dma_start(out=keysf[:], in_=d["keysT"][:, :]), dma=True, tag="keys")
    p.op("vector", lambda e: e.tensor_copy(keysb[0:64, :], keysf[:]), after=[l_keys], tag="keyscast")
    cast_keys = p.op("vector", lambda e: e.tensor_copy(keysb[64:128, :], keysf[:]), after=[l_keys], tag="keysdup")
    l_bq = p.op("sync", lambda e: e.dma_start(out=bqkv[:], in_=d["bqkv_pp"][:, :]), dma=True, tag="bqkv")
    l_bv = p.op("sync", lambda e: e.dma_start(out=bvb[:], in_=d["bv_row"][0:1, :].to_broadcast([128, 1024])), dma=True, tag="bv")

    xn_done = []
    tp_done = []
    xld = [None] * NT
    xst = [None] * NT
    for i in range(NT):
        b = (i % 2) * 1024
        dep = [xn_done[i - 2]] if i >= 2 else []
        xld[i] = p.op("sync", lambda e, i=i, b=b: e.dma_start(
            out=xt[:, b:b + 1024], in_=d["x"][i * 128:(i + 1) * 128, :]),
            after=dep, dma=True, tag=f"xld{i}")
        rs1 = p.op("vector", lambda e, b=b: e.reduce_sum(stat[:, 0:1], xt[:, b:b + 1024], axis=AX), after=[xld[i]], tag="lnsum")
        mn = p.op("vector", lambda e: e.tensor_scalar(aggr[:, 0:1], stat[:, 0:1], 1.0 / 1024, None, op0=ALU.mult), after=[p.force(rs1)], tag="lnmean")
        sqv = p.op("scalar", lambda e, b=b: e.square(sqbuf[:], xt[:, b:b + 1024]), after=[xld[i]], tag="lnsq")
        rs2 = p.op("vector", lambda e: e.reduce_sum(stat[:, 1:2], sqbuf[:], axis=AX), after=[sqv], tag="lnsum2")
        ex2 = p.op("vector", lambda e: e.tensor_scalar(aggr[:, 1:2], stat[:, 1:2], 1.0 / 1024, None, op0=ALU.mult), after=[p.force(rs2)], tag="lnex2")
        nv = p.op("vector", lambda e: e.scalar_tensor_tensor(
            out=mv[:, 2:3], in0=aggr[:, 0:1], scalar=aggr[:, 0:1], in1=aggr[:, 1:2],
            op0=ALU.mult, op1=ALU.subtract), after=[p.force(mn), p.force(ex2)], tag="lnnegvar")
        veps = p.op("vector", lambda e: e.tensor_scalar(
            mv[:, 2:3], mv[:, 2:3], -1.0, 1e-5, op0=ALU.mult, op1=ALU.add), after=[p.force(nv)], tag="lnveps")
        sq = p.op("scalar", lambda e: e.sqrt(mv[:, 3:4], mv[:, 2:3]), after=[p.force(veps)], tag="sqrt")
        rc = p.op("vector", lambda e: e.reciprocal(mv[:, 0:1], mv[:, 3:4]), after=[p.force(sq)], tag="recip")
        nm = p.op("vector", lambda e: e.scalar_tensor_tensor(
            out=mv[:, 1:2], in0=aggr[:, 0:1], scalar=-1.0, in1=mv[:, 0:1],
            op0=ALU.mult, op1=ALU.mult), after=[p.force(rc)], tag="negm")
        dep2 = [p.force(rc), p.force(nm)]
        if i >= 2:
            dep2.append(xst[i - 2])
        xn = p.op("scalar", lambda e, b=b: e.activation(
            xnt_sb[:, b:b + 1024], xt[:, b:b + 1024], AF.Identity,
            bias=mv[:, 1:2], scale=mv[:, 0:1]), after=dep2, tag=f"xn{i}")
        xn_done.append(xn)
        xst[i] = p.op("sync", lambda e, i=i, b=b: e.dma_start(
            out=d["xn_t"][i * 128:(i + 1) * 128, :], in_=xnt_sb[:, b:b + 1024]),
            after=[xn], dma=True, tag=f"xnst{i}")
        for kc in range(8 if PARTS >= 1 else 0):
            par = (i * 8 + kc) % 2
            dep3 = [xn, h0["ident"]]
            if i * 8 + kc >= 2:
                dep3.append(tp_done[-2])
            tp = p.op("tensor", lambda e, b=b, kc=kc, par=par: e.transpose(
                ps_tp[par][:], xnt_sb[:, b + kc * 128:b + kc * 128 + 128],
                c.sb["ident"][:]), after=dep3, tag="tp")
            ev = p.op("scalar", lambda e, i=i, kc=kc, par=par: e.copy(
                xn_f[:, kc * T + i * 128: kc * T + i * 128 + 128], ps_tp[par][:]),
                after=[tp], tag="tpev")
            tp_done.append(ev)
    if PARTS < 1:
        return {}
    all_xnf = tp_done[-1]
    xnf_st = []
    for kc in range(8):
        xnf_st.append(p.op("sync", lambda e, kc=kc: e.dma_start(
            out=d["xn_f"][kc * 128:(kc + 1) * 128, :], in_=xn_f[:, kc * T:(kc + 1) * T]),
            after=[all_xnf], dma=True, tag="xnfst"))

    if PARTS < 2:
        return {}
    mm_done = []
    n_ev = 0

    def proj_group(bank, mm_fn, ev_fn, extra_dep=()):
        nonlocal n_ev
        dep = list(extra_dep)
        if n_ev >= 4:
            dep.append(mm_done[n_ev - 4][1])
        mm = mm_fn(dep)
        dep_store = [mm_done[n_ev - 4][2]] if (n_ev >= 4 and mm_done[n_ev - 4][2] is not None) else []
        ev, st = ev_fn(mm, dep_store)
        mm_done.append((mm, ev, st))
        n_ev += 1
        return ev

    for ec in range(16):
        for tc in range(T // 512):
            bank = n_ev % 4

            def mk_mm(dep, ec=ec, tc=tc, bank=bank):
                first = None
                for kc in range(8):
                    o = p.op("tensor", lambda e, kc=kc, ec=ec, tc=tc, bank=bank: e.matmul(
                        ps_mm[:, bank * 512:(bank + 1) * 512],
                        wqkv[:, kc * 3072 + ec * 128: kc * 3072 + ec * 128 + 128],
                        xn_f[:, kc * T + tc * 512: kc * T + tc * 512 + 512],
                        start=(kc == 0), stop=(kc == 7)),
                        after=(dep + [all_xnf] + wl) if kc == 0 else [], tag=f"qkmm")
                return o

            def mk_ev(mm, dep_store, ec=ec, tc=tc, bank=bank):
                ev = p.op("vector", lambda e: e.tensor_scalar(
                    evac[:, bank * 512:(bank + 1) * 512],
                    ps_mm[:, bank * 512:(bank + 1) * 512],
                    bqkv[:, ec:ec + 1], None, op0=ALU.add),
                    after=[mm, l_bq] + dep_store, tag="qkev")
                st = p.op("sync", lambda e: e.dma_start(
                    out=d["qk0_f"][ec * 128:(ec + 1) * 128, tc * 512:(tc + 1) * 512],
                    in_=evac[:, bank * 512:(bank + 1) * 512]), after=[ev], dma=True, tag="qkst")
                return ev, st

            proj_group(bank, mk_mm, mk_ev)

    for i in range(NT if PARTS >= 3 else 0):
        for half in range(2):
            bank = n_ev % 4

            def mk_mm(dep, i=i, half=half, bank=bank):
                for kc in range(8):
                    o = p.op("tensor", lambda e, kc=kc, i=i, half=half, bank=bank: e.matmul(
                        ps_mm[:, bank * 512:(bank + 1) * 512],
                        xn_f[:, kc * T + i * 128: kc * T + i * 128 + 128],
                        wqkv[:, kc * 3072 + 2048 + half * 512: kc * 3072 + 2048 + (half + 1) * 512],
                        start=(kc == 0), stop=(kc == 7)), after=dep if kc == 0 else [], tag="vmm")
                return o

            def mk_ev(mm, dep_store, i=i, half=half, bank=bank):
                ev = p.op("vector", lambda e: e.tensor_tensor(
                    evac[:, bank * 512:(bank + 1) * 512],
                    ps_mm[:, bank * 512:(bank + 1) * 512],
                    bvb[:, half * 512:(half + 1) * 512], op=ALU.add),
                    after=[mm, l_bv] + dep_store, tag="vev")
                st = p.op("sync", lambda e: e.dma_start(
                    out=d["v0_t"][i * 128:(i + 1) * 128, half * 512:(half + 1) * 512],
                    in_=evac[:, bank * 512:(bank + 1) * 512]), after=[ev], dma=True, tag="vst")
                return ev, st

            proj_group(bank, mk_mm, mk_ev)

    if PARTS < 4:
        return {}
    for ec in range(4):
        for tc in range(T // 512):
            bank = n_ev % 4

            def mk_mm(dep, ec=ec, tc=tc, bank=bank):
                for kc in range(8):
                    o = p.op("tensor", lambda e, kc=kc, ec=ec, tc=tc, bank=bank: e.matmul(
                        ps_mm[:, bank * 512:(bank + 1) * 512],
                        wqp[:, kc * 512 + ec * 128: kc * 512 + ec * 128 + 128],
                        xn_f[:, kc * T + tc * 512: kc * T + tc * 512 + 512],
                        start=(kc == 0), stop=(kc == 7)), after=dep if kc == 0 else [], tag="qpmm")
                return o

            def mk_ev(mm, dep_store, ec=ec, tc=tc, bank=bank):
                ev = p.op("vector", lambda e: e.tensor_copy(
                    qp_f[:, ec * T + tc * 512: ec * T + tc * 512 + 512],
                    ps_mm[:, bank * 512:(bank + 1) * 512]), after=[mm], tag="qpev")
                return ev, None

            proj_group(bank, mk_mm, mk_ev)

    qp_ready = mm_done[-1][1]

    if PARTS < 5:
        return {}
    sim_st = []
    for i in range(NT):
        sb = (i % 2) * 512
        evs = []
        for rnd in range(2):
            mms = []
            for g4 in range(4):
                g = rnd * 4 + g4
                pp, hh = divmod(g, 4)
                rows = pp * 256 + hh * 64
                chunk, base = divmod(rows, 128)
                dep = [qp_ready, cast_keys]
                if n_ev >= 4 or rnd or i:
                    if evs:
                        dep.append(evs[-4 + g4] if len(evs) >= 4 else None)
                    elif i:
                        dep.append(sim_st[-1])
                    else:
                        dep.append(mm_done[n_ev - 4][1])
                mms.append(p.op("tensor", lambda e, g=g, i=i, g4=g4, chunk=chunk, base=base: e.matmul(
                    ps_mm[:, g4 * 512: g4 * 512 + 64],
                    qp_f[base:base + 64, chunk * T + i * 128: chunk * T + i * 128 + 128],
                    keysb[base:base + 64, g * 64:(g + 1) * 64], start=True, stop=True),
                    after=[dd for dd in dep if dd is not None], tag="simmm"))
            for g4 in range(4):
                g = rnd * 4 + g4
                evs.append(p.op("scalar", lambda e, g=g, g4=g4, sb=sb: e.copy(
                    sim_sb[:, sb + g * 64: sb + (g + 1) * 64], ps_mm[:, g4 * 512: g4 * 512 + 64]),
                    after=[mms[g4]] + ([sim_st[i - 2]] if (rnd == 0 and g4 == 0 and i >= 2) else []),
                    tag="simev"))
        st = p.op("sync", lambda e, i=i, sb=sb: e.dma_start(
            out=d["sims"][i * 128:(i + 1) * 128, :], in_=sim_sb[:, sb:sb + 512]),
            after=[evs[-1]], dma=True, tag="simst")
        sim_st.append(st)

    return {}


L0ATTN_LHS = "of"
L0ATTN_PARTS = 5


def emit_l0attn(p, c, d, h0):
    T, NT = c.T, c.NT
    PARTS = L0ATTN_PARTS
    k_f = c.sbuf("b_kf", [128, 8 * T], BF16)
    v_t = c.sbuf("b_vt", [128, NT * 1024], BF16)
    wproj = c.sbuf("b_wproj", [128, 8 * 1024], BF16)
    bprojb = c.sbuf("b_bprojb", [128, 1024], F32)
    q_sb = c.sbuf("b_q", [128, 2 * 8 * 128], BF16)
    scores = c.sbuf("b_scores", [128, T], F32)
    sg = c.sbuf("b_sg", [128, 4 * 512], BF16)
    mbuf = c.sbuf("b_m", [128, 3 * 512], BF16)
    pexp = c.sbuf("b_p", [128, T], BF16)
    pt = c.sbuf("b_pt", [128, T], BF16)
    nmx = c.sbuf("b_nmx", [128, 2], F32)
    den = c.sbuf("b_den", [128, 1], F32)
    o_sb = c.sbuf("b_o", [128, 1024], BF16)
    o_f = c.sbuf("b_of", [128, 8 * 128], BF16)
    at_sb = c.sbuf("b_at", [128, 2 * 1024], F32)
    ps_pr = c.psum("b_pspr", [128, 512], F32)
    ps_qk = [c.psum("b_psqk0", [128, 512], F32), c.psum("b_psqk1", [128, 512], F32)]
    ps_tp = [c.psum("b_pstp0", [128, 128], BF16), c.psum("b_pstp1", [128, 128], BF16)]
    ps_pv = [c.psum("b_pspv0", [128, 512], F32), c.psum("b_pspv1", [128, 512], F32)]

    kl = []
    for kc in range(8):
        kl.append(p.op("sync", lambda e, kc=kc: e.dma_start(
            out=k_f[:, kc * T:(kc + 1) * T],
            in_=d["qk0_f"][1024 + kc * 128: 1024 + (kc + 1) * 128, :]), dma=True, tag="kld"))
    vl = []
    for i in range(NT):
        vl.append(p.op("sync", lambda e, i=i: e.dma_start(
            out=v_t[:, i * 1024:(i + 1) * 1024],
            in_=d["v0_t"][i * 128:(i + 1) * 128, :]), dma=True, tag="vld"))
    wl = []
    for kc in range(8):
        wl.append(p.op("gpsimd", lambda e, kc=kc: e.dma_start(
            out=wproj[:, kc * 1024:(kc + 1) * 1024],
            in_=d["Wproj"][kc * 128:(kc + 1) * 128, :]), dma=True, tag="wprojld"))
    l_bp = p.op("sync", lambda e: e.dma_start(
        out=bprojb[:], in_=d["bproj_row"][0:1, :].to_broadcast([128, 1024])), dma=True, tag="bprojld")

    prev = {"qld": [None, None], "stt": [None, None], "exp": None, "ptev": [None, None],
            "oev": [None, None], "atst": [None, None], "sg": None, "qk_last": {}}
    for i in range(NT):
        qb = (i % 2) * 1024
        dep = [prev["qk_last"].get(i - 2)] if i >= 2 else []
        qld = p.op("sync", lambda e, i=i, qb=qb: e.dma_start(
            out=q_sb[:, qb:qb + 1024].rearrange("p (c t) -> p c t", c=8),
            in_=d["qk0_f"][0:1024, i * 128:(i + 1) * 128].rearrange("(c p) t -> p c t", p=128)),
            after=dep + ([] if i > 0 else kl), dma=True, tag=f"qld{i}")
        sps_done = []
        for nk in range(T // 512):
            par = nk % 2
            mmdep = [qld]
            if nk >= 2:
                mmdep.append(sps_done[nk - 2])
            elif i > 0:
                mmdep.append(prev["stt"][par])
            if i == 0 and nk == 0:
                mmdep += kl
            last = None
            for kc in range(8):
                last = p.op("tensor", lambda e, kc=kc, nk=nk, par=par, qb=qb: e.matmul(
                    ps_qk[par][:], q_sb[:, qb + kc * 128: qb + kc * 128 + 128],
                    k_f[:, kc * T + nk * 512: kc * T + nk * 512 + 512],
                    start=(kc == 0), stop=(kc == 7)),
                    after=mmdep if kc == 0 else [], tag="qkmm")
            prev["qk_last"][i] = last
            sgdep = [last, h0["consts"]]
            if prev["sg"] is not None:
                sgdep.append(prev["sg"])
            sgops = []
            for j, ml in enumerate(["ml1", "ml2", "ml3", "ml4"]):
                sgops.append(p.op("scalar", lambda e, j=j, ml=ml, par=par: e.activation(
                    sg[:, j * 512:(j + 1) * 512], ps_qk[par][:], AF.Sign,
                    bias=c.cc(ml), scale=c.cc("scale0")),
                    after=sgdep if j == 0 else [], tag="sg"))
            v1 = p.op("vector", lambda e: e.scalar_tensor_tensor(
                out=mbuf[:, 0:512], in0=sg[:, 512:1024], scalar=3.5, in1=sg[:, 0:512],
                op0=ALU.mult, op1=ALU.add), after=[sgops[-1]], tag="spsv1")
            v2 = p.op("vector", lambda e: e.tensor_add(
                mbuf[:, 512:1024], sg[:, 1024:1536], sg[:, 1536:2048]), tag="spsv2")
            prev["sg"] = v2
            v3 = p.op("vector", lambda e: e.scalar_tensor_tensor(
                out=mbuf[:, 0:512], in0=mbuf[:, 512:1024], scalar=2.5, in1=mbuf[:, 0:512],
                op0=ALU.mult, op1=ALU.add), tag="spsv3")
            v4 = p.op("vector", lambda e: e.tensor_scalar(
                mbuf[:, 1024:1536], mbuf[:, 0:512], 0.1, 1.05, op0=ALU.mult, op1=ALU.add), tag="spsv4")
            stt = p.op("vector", lambda e, nk=nk, par=par: e.scalar_tensor_tensor(
                out=scores[:, nk * 512:(nk + 1) * 512], in0=ps_qk[par][:],
                scalar=c.cc("scale0"), in1=mbuf[:, 1024:1536],
                op0=ALU.mult, op1=ALU.mult), tag="spsstt")
            prev["stt"][par] = stt
            sps_done.append(stt)
        if PARTS < 2:
            continue
        mx = p.op("vector", lambda e: e.reduce_max(
            nmx[:, 0:1], scores[:], axis=AX, negate=True), after=[sps_done[-1]], tag="mx")
        expdep = [p.force(mx)]
        if prev["exp"] is not None:
            expdep.append(prev["exp"])
        ex = p.op("scalar", lambda e: e.activation(
            pexp[:], scores[:], AF.Exp, bias=nmx[:, 0:1], accum_out=den[:]),
            after=expdep, tag="exp")
        rc = p.op("vector", lambda e: e.reciprocal(nmx[:, 1:2], den[:]), after=[ex], tag="rcp")
        tpev = []
        for c32 in range(NT):
            par = c32 % 2
            dep2 = [ex]
            if c32 >= 2:
                dep2.append(tpev[c32 - 2])
            elif prev["ptev"][par] is not None:
                dep2.append(prev["ptev"][par])
            tp = p.op("tensor", lambda e, c32=c32, par=par: e.transpose(
                ps_tp[par][:], pexp[:, c32 * 128:(c32 + 1) * 128], c.sb["ident"][:]),
                after=dep2, tag="ptp")
            ev = p.op("scalar", lambda e, c32=c32, par=par: e.copy(
                pt[:, c32 * 128:(c32 + 1) * 128], ps_tp[par][:]), after=[tp], tag="ptev")
            tpev.append(ev)
        prev["ptev"] = [tpev[-2], tpev[-1]]
        prev["exp"] = tpev[-1]
        if PARTS < 3:
            continue
        for half in range(2):
            dep3 = [tpev[-1]] + (vl if i == 0 else []) + \
                   ([prev["oev"][half]] if prev["oev"][half] is not None else [])
            last = None
            for c32 in range(NT):
                last = p.op("tensor", lambda e, c32=c32, half=half: e.matmul(
                    ps_pv[half][:], pt[:, c32 * 128:(c32 + 1) * 128],
                    v_t[:, c32 * 1024 + half * 512: c32 * 1024 + (half + 1) * 512],
                    start=(c32 == 0), stop=(c32 == NT - 1)),
                    after=dep3 if c32 == 0 else [], tag="pvmm")
            oev = p.op("scalar", lambda e, half=half: e.activation(
                o_sb[:, half * 512:(half + 1) * 512], ps_pv[half][:], AF.Identity,
                scale=nmx[:, 1:2]), after=[last, p.force(rc)], tag="oev")
            prev["oev"][half] = oev
        ofev = []
        for kc in range(8):
            par = kc % 2
            dep4 = [prev["oev"][kc // 4]]
            if kc >= 2:
                dep4.append(ofev[kc - 2])
            else:
                dep4.append(tpev[-2 + kc])
            tp = p.op("tensor", lambda e, kc=kc, par=par: e.transpose(
                ps_tp[par][:], o_sb[:, kc * 128:(kc + 1) * 128], c.sb["ident"][:]),
                after=dep4, tag="otp")
            ev = p.op("scalar", lambda e, kc=kc, par=par: e.copy(
                o_f[:, kc * 128:(kc + 1) * 128], ps_tp[par][:]), after=[tp], tag="otev")
            ofev.append(ev)
        prev["ptev"] = [ofev[-2], ofev[-1]]
        if PARTS < 4:
            continue
        ab = (i % 2) * 1024
        for half in range(2):
            dep5 = [ofev[-1]] + (wl if i == 0 else []) + \
                   ([prev["atst"][half]] if prev["atst"][half] is not None else [])
            if prev.get("pr_ev") is not None:
                dep5.append(prev["pr_ev"])
            last = None
            for kc in range(8):
                src_lhs = (o_f[:, kc * 128:(kc + 1) * 128] if L0ATTN_LHS == "of"
                           else None)
                last = p.op("tensor", lambda e, kc=kc, half=half: e.matmul(
                    ps_pr[:],
                    (o_f if L0ATTN_LHS == "of" else wproj)[:, kc * 128:(kc + 1) * 128],
                    wproj[:, kc * 1024 + half * 512: kc * 1024 + (half + 1) * 512],
                    start=(kc == 0), stop=(kc == 7)),
                    after=dep5 if kc == 0 else [], tag="prmm")
            ev = p.op("vector", lambda e, half=half, ab=ab: e.tensor_tensor(
                at_sb[:, ab + half * 512: ab + (half + 1) * 512], ps_pr[:],
                bprojb[:, half * 512:(half + 1) * 512], op=ALU.add),
                after=[last, l_bp], tag="prev")
            prev["pr_ev"] = ev
            if PARTS >= 5:
                st = p.op("sync", lambda e, i=i, half=half, ab=ab: e.dma_start(
                    out=d["attn0"][i * 128:(i + 1) * 128, half * 512:(half + 1) * 512],
                    in_=at_sb[:, ab + half * 512: ab + (half + 1) * 512]),
                    after=[ev], dma=True, tag="atst")
                prev["atst"][half] = st
            else:
                prev["atst"][half] = ev


def emit_l0pkm(p, c, d, h0):
    T, NT = c.T, c.NT
    sm = c.sbuf("p_sm", [128, 2 * 512], F32)
    xnt = c.sbuf("p_xnt", [128, 2 * 1024], BF16)
    mx8 = c.sbuf("p_mx8", [128, 64], F32)
    ixv = c.sbuf("p_ixv", [128, 64], F32)
    scr64 = c.sbuf("p_scr64", [128, 64], F32)
    asc = c.sbuf("p_asc", [128, 64], F32)
    aix = c.sbuf("p_aix", [128, 64], F32)
    xx64 = c.sbuf("p_xx64", [128, 8], F32)
    sc8 = c.sbuf("p_sc8", [128, 32], F32)
    idxf = c.sbuf("p_idxf", [128, 32], F32)
    idxu = c.sbuf("p_idxu", [128, 2 * 32], U32)
    nmh = c.sbuf("p_nmh", [128, 4], F32)
    ex8 = c.sbuf("p_ex8", [128, 32], F32)
    zz = c.sbuf("p_z", [128, 4], F32)
    zr = c.sbuf("p_zr", [128, 4], F32)
    sm8 = c.sbuf("p_sm8", [128, 32], F32)
    wsel = c.sbuf("p_wsel", [128, 32], F32)
    sw = c.sbuf("p_sw", [128, 32], F32)
    ccf = c.sbuf("p_cc", [128, 2 * 32], F32)
    deg = c.sbuf("p_deg", [128, 4 * 1024], BF16)
    ueg = c.sbuf("p_ueg", [128, 8 * 1024], BF16)
    dscr = c.sbuf("p_dscr", [128, 1024], BF16)
    acc = c.sbuf("p_acc", [128, 2 * 1024], F32)

    tbl = []
    for t8 in range(8):
        tbl.append(p.op("gpsimd", lambda e, t8=t8: e.dma_start(
            out=d["dembh"][t8 * 512:(t8 + 1) * 512, :],
            in_=d["demb"][t8 * 512:(t8 + 1) * 512, :]), dma=True, tag="dembcast"))
        tbl.append(p.op("gpsimd", lambda e, t8=t8: e.dma_start(
            out=d["uembh"][t8 * 512:(t8 + 1) * 512, :],
            in_=d["uemb"][t8 * 512:(t8 + 1) * 512, :]), dma=True, tag="uembcast"))

    prev = {"dot": {}, "fma": {}, "accst": [None, None], "idxcp": None, "s1ex": {}}
    bcreg = {}

    def bc(e):
        if "r" not in bcreg:
            bcreg["r"] = e.to_reg(4095)
        return bcreg["r"]
    for i in range(NT):
        sb = (i % 2) * 512
        xb = (i % 2) * 1024
        ib = (i % 2) * 32
        smdep = [prev["s1ex"][i - 2]] if i >= 2 else []
        smld = p.op("sync", lambda e, i=i, sb=sb: e.dma_start(
            out=sm[:, sb:sb + 512], in_=d["sims"][i * 128:(i + 1) * 128, :]),
            after=smdep, dma=True, tag="smld")
        xndep = [prev["dot"].get((i - 2, 31))] if i >= 2 else []
        xnld = p.op("sync", lambda e, i=i, xb=xb: e.dma_start(
            out=xnt[:, xb:xb + 1024], in_=d["xn_t"][i * 128:(i + 1) * 128, :]),
            after=xndep, dma=True, tag="xnld")
        mxops = []
        for g in range(8):
            mxops.append(p.op("vector", lambda e, g=g, sb=sb: e.max(
                out=mx8[:, g * 8:(g + 1) * 8], in_=sm[:, sb + g * 64: sb + (g + 1) * 64]),
                after=[smld] if g == 0 else [], tag="s1max"))
        bar1 = mxops[-1]
        exops = []
        for g in range(8):
            for s_ in range(8):
                exops.append(p.op("vector", lambda e, g=g, s_=s_, sb=sb: e.scalar_tensor_tensor(
                    out=scr64[:], in0=sm[:, sb + g * 64: sb + (g + 1) * 64],
                    scalar=mx8[:, g * 8 + s_: g * 8 + s_ + 1], in1=c.sb["iota64"][:],
                    op0=ALU.is_equal, op1=ALU.mult,
                    accum_out=ixv[:, g * 8 + s_: g * 8 + s_ + 1]),
                    after=[p.force(bar1), h0["iota"]] if (g == 0 and s_ == 0) else [], tag="s1ex"))
        bar2 = exops[-1]
        prev["s1ex"][i] = bar2
        s2ops = []
        for h in range(4):
            gx, gy = h, 4 + h
            a1 = p.op("vector", lambda e, gx=gx: e.tensor_scalar(
                xx64[:], ixv[:, gx * 8:(gx + 1) * 8], 64.0, None, op0=ALU.mult),
                after=[p.force(bar2)] if h == 0 else [], tag="xx64")
            for i8 in range(8):
                p.op("vector", lambda e, i8=i8, gx=gx, gy=gy: e.tensor_scalar(
                    asc[:, i8 * 8:(i8 + 1) * 8], mx8[:, gy * 8:(gy + 1) * 8],
                    mx8[:, gx * 8 + i8: gx * 8 + i8 + 1], None, op0=ALU.add), tag="ascb")
                p.op("vector", lambda e, i8=i8, gy=gy: e.tensor_scalar(
                    aix[:, i8 * 8:(i8 + 1) * 8], ixv[:, gy * 8:(gy + 1) * 8],
                    xx64[:, i8:i8 + 1], None, op0=ALU.add), tag="aixb")
            m2 = p.op("vector", lambda e, h=h: e.max(
                out=sc8[:, h * 8:(h + 1) * 8], in_=asc[:]), tag="s2max")
            for s_ in range(8):
                s2ops.append(p.op("vector", lambda e, h=h, s_=s_: e.scalar_tensor_tensor(
                    out=scr64[:], in0=asc[:], scalar=sc8[:, h * 8 + s_: h * 8 + s_ + 1],
                    in1=aix[:], op0=ALU.is_equal, op1=ALU.mult,
                    accum_out=idxf[:, h * 8 + s_: h * 8 + s_ + 1]),
                    after=[p.force(m2)] if s_ == 0 else [], tag="s2ex"))
            nmop = p.op("vector", lambda e, h=h: e.tensor_scalar(
                nmh[:, h:h + 1], sc8[:, h * 8:h * 8 + 1], -1.0, None, op0=ALU.mult), tag="nmh")
        bar3 = nmop
        depi = [prev["fma"].get((i - 2, 31)), p.force(bar3)]
        idxcp = p.op("vector", lambda e, ib=ib: e.tensor_copy(idxu[:, ib:ib + 32], idxf[:]),
                     after=depi, tag="idxu")
        eops = []
        for h in range(4):
            eops.append(p.op("scalar", lambda e, h=h: e.activation(
                ex8[:, h * 8:(h + 1) * 8], sc8[:, h * 8:(h + 1) * 8], AF.Exp,
                bias=nmh[:, h:h + 1], accum_out=zz[:, h:h + 1]),
                after=[bar3], tag="hexp"))
        rcz = p.op("vector", lambda e: e.reciprocal(zr[:], zz[:]), after=[eops[-1]], tag="zrec")
        smops = []
        for h in range(4):
            smops.append(p.op("vector", lambda e, h=h: e.tensor_scalar(
                sm8[:, h * 8:(h + 1) * 8], ex8[:, h * 8:(h + 1) * 8],
                zr[:, h:h + 1], None, op0=ALU.mult),
                after=[p.force(rcz)] if h == 0 else [], tag="smsc"))
        dots = []
        for s_ in range(32):
            gb = (s_ % 4) * 1024
            depg = [idxcp] + tbl if (i == 0 and s_ == 0) else [idxcp]
            if s_ >= 4:
                depg.append(dots[s_ - 4])
            elif prev["dot"].get((i - 1, 28 + s_)) is not None:
                depg.append(prev["dot"][(i - 1, 28 + s_)])
            g1 = p.op("gpsimd", lambda e, s_=s_, gb=gb, ib=ib: e.indirect_dma_start(
                out=deg[:, gb:gb + 1024], out_offset=None, in_=d["dembh"][:, :],
                in_offset=IndirectOffsetOnAxis(idxu[:, ib + s_: ib + s_ + 1], axis=0),
                bounds_check=bc(e), oob_is_err=False),
                after=depg, dma=True, tag="deg")
            dot = p.op("vector", lambda e, s_=s_, gb=gb, xb=xb: e.scalar_tensor_tensor(
                out=dscr[:], in0=deg[:, gb:gb + 1024], scalar=0.0,
                in1=xnt[:, xb:xb + 1024], op0=ALU.bypass, op1=ALU.mult,
                accum_out=wsel[:, s_:s_ + 1]), after=[g1, xnld], tag="dot")
            dots.append(dot)
            prev["dot"][(i, s_)] = dot
        sil = p.op("scalar", lambda e: e.activation(sw[:], wsel[:], AF.Silu),
                   after=[dots[-1]], tag="silu")
        ccop = p.op("vector", lambda e, ib=ib: e.tensor_mul(ccf[:, ib:ib + 32], sw[:], sm8[:]),
                    after=[sil, p.force(smops[-1])] + ([prev["fma"].get((i - 2, 31))] if i >= 2 else []),
                    tag="cc")
        for s_ in range(32):
            ub = (s_ % 8) * 1024
            depg = [idxcp]
            if s_ >= 8:
                depg.append(prev["fma"][(i, s_ - 8)])
            elif prev["fma"].get((i - 1, 24 + s_)) is not None:
                depg.append(prev["fma"][(i - 1, 24 + s_)])
            g2 = p.op("gpsimd", lambda e, s_=s_, ub=ub, ib=ib: e.indirect_dma_start(
                out=ueg[:, ub:ub + 1024], out_offset=None, in_=d["uembh"][:, :],
                in_offset=IndirectOffsetOnAxis(idxu[:, ib + s_: ib + s_ + 1], axis=0),
                bounds_check=bc(e), oob_is_err=False),
                after=depg, dma=True, tag="ueg")
            depf = [g2, p.force(ccop)] if s_ == 0 else [g2]
            if s_ == 0 and prev["accst"][i % 2] is not None:
                depf.append(prev["accst"][i % 2])
            if s_ == 0:
                fma = p.op("vector", lambda e, ub=ub, xb=xb, ib=ib, s_=s_: e.tensor_scalar(
                    acc[:, xb:xb + 1024], ueg[:, ub:ub + 1024],
                    ccf[:, ib + s_: ib + s_ + 1], None, op0=ALU.mult),
                    after=depf, tag="fma0")
            else:
                fma = p.op("vector", lambda e, ub=ub, xb=xb, ib=ib, s_=s_: e.scalar_tensor_tensor(
                    out=acc[:, xb:xb + 1024], in0=ueg[:, ub:ub + 1024],
                    scalar=ccf[:, ib + s_: ib + s_ + 1], in1=acc[:, xb:xb + 1024],
                    op0=ALU.mult, op1=ALU.add), after=depf, tag="fma")
            prev["fma"][(i, s_)] = fma
        st = p.op("sync", lambda e, i=i, xb=xb: e.dma_start(
            out=d["pkm"][i * 128:(i + 1) * 128, :], in_=acc[:, xb:xb + 1024]),
            after=[prev["fma"][(i, 31)]], dma=True, tag="pkmst")
        prev["accst"][i % 2] = st


def emit_l0mlp(p, c, d, h0):
    T, NT = c.T, c.NT
    NTC = T // 512
    wg = c.sbuf("m_wg", [128, 8 * 2048], BF16)
    wu = c.sbuf("m_wu", [128, 8 * 2048], BF16)
    wd = c.sbuf("m_wd", [128, 16 * 1024], BF16)
    bg = c.sbuf("m_bg", [128, 32], F32)
    bu = c.sbuf("m_bu", [128, 32], F32)
    bdb = c.sbuf("m_bdb", [128, 1024], F32)
    xnf = c.sbuf("m_xnf", [128, 2 * 4096], BF16)
    gbuf = c.sbuf("m_g", [128, 512], BF16)
    abuf = c.sbuf("m_a", [128, 16 * 512], BF16)
    rx = c.sbuf("m_rx", [128, 2 * 1024], F32)
    rat = c.sbuf("m_rat", [128, 2 * 1024], F32)
    rpk = c.sbuf("m_rpk", [128, 2 * 1024], F32)
    prebuf = c.sbuf("m_pre", [128, 2 * 1024], F32)
    out_sb = c.sbuf("m_out", [128, 4096], F32)
    ps_gu = [c.psum("m_psg0", [128, 512], F32), c.psum("m_psg1", [128, 512], F32)]
    ps_dn = [c.psum("m_psd0", [128, 512], F32), c.psum("m_psd1", [128, 512], F32)]

    l_bg = p.op("sync", lambda e: e.dma_start(out=bg[:], in_=d["bg0_pp"][:, :]), dma=True, tag="bg")
    l_bu = p.op("sync", lambda e: e.dma_start(out=bu[:], in_=d["bu0_pp"][:, :]), dma=True, tag="bu")
    l_bd = p.op("sync", lambda e: e.dma_start(out=bdb[:], in_=d["bd0_row"][0:1, :].to_broadcast([128, 1024])), dma=True, tag="bd")

    pre_st = {}
    pre_v = {}
    for i in range(NT):
        b = (i % 2) * 1024
        dep = [pre_v.get(i - 2)]
        l1 = p.op("sync", lambda e, i=i, b=b: e.dma_start(
            out=rx[:, b:b + 1024], in_=d["x"][i * 128:(i + 1) * 128, :]), after=dep, dma=True, tag="rx")
        l2 = p.op("sync", lambda e, i=i, b=b: e.dma_start(
            out=rat[:, b:b + 1024], in_=d["attn0"][i * 128:(i + 1) * 128, :]), after=dep, dma=True, tag="rat")
        l3 = p.op("sync", lambda e, i=i, b=b: e.dma_start(
            out=rpk[:, b:b + 1024], in_=d["pkm"][i * 128:(i + 1) * 128, :]), after=dep, dma=True, tag="rpk")
        sdep = [pre_st.get(i - 2)]
        v1 = p.op("vector", lambda e, b=b: e.tensor_add(
            prebuf[:, b:b + 1024], rx[:, b:b + 1024], rat[:, b:b + 1024]),
            after=[l1, l2] + sdep, tag="pre1")
        p.op("vector", lambda e, b=b: e.tensor_add(
            prebuf[:, b:b + 1024], prebuf[:, b:b + 1024], rpk[:, b:b + 1024]), after=[l3], tag="pre2")
        v3 = p.op("vector", lambda e, b=b: e.tensor_add(
            prebuf[:, b:b + 1024], prebuf[:, b:b + 1024], bdb[:]), after=[l_bd], tag="pre3")
        pre_v[i] = v3
        pre_st[i] = p.op("sync", lambda e, i=i, b=b: e.dma_start(
            out=d["x1"][i * 128:(i + 1) * 128, :], in_=prebuf[:, b:b + 1024]),
            after=[v3], dma=True, tag="prest")

    a_done_hist = {}
    dn_seq_hist = {}
    st_hist = {}
    for half in range(2):
        wdep = [dn_seq_hist[(half - 1, NTC - 1)][-1]] if half else []
        wl = []
        for kc in range(8):
            wl.append(p.op("gpsimd", lambda e, kc=kc, half=half: e.dma_start(
                out=wg[:, kc * 2048:(kc + 1) * 2048],
                in_=d["Wg0"][kc * 128:(kc + 1) * 128, half * 2048:(half + 1) * 2048]),
                after=wdep, dma=True, tag="wg"))
            wl.append(p.op("gpsimd", lambda e, kc=kc, half=half: e.dma_start(
                out=wu[:, kc * 2048:(kc + 1) * 2048],
                in_=d["Wu0"][kc * 128:(kc + 1) * 128, half * 2048:(half + 1) * 2048]),
                after=wdep, dma=True, tag="wu"))
        for ic in range(16):
            wl.append(p.op("gpsimd", lambda e, ic=ic, half=half: e.dma_start(
                out=wd[:, ic * 1024:(ic + 1) * 1024],
                in_=d["Wd0"][half * 2048 + ic * 128: half * 2048 + ic * 128 + 128, :]),
                after=wdep, dma=True, tag="wd"))
        for tc in range(NTC):
            xb = (tc % 2) * 4096
            hc, tcp = (half, tc - 2) if tc >= 2 else (half - 1, NTC - 2 + tc)
            xnd = [a_done_hist.get((hc, tcp))]
            xnld = p.op("sync", lambda e, tc=tc, xb=xb: e.dma_start(
                out=xnf[:, xb:xb + 4096].rearrange("p (c t) -> p c t", c=8),
                in_=d["xn_f"][:, tc * 512:(tc + 1) * 512].rearrange("(c p) t -> p c t", p=128)),
                after=xnd, dma=True, tag="xnld")
            aev_list = []
            for ic in range(16):
                par = ic % 2
                if ic >= 2:
                    guev = aev_list[ic - 2]
                else:
                    guev = a_done_hist.get((half, tc - 1)) or (a_done_hist.get((half - 1, NTC - 1)) if half else None)
                mmdep = [xnld] + (wl if (tc == 0 and ic == 0) else []) + ([guev] if guev else [])
                adep = []
                if ic == 0:
                    pk = (half, tc - 1) if tc else (half - 1, NTC - 1)
                    if pk in dn_seq_hist:
                        adep.append(dn_seq_hist[pk][-1])
                last = None
                for kc in range(8):
                    last = p.op("tensor", lambda e, kc=kc, ic=ic, par=par, xb=xb: e.matmul(
                        ps_gu[par][:], wg[:, kc * 2048 + ic * 128: kc * 2048 + ic * 128 + 128],
                        xnf[:, xb + kc * 512: xb + kc * 512 + 512],
                        start=(kc == 0), stop=(kc == 7)),
                        after=(mmdep + adep) if kc == 0 else [], tag="gmm")
                gev = p.op("scalar", lambda e, ic=ic, par=par, half=half: e.activation(
                    gbuf[:], ps_gu[par][:], AF.Silu, bias=bg[:, half * 16 + ic: half * 16 + ic + 1]),
                    after=[last, l_bg], tag="gev")
                for kc in range(8):
                    last = p.op("tensor", lambda e, kc=kc, ic=ic, par=par, xb=xb: e.matmul(
                        ps_gu[par][:], wu[:, kc * 2048 + ic * 128: kc * 2048 + ic * 128 + 128],
                        xnf[:, xb + kc * 512: xb + kc * 512 + 512],
                        start=(kc == 0), stop=(kc == 7)),
                        after=[gev] if kc == 0 else [], tag="umm")
                aev = p.op("vector", lambda e, ic=ic, par=par, half=half: e.scalar_tensor_tensor(
                    out=abuf[:, ic * 512:(ic + 1) * 512], in0=ps_gu[par][:],
                    scalar=bu[:, half * 16 + ic: half * 16 + ic + 1], in1=gbuf[:],
                    op0=ALU.add, op1=ALU.mult),
                    after=[last, l_bu, p.force(gev)] + adep, tag="aev")
                aev_list.append(aev)
            a_done = aev_list[-1]
            a_done_hist[(half, tc)] = a_done
            dn_seq = []
            for piece in range(8):
                tt, nh = divmod(piece, 2)
                par = piece % 2
                dndep = [a_done]
                if piece >= 2:
                    dndep.append(dn_seq[piece - 2])
                else:
                    pk = (half, tc - 1) if tc else (half - 1, NTC - 1)
                    if pk in dn_seq_hist:
                        dndep.append(dn_seq_hist[pk][piece - 2])
                last = None
                for ic in range(16):
                    last = p.op("tensor", lambda e, ic=ic, tt=tt, nh=nh, par=par: e.matmul(
                        ps_dn[par][:], abuf[:, ic * 512 + tt * 128: ic * 512 + tt * 128 + 128],
                        wd[:, ic * 1024 + nh * 512: ic * 1024 + (nh + 1) * 512],
                        start=(ic == 0), stop=(ic == 15)),
                        after=dndep if ic == 0 else [], tag="dmm")
                ob = tt * 1024 + nh * 512
                evdep = [last]
                if piece == 0 and st_hist.get((half, tc - 1)) is not None:
                    evdep.append(st_hist[(half, tc - 1)])
                if piece == 0 and half == 1 and tc == 0 and st_hist.get((0, NTC - 1)) is not None:
                    evdep.append(st_hist[(0, NTC - 1)])
                ev = p.op("vector", lambda e, par=par, ob=ob: e.tensor_copy(
                    out_sb[:, ob:ob + 512], ps_dn[par][:]), after=evdep, tag="dev")
                dn_seq.append(ev)
            dn_seq_hist[(half, tc)] = dn_seq
            sdep = [dn_seq[-1]] + [pre_st[tc * 4 + j] for j in range(4)]
            st = p.op("gpsimd", lambda e, tc=tc: e.dma_start(
                out=d["x1"][tc * 512:(tc + 1) * 512, :].rearrange("(a p) q -> p a q", p=128),
                in_=out_sb[:].rearrange("p (a q) -> p a q", a=4),
                accum_op=ALU.add), after=sdep, dma=True, tag="x1acc")
            st_hist[(half, tc)] = st


def emit_l1pre(p, c, d, h0):
    T, NT = c.T, c.NT
    wq = c.sbuf("e_wq", [128, 8 * 1024], BF16)
    wk = c.sbuf("e_wk", [128, 8 * 512], BF16)
    wv = c.sbuf("e_wv", [128, 8 * 512], BF16)
    wr = c.sbuf("e_wr", [128, 8 * 8], BF16)
    rbb = c.sbuf("e_rbb", [128, 8], F32)
    xn_f = c.sbuf("e_xnf", [128, 8 * T], BF16)
    x1t = c.sbuf("e_x1t", [128, 2 * 1024], F32)
    sqbuf = c.sbuf("e_sq", [128, 1024], F32)
    stat = c.sbuf("e_stat", [128, 12], F32)
    aggr = c.sbuf("e_aggr", [128, 2], F32)
    mv = c.sbuf("e_mv", [128, 4], F32)
    xnt_sb = c.sbuf("e_xnt", [128, 2 * 1024], BF16)
    iot = c.sbuf("e_iot", [128, T], U32)
    cosf = c.sbuf("e_cos", [128, T], F32)
    sinm = c.sbuf("e_sinm", [128, T], F32)
    rscr = c.sbuf("e_rscr", [128, T], F32)
    ivf = c.sbuf("e_ivf", [128, 4], F32)
    rope_a = c.sbuf("e_ra", [128, 512], BF16)
    rope_b = c.sbuf("e_rb", [128, 512], BF16)
    evac = c.sbuf("e_evac", [128, 4 * 512], BF16)
    logits = c.sbuf("e_log", [128, 8], F32)
    probs = c.sbuf("e_pr", [128, 8], F32)
    top2 = c.sbuf("e_t2", [128, 8], F32)
    cw_sb = c.sbuf("e_cw", [128, 2 * 8], F32)
    ps_tp = [c.psum("e_pstp0", [128, 128], BF16), c.psum("e_pstp1", [128, 128], BF16)]
    ps_mm = c.psum("e_psmm", [128, 4 * 512], F32)
    ps_r = c.psum("e_psr", [128, 128], F32)

    wl = []
    for kc in range(8):
        wl.append(p.op("gpsimd", lambda e, kc=kc: e.dma_start(
            out=wq[:, kc * 1024:(kc + 1) * 1024],
            in_=d["Wq1"][kc * 128:(kc + 1) * 128, :]), dma=True, tag="wq1"))
        wl.append(p.op("gpsimd", lambda e, kc=kc: e.dma_start(
            out=wk[:, kc * 512:(kc + 1) * 512],
            in_=d["Wk1"][kc * 128:(kc + 1) * 128, :]), dma=True, tag="wk1"))
        wl.append(p.op("gpsimd", lambda e, kc=kc: e.dma_start(
            out=wv[:, kc * 512:(kc + 1) * 512],
            in_=d["Wv1"][kc * 128:(kc + 1) * 128, :]), dma=True, tag="wv1"))
        wl.append(p.op("gpsimd", lambda e, kc=kc: e.dma_start(
            out=wr[:, kc * 8:(kc + 1) * 8],
            in_=d["WrT"][kc * 128:(kc + 1) * 128, :]), dma=True, tag="wr1"))
    l_rb = p.op("sync", lambda e: e.dma_start(
        out=rbb[:], in_=d["rb_row"][0:1, :].to_broadcast([128, 8])), dma=True, tag="rb")

    g1 = p.op("gpsimd", lambda e: e.iota(iot[:, 0:1], pattern=[[0, 1]], base=0,
                                         channel_multiplier=1), tag="iotp")
    g2 = None
    v1 = p.op("vector", lambda e: e.tensor_copy(ivf[:, 0:1], iot[:, 0:1]), after=[g1], tag="ivc")
    v2a = p.op("vector", lambda e: e.tensor_scalar(
        ivf[:, 1:2], ivf[:, 0:1], 64.0, -64.0, op0=ALU.is_ge, op1=ALU.mult),
        after=[p.force(v1)], tag="ivge")
    v2 = p.op("vector", lambda e: e.tensor_add(
        ivf[:, 1:2], ivf[:, 1:2], ivf[:, 0:1]), after=[p.force(v2a)], tag="ivmod")
    v3 = p.op("vector", lambda e: e.tensor_scalar(
        ivf[:, 1:2], ivf[:, 1:2], -math.log(10000.0) / 64.0, None, op0=ALU.mult),
        after=[p.force(v2)], tag="ivsc")
    e1 = p.op("scalar", lambda e: e.activation(ivf[:, 0:1], ivf[:, 1:2], AF.Exp),
              after=[v3], tag="ivexp")
    g2 = p.op("gpsimd", lambda e: e.iota(iot[:], pattern=[[1, T]], base=0,
                                          channel_multiplier=0), after=[v1], tag="iott")
    v4 = p.op("vector", lambda e: e.tensor_copy(rscr[:], iot[:]), after=[g2], tag="tf")
    v5 = p.op("vector", lambda e: e.tensor_scalar(
        rscr[:], rscr[:], ivf[:, 0:1], None, op0=ALU.mult), after=[p.force(e1)], tag="xmul")
    v6 = p.op("vector", lambda e: e.tensor_scalar(
        cosf[:], rscr[:], 1.0 / (2 * math.pi), 0.5, op0=ALU.mult, op1=ALU.add), tag="ydiv")
    v7 = p.op("vector", lambda e: e.tensor_copy(iot[:], cosf[:]), tag="ytrunc")
    v8 = p.op("vector", lambda e: e.tensor_copy(cosf[:], iot[:]), tag="yback")
    v9 = p.op("vector", lambda e: e.scalar_tensor_tensor(
        out=rscr[:], in0=cosf[:], scalar=-2 * math.pi, in1=rscr[:],
        op0=ALU.mult, op1=ALU.add), tag="rred")
    sn = p.op("scalar", lambda e: e.activation(sinm[:], rscr[:], AF.Sin), after=[v9], tag="sin")
    sgn = p.op("gpsimd", lambda e: e.memset(ivf[:, 2:3], 1.0), tag="sgn1")
    sgn2 = p.op("gpsimd", lambda e: e.memset(ivf[0:64, 2:3], -1.0), after=[sgn], tag="sgn2")
    sm = p.op("vector", lambda e: e.tensor_scalar(
        sinm[:], sinm[:], ivf[:, 2:3], None, op0=ALU.mult), after=[sn, sgn2], tag="sinm")
    ch = p.op("scalar", lambda e: e.activation(cosf[:], rscr[:], AF.Sin, scale=c.cc("half")),
              after=[v9, h0["consts"], p.force(v8)], tag="sinh")
    ch2 = p.op("scalar", lambda e: e.square(cosf[:], cosf[:]), tag="sq")
    ch3 = p.op("vector", lambda e: e.tensor_scalar(
        cosf[:], cosf[:], -2.0, 1.0, op0=ALU.mult, op1=ALU.add), after=[ch2], tag="cosf")
    rope_tabs = [sm, ch3]

    xn_done = []
    tp_done = []
    for i in range(NT):
        b = (i % 2) * 1024
        dep = [xn_done[i - 2]] if i >= 2 else []
        xld = p.op("sync", lambda e, i=i, b=b: e.dma_start(
            out=x1t[:, b:b + 1024], in_=d["x1"][i * 128:(i + 1) * 128, :]),
            after=dep, dma=True, tag=f"x1ld{i}")
        rs1 = p.op("vector", lambda e, b=b: e.reduce_sum(stat[:, 0:1], x1t[:, b:b + 1024], axis=AX), after=[xld], tag="lnsum")
        mn = p.op("vector", lambda e: e.tensor_scalar(aggr[:, 0:1], stat[:, 0:1], 1.0 / 1024, None, op0=ALU.mult), after=[p.force(rs1)], tag="lnmean")
        sqv = p.op("scalar", lambda e, b=b: e.square(sqbuf[:], x1t[:, b:b + 1024]), after=[xld], tag="lnsq")
        rs2 = p.op("vector", lambda e: e.reduce_sum(stat[:, 1:2], sqbuf[:], axis=AX), after=[sqv], tag="lnsum2")
        ex2 = p.op("vector", lambda e: e.tensor_scalar(aggr[:, 1:2], stat[:, 1:2], 1.0 / 1024, None, op0=ALU.mult), after=[p.force(rs2)], tag="lnex2")
        nv = p.op("vector", lambda e: e.scalar_tensor_tensor(
            out=mv[:, 2:3], in0=aggr[:, 0:1], scalar=aggr[:, 0:1], in1=aggr[:, 1:2],
            op0=ALU.mult, op1=ALU.subtract), after=[p.force(mn), p.force(ex2)], tag="lnnegvar")
        veps = p.op("vector", lambda e: e.tensor_scalar(
            mv[:, 2:3], mv[:, 2:3], -1.0, 1e-5, op0=ALU.mult, op1=ALU.add), after=[p.force(nv)], tag="lnveps")
        sq = p.op("scalar", lambda e: e.sqrt(mv[:, 3:4], mv[:, 2:3]), after=[p.force(veps)], tag="sqrt")
        rc = p.op("vector", lambda e: e.reciprocal(mv[:, 0:1], mv[:, 3:4]), after=[p.force(sq)], tag="recip")
        nm = p.op("vector", lambda e: e.scalar_tensor_tensor(
            out=mv[:, 1:2], in0=aggr[:, 0:1], scalar=-1.0, in1=mv[:, 0:1],
            op0=ALU.mult, op1=ALU.mult), after=[p.force(rc)], tag="negm")
        dep2 = [p.force(rc), p.force(nm)]
        if i >= 2:
            dep2.append(tp_done[(i - 2) * 8 + 7])
        xn = p.op("scalar", lambda e, b=b: e.activation(
            xnt_sb[:, b:b + 1024], x1t[:, b:b + 1024], AF.Identity,
            bias=mv[:, 1:2], scale=mv[:, 0:1]), after=dep2, tag=f"xn1_{i}")
        xn_done.append(xn)
        for kc in range(8):
            par = (i * 8 + kc) % 2
            dep3 = [xn, h0["ident"]]
            if i * 8 + kc >= 2:
                dep3.append(tp_done[-2])
            tp = p.op("tensor", lambda e, b=b, kc=kc, par=par: e.transpose(
                ps_tp[par][:], xnt_sb[:, b + kc * 128:b + kc * 128 + 128],
                c.sb["ident"][:]), after=dep3, tag="tp1")
            ev = p.op("scalar", lambda e, i=i, kc=kc, par=par: e.copy(
                xn_f[:, kc * T + i * 128: kc * T + i * 128 + 128], ps_tp[par][:]),
                after=[tp], tag="tpev1")
            tp_done.append(ev)
    all_xnf = tp_done[-1]
    for kc in range(8):
        p.op("sync", lambda e, kc=kc: e.dma_start(
            out=d["xn1_f"][kc * 128:(kc + 1) * 128, :], in_=xn_f[:, kc * T:(kc + 1) * T]),
            after=[all_xnf], dma=True, tag="xn1fst")

    mm_done = []
    n_ev = 0

    def head_proj(w_sb, wcols, nheads, dname):
        nonlocal n_ev
        for hd in range(nheads):
            for tc in range(T // 512):
                bank = n_ev % 4
                dep = [all_xnf] + (wl if n_ev == 0 else [])
                if n_ev >= 4:
                    dep.append(mm_done[n_ev - 4])
                last = None
                for kc in range(8):
                    last = p.op("tensor", lambda e, kc=kc, hd=hd, tc=tc, bank=bank, w_sb=w_sb, wcols=wcols: e.matmul(
                        ps_mm[:, bank * 512:(bank + 1) * 512],
                        w_sb[:, kc * wcols + hd * 128: kc * wcols + hd * 128 + 128],
                        xn_f[:, kc * T + tc * 512: kc * T + tc * 512 + 512],
                        start=(kc == 0), stop=(kc == 7)),
                        after=dep if kc == 0 else [], tag="hq")
                ev0 = p.op("scalar", lambda e, bank=bank: e.copy(
                    rope_a[:], ps_mm[:, bank * 512:(bank + 1) * 512]),
                    after=[last] + ([mm_done[-1]] if mm_done else []), tag="rpev")
                r1 = p.op("vector", lambda e: e.tensor_copy(rope_b[0:64, :], rope_a[64:128, :]),
                          after=[ev0], tag="rot1")
                r2 = p.op("vector", lambda e: e.tensor_copy(rope_b[64:128, :], rope_a[0:64, :]), tag="rot2")
                m1 = p.op("vector", lambda e, tc=tc: e.tensor_mul(
                    rope_a[:], rope_a[:], cosf[:, tc * 512:(tc + 1) * 512]),
                    after=rope_tabs + [r2], tag="rm1")
                m2 = p.op("vector", lambda e, tc=tc: e.tensor_mul(
                    rope_b[:], rope_b[:], sinm[:, tc * 512:(tc + 1) * 512]), tag="rm2")
                m3 = p.op("vector", lambda e: e.tensor_add(rope_a[:], rope_a[:], rope_b[:]), tag="rm3")
                st = p.op("sync", lambda e, hd=hd, tc=tc, dname=dname: e.dma_start(
                    out=d[dname][hd * 128:(hd + 1) * 128, tc * 512:(tc + 1) * 512],
                    in_=rope_a[:]), after=[m3], dma=True, tag="rpst")
                mm_done.append(st)
                n_ev += 1

    head_proj(wq, 1024, 8, "q1_f")
    head_proj(wk, 512, 4, "k1_f")

    for i in range(NT):
        bank = n_ev % 4
        dep = [all_xnf]
        if n_ev >= 4:
            dep.append(mm_done[n_ev - 4])
        last = None
        for kc in range(8):
            last = p.op("tensor", lambda e, kc=kc, i=i, bank=bank: e.matmul(
                ps_mm[:, bank * 512:(bank + 1) * 512],
                xn_f[:, kc * T + i * 128: kc * T + i * 128 + 128],
                wv[:, kc * 512:(kc + 1) * 512],
                start=(kc == 0), stop=(kc == 7)), after=dep if kc == 0 else [], tag="vmm1")
        eb = (n_ev % 4) * 512
        ev = p.op("vector", lambda e, bank=bank, eb=eb: e.tensor_copy(
            evac[:, eb:eb + 512], ps_mm[:, bank * 512:(bank + 1) * 512]), after=[last], tag="vev1")
        st = p.op("sync", lambda e, i=i, eb=eb: e.dma_start(
            out=d["v1_t"][i * 128:(i + 1) * 128, :], in_=evac[:, eb:eb + 512]),
            after=[ev], dma=True, tag="vst1")
        mm_done.append(st)
        n_ev += 1

    prev_cw = [None, None]
    prev_ev = None
    for i in range(NT):
        dep = [all_xnf, l_rb] + ([prev_cw[i % 2]] if prev_cw[i % 2] else []) \
            + ([prev_ev] if prev_ev else [])
        last = None
        for kc in range(8):
            last = p.op("tensor", lambda e, kc=kc, i=i: e.matmul(
                ps_r[:, 0:8], xn_f[:, kc * T + i * 128: kc * T + i * 128 + 128],
                wr[:, kc * 8:(kc + 1) * 8],
                start=(kc == 0), stop=(kc == 7)), after=dep if kc == 0 else [], tag="rmm")
        lg = p.op("vector", lambda e: e.tensor_add(logits[:], ps_r[:, 0:8], rbb[:]),
                  after=[last], tag="lg")
        sg = p.op("scalar", lambda e: e.activation(probs[:], logits[:], AF.Sigmoid),
                  after=[lg], tag="sig")
        t2 = p.op("vector", lambda e: e.max(out=top2[:], in_=probs[:]), after=[sg], tag="top2")
        cb = (i % 2) * 8
        cw = p.op("vector", lambda e, cb=cb: e.scalar_tensor_tensor(
            out=cw_sb[:, cb:cb + 8], in0=probs[:], scalar=top2[:, 1:2], in1=probs[:],
            op0=ALU.is_ge, op1=ALU.mult), after=[p.force(t2)], tag="cw")
        prev_ev = cw
        st = p.op("sync", lambda e, i=i, cb=cb: e.dma_start(
            out=d["cw_t"][i * 128:(i + 1) * 128, :], in_=cw_sb[:, cb:cb + 8]),
            after=[cw], dma=True, tag="cwst")
        prev_cw[i % 2] = st


def emit_l1attn(p, c, d, h0):
    T, NT = c.T, c.NT
    k_f = c.sbuf("f_kf", [128, 4 * T], BF16)
    v_sb = c.sbuf("f_v", [128, NT * 512], BF16)
    wo = c.sbuf("f_wo", [128, 8 * 1024], BF16)
    q_sb = c.sbuf("f_q", [128, 2 * 128], BF16)
    scores = c.sbuf("f_sc", [128, T], F32)
    trimask = c.sbuf("f_tri", [128, 128], F32)
    iotp = c.sbuf("f_iotp", [128, 128], U32)
    iotf = c.sbuf("f_iotf", [128, 129], F32)
    pexp = c.sbuf("f_p", [128, T], BF16)
    pt = c.sbuf("f_pt", [128, T], BF16)
    nmx = c.sbuf("f_nmx", [128, 2], F32)
    den = c.sbuf("f_den", [128, 1], F32)
    o_sb = c.sbuf("f_o", [128, 128], BF16)
    o_f = c.sbuf("f_of", [128, 8 * 128], BF16)
    at_sb = c.sbuf("f_at", [128, 2 * 1024], F32)
    ps_qk = [c.psum("f_psqk0", [128, 512], F32), c.psum("f_psqk1", [128, 512], F32)]
    ps_tp = [c.psum("f_pstp0", [128, 128], BF16), c.psum("f_pstp1", [128, 128], BF16)]
    ps_pv = c.psum("f_pspv", [128, 128], F32)
    ps_pr = c.psum("f_pspr", [128, 512], F32)

    kl = []
    for kc in range(4):
        kl.append(p.op("sync", lambda e, kc=kc: e.dma_start(
            out=k_f[:, kc * T:(kc + 1) * T],
            in_=d["k1_f"][kc * 128:(kc + 1) * 128, :]), dma=True, tag="kld1"))
    vl = []
    for i in range(NT):
        vl.append(p.op("sync", lambda e, i=i: e.dma_start(
            out=v_sb[:, i * 512:(i + 1) * 512],
            in_=d["v1_t"][i * 128:(i + 1) * 128, :]), dma=True, tag="vld1"))
    wl = []
    for kc in range(8):
        wl.append(p.op("gpsimd", lambda e, kc=kc: e.dma_start(
            out=wo[:, kc * 1024:(kc + 1) * 1024],
            in_=d["Wo1"][kc * 128:(kc + 1) * 128, :]), dma=True, tag="wold"))
    g1 = p.op("gpsimd", lambda e: e.iota(iotp[:], pattern=[[1, 128]], base=0,
                                         channel_multiplier=0), tag="fiota")
    m1 = p.op("vector", lambda e: e.tensor_copy(iotf[:, 0:128], iotp[:]), after=[g1], tag="fiotf")
    g2 = p.op("gpsimd", lambda e: e.iota(iotp[:, 0:1], pattern=[[0, 1]], base=0,
                                         channel_multiplier=1), after=[m1], tag="fiotp")
    m2 = p.op("vector", lambda e: e.tensor_copy(iotf[:, 128:129], iotp[:, 0:1]), after=[g2], tag="fiotpf")
    m3 = p.op("vector", lambda e: e.tensor_scalar(
        trimask[:], iotf[:, 0:128], iotf[:, 128:129], NEG, op0=ALU.is_gt, op1=ALU.mult),
        after=[p.force(m2)], tag="ftri")

    prev = {"qld": {}, "stt": [None, None], "exp": None, "ptev": [None, None],
            "oev": None, "otev": {}, "atst": [None, None], "pr_ev": None, "qk_last": {}}
    for i in range(NT):
        nkeys = (i + 1) * 128
        nchunks = (nkeys + 511) // 512
        for h in range(8):
            kv = h // 2
            qb = ((i * 8 + h) % 2) * 128
            qdep = [prev["qk_last"].get((i * 8 + h) - 2)]
            if i == 0 and h == 0:
                qdep += kl
            qld = p.op("sync", lambda e, i=i, h=h, qb=qb: e.dma_start(
                out=q_sb[:, qb:qb + 128],
                in_=d["q1_f"][h * 128:(h + 1) * 128, i * 128:(i + 1) * 128]),
                after=qdep, dma=True, tag="qld1")
            ev_done = []
            for nk in range(nchunks):
                par = nk % 2
                ncols = min(512, nkeys - nk * 512)
                mmdep = [qld]
                if nk >= 2:
                    mmdep.append(ev_done[nk - 2])
                else:
                    mmdep.append(prev["stt"][par])
                last = p.op("tensor", lambda e, nk=nk, par=par, kv=kv, qb=qb, ncols=ncols: e.matmul(
                    ps_qk[par][:, 0:ncols], q_sb[:, qb:qb + 128],
                    k_f[:, kv * T + nk * 512: kv * T + nk * 512 + ncols],
                    start=True, stop=True), after=mmdep, tag="qkmm1")
                prev["qk_last"][i * 8 + h] = last
                ev = p.op("scalar", lambda e, nk=nk, par=par, ncols=ncols: e.activation(
                    scores[:, nk * 512: nk * 512 + ncols], ps_qk[par][:, 0:ncols],
                    AF.Identity, scale=c.cc("scale1")),
                    after=[last, h0["consts"]], tag="scev")
                ev_done.append(ev)
                prev["stt"][par] = ev
            mk = p.op("vector", lambda e, nkeys=nkeys: e.tensor_add(
                scores[:, nkeys - 128: nkeys], scores[:, nkeys - 128: nkeys], trimask[:]),
                after=[ev_done[-1], m3], tag="mask")
            mx = p.op("vector", lambda e, nkeys=nkeys: e.reduce_max(
                nmx[:, 0:1], scores[:, 0:nkeys], axis=AX, negate=True), after=[mk], tag="mx1")
            expdep = [p.force(mx)]
            if prev["exp"] is not None:
                expdep.append(prev["exp"])
            ex = p.op("scalar", lambda e, nkeys=nkeys: e.activation(
                pexp[:, 0:nkeys], scores[:, 0:nkeys], AF.Exp, bias=nmx[:, 0:1],
                accum_out=den[:]), after=expdep, tag="exp1")
            rc = p.op("vector", lambda e: e.reciprocal(nmx[:, 1:2], den[:]), after=[ex], tag="rcp1")
            tpev = []
            for c32 in range(i + 1):
                par = c32 % 2
                dep2 = [ex]
                if c32 >= 2:
                    dep2.append(tpev[c32 - 2])
                elif prev["ptev"][par] is not None:
                    dep2.append(prev["ptev"][par])
                tp = p.op("tensor", lambda e, c32=c32, par=par: e.transpose(
                    ps_tp[par][:], pexp[:, c32 * 128:(c32 + 1) * 128], c.sb["ident"][:]),
                    after=dep2, tag="ptp1")
                ev = p.op("scalar", lambda e, c32=c32, par=par: e.copy(
                    pt[:, c32 * 128:(c32 + 1) * 128], ps_tp[par][:]), after=[tp], tag="ptev1")
                tpev.append(ev)
            if len(tpev) >= 2:
                prev["ptev"] = [tpev[-2], tpev[-1]]
            else:
                prev["ptev"] = [prev["ptev"][1] if prev["ptev"][1] else tpev[-1], tpev[-1]]
            prev["exp"] = tpev[-1]
            dep3 = [tpev[-1]] + (vl if (i == 0 and h == 0) else []) + \
                   ([prev["oev"]] if prev["oev"] is not None else [])
            last = None
            for c32 in range(i + 1):
                last = p.op("tensor", lambda e, c32=c32, kv=kv, i=i: e.matmul(
                    ps_pv[:], pt[:, c32 * 128:(c32 + 1) * 128],
                    v_sb[:, c32 * 512 + kv * 128: c32 * 512 + kv * 128 + 128],
                    start=(c32 == 0), stop=(c32 == i)),
                    after=dep3 if c32 == 0 else [], tag="pvmm1")
            oev = p.op("scalar", lambda e: e.activation(
                o_sb[:], ps_pv[:], AF.Identity, scale=nmx[:, 1:2]),
                after=[last, p.force(rc)], tag="oev1")
            prev["oev"] = oev
            par = h % 2
            dep4 = [oev]
            if prev["otev"].get(h - 2) is not None:
                dep4.append(prev["otev"][h - 2])
            elif h < 2 and len(tpev) >= 2:
                dep4.append(tpev[-1])
            otp = p.op("tensor", lambda e, par=par: e.transpose(
                ps_tp[par][:], o_sb[:], c.sb["ident"][:]),
                after=dep4 + [tpev[-2] if len(tpev) >= 2 else tpev[-1]], tag="otp1")
            otev = p.op("scalar", lambda e, h=h, par=par: e.copy(
                o_f[:, h * 128:(h + 1) * 128], ps_tp[par][:]), after=[otp], tag="otev1")
            prev["otev"][h] = otev
            prev["ptev"] = [otev, otev]
        ab = (i % 2) * 1024
        all_o = prev["otev"][7]
        for half in range(2):
            dep5 = [all_o] + (wl if i == 0 else []) + \
                   ([prev["atst"][half]] if prev["atst"][half] is not None else []) + \
                   ([prev["pr_ev"]] if prev["pr_ev"] is not None else [])
            last = None
            for kc in range(8):
                last = p.op("tensor", lambda e, kc=kc, half=half: e.matmul(
                    ps_pr[:], o_f[:, kc * 128:(kc + 1) * 128],
                    wo[:, kc * 1024 + half * 512: kc * 1024 + (half + 1) * 512],
                    start=(kc == 0), stop=(kc == 7)),
                    after=dep5 if kc == 0 else [], tag="prmm1")
            ev = p.op("vector", lambda e, half=half, ab=ab: e.tensor_copy(
                at_sb[:, ab + half * 512: ab + (half + 1) * 512], ps_pr[:]),
                after=[last], tag="prev1")
            prev["pr_ev"] = ev
            st = p.op("sync", lambda e, i=i, half=half, ab=ab: e.dma_start(
                out=d["attn1"][i * 128:(i + 1) * 128, half * 512:(half + 1) * 512],
                in_=at_sb[:, ab + half * 512: ab + (half + 1) * 512]),
                after=[ev], dma=True, tag="atst1")
            prev["atst"][half] = st
        prev["otev"] = {-2: prev["pr_ev"], -1: prev["pr_ev"]}


def emit_l1moe(p, c, d, h0):
    T, NT = c.T, c.NT
    NTC = T // 512
    wgt = c.sbuf("g_wg", [128, 8 * 2048], BF16)
    wut = c.sbuf("g_wu", [128, 8 * 2048], BF16)
    wdt = c.sbuf("g_wd", [128, 16 * 1024], BF16)
    xnf = c.sbuf("g_xnf", [128, 2 * 4096], BF16)
    gbuf = c.sbuf("g_g", [128, 512], BF16)
    abuf = c.sbuf("g_a", [128, 16 * 512], BF16)
    cwc = c.sbuf("g_cw", [128, 8 * (c.T // 128)], F32)
    out_sb = c.sbuf("g_out", [128, 4096], F32)
    rx1 = c.sbuf("g_rx1", [128, 2 * 1024], F32)
    rat = c.sbuf("g_rat", [128, 2 * 1024], F32)
    rmo = c.sbuf("g_rmo", [128, 2 * 1024], F32)
    fout = c.sbuf("g_fout", [128, 2 * 1024], F32)
    ps_gu = [c.psum("g_psg0", [128, 512], F32), c.psum("g_psg1", [128, 512], F32)]
    ps_dn = [c.psum("g_psd0", [128, 512], F32), c.psum("g_psd1", [128, 512], F32)]

    l_cw = p.op("sync", lambda e: e.dma_start(
        out=cwc[:].rearrange("p (n q) -> p n q", q=8),
        in_=d["cw_t"][:, :].rearrange("(n p) q -> p n q", p=128)), dma=True, tag="cwld")

    a_done_hist = {}
    dn_seq_hist = {}
    st_hist = {}
    xn_hist = {}
    for ex in range(9):
        routed = ex > 0
        wsrc = (("WgTs", "WuTs", "WdTs") if not routed else ("WgT", "WuT", "WdT"))
        wdep = [dn_seq_hist[(ex - 1, NTC - 1)][-1]] if ex else []
        wl = []
        for kc in range(8):
            wl.append(p.op("gpsimd", lambda e, kc=kc, ex=ex, w=wsrc[0]: e.dma_start(
                out=wgt[:, kc * 2048:(kc + 1) * 2048],
                in_=(d[w][kc * 128:(kc + 1) * 128, :] if ex == 0 else
                     d[w][ex - 1, kc * 128:(kc + 1) * 128, :])),
                after=wdep, dma=True, tag="gwg"))
            wl.append(p.op("gpsimd", lambda e, kc=kc, ex=ex, w=wsrc[1]: e.dma_start(
                out=wut[:, kc * 2048:(kc + 1) * 2048],
                in_=(d[w][kc * 128:(kc + 1) * 128, :] if ex == 0 else
                     d[w][ex - 1, kc * 128:(kc + 1) * 128, :])),
                after=wdep, dma=True, tag="gwu"))
        for ic in range(16):
            wl.append(p.op("gpsimd", lambda e, ic=ic, ex=ex, w=wsrc[2]: e.dma_start(
                out=wdt[:, ic * 1024:(ic + 1) * 1024],
                in_=(d[w][ic * 128:(ic + 1) * 128, :] if ex == 0 else
                     d[w][ex - 1, ic * 128:(ic + 1) * 128, :])),
                after=wdep, dma=True, tag="gwd"))
        for tc in range(NTC):
            xb = (tc % 2) * 4096
            hc, tcp = (ex, tc - 2) if tc >= 2 else (ex - 1, NTC - 2 + tc)
            xnd = [xn_hist.get((hc, tcp))]
            xnld = p.op("sync", lambda e, tc=tc, xb=xb: e.dma_start(
                out=xnf[:, xb:xb + 4096].rearrange("p (c t) -> p c t", c=8),
                in_=d["xn1_f"][:, tc * 512:(tc + 1) * 512].rearrange("(c p) t -> p c t", p=128)),
                after=xnd, dma=True, tag="gxnld")
            aev_list = []
            for ic in range(16):
                par = ic % 2
                if ic >= 2:
                    guev = aev_list[ic - 2]
                else:
                    guev = a_done_hist.get((ex, tc - 1)) or (a_done_hist.get((ex - 1, NTC - 1)) if ex else None)
                mmdep = [xnld] + (wl if (tc == 0 and ic == 0) else []) + ([guev] if guev else [])
                adep = []
                if ic == 0:
                    pk = (ex, tc - 1) if tc else (ex - 1, NTC - 1)
                    if pk in dn_seq_hist:
                        adep.append(dn_seq_hist[pk][-1])
                last = None
                for kc in range(8):
                    last = p.op("tensor", lambda e, kc=kc, ic=ic, par=par, xb=xb: e.matmul(
                        ps_gu[par][:], wgt[:, kc * 2048 + ic * 128: kc * 2048 + ic * 128 + 128],
                        xnf[:, xb + kc * 512: xb + kc * 512 + 512],
                        start=(kc == 0), stop=(kc == 7)),
                        after=(mmdep + adep) if kc == 0 else [], tag="ggmm")
                gev = p.op("scalar", lambda e, par=par: e.activation(
                    gbuf[:], ps_gu[par][:], AF.Silu), after=[last], tag="ggev")
                for kc in range(8):
                    last = p.op("tensor", lambda e, kc=kc, ic=ic, par=par, xb=xb: e.matmul(
                        ps_gu[par][:], wut[:, kc * 2048 + ic * 128: kc * 2048 + ic * 128 + 128],
                        xnf[:, xb + kc * 512: xb + kc * 512 + 512],
                        start=(kc == 0), stop=(kc == 7)),
                        after=[gev] if kc == 0 else [], tag="gumm")
                aev = p.op("vector", lambda e, ic=ic, par=par: e.scalar_tensor_tensor(
                    out=abuf[:, ic * 512:(ic + 1) * 512], in0=ps_gu[par][:],
                    scalar=0.0, in1=gbuf[:], op0=ALU.bypass, op1=ALU.mult),
                    after=[last, p.force(gev)] + adep, tag="gaev")
                aev_list.append(aev)
            a_done = aev_list[-1]
            a_done_hist[(ex, tc)] = a_done
            xn_hist[(ex, tc)] = a_done
            dn_seq = []
            for piece in range(8):
                tt, nh = divmod(piece, 2)
                par = piece % 2
                dndep = [a_done]
                if piece >= 2:
                    dndep.append(dn_seq[piece - 2])
                else:
                    pk = (ex, tc - 1) if tc else (ex - 1, NTC - 1)
                    if pk in dn_seq_hist:
                        dndep.append(dn_seq_hist[pk][piece - 2])
                last = None
                for ic in range(16):
                    last = p.op("tensor", lambda e, ic=ic, tt=tt, nh=nh, par=par: e.matmul(
                        ps_dn[par][:], abuf[:, ic * 512 + tt * 128: ic * 512 + tt * 128 + 128],
                        wdt[:, ic * 1024 + nh * 512: ic * 1024 + (nh + 1) * 512],
                        start=(ic == 0), stop=(ic == 15)),
                        after=dndep if ic == 0 else [], tag="gdmm")
                ob = tt * 1024 + nh * 512
                evdep = [last]
                if piece == 0 and st_hist.get((ex, tc - 1)) is not None:
                    evdep.append(st_hist[(ex, tc - 1)])
                if piece == 0 and tc == 0 and ex and st_hist.get((ex - 1, NTC - 1)) is not None:
                    evdep.append(st_hist[(ex - 1, NTC - 1)])
                if routed:
                    col = (tc * 4 + tt) * 8 + (ex - 1)
                    ev = p.op("vector", lambda e, par=par, ob=ob, col=col: e.tensor_scalar(
                        out_sb[:, ob:ob + 512], ps_dn[par][:], cwc[:, col:col + 1],
                        None, op0=ALU.mult), after=evdep + [l_cw], tag="gdev")
                else:
                    ev = p.op("vector", lambda e, par=par, ob=ob: e.tensor_copy(
                        out_sb[:, ob:ob + 512], ps_dn[par][:]), after=evdep, tag="gdev0")
                dn_seq.append(ev)
            dn_seq_hist[(ex, tc)] = dn_seq
            sdep = [dn_seq[-1]]
            if ex == 0:
                st = p.op("sync", lambda e, tc=tc: e.dma_start(
                    out=d["moe"][tc * 512:(tc + 1) * 512, :].rearrange("(a p) q -> p a q", p=128),
                    in_=out_sb[:].rearrange("p (a q) -> p a q", a=4)),
                    after=sdep, dma=True, tag="moest")
            else:
                st = p.op("gpsimd", lambda e, tc=tc: e.dma_start(
                    out=d["moe"][tc * 512:(tc + 1) * 512, :].rearrange("(a p) q -> p a q", p=128),
                    in_=out_sb[:].rearrange("p (a q) -> p a q", a=4),
                    accum_op=ALU.add), after=sdep + [st_hist[(0, tc)]], dma=True, tag="moeacc")
            st_hist[(ex, tc)] = st

    fin_v = {}
    fin_st = {}
    for i in range(NT):
        b = (i % 2) * 1024
        dep = [fin_v.get(i - 2), st_hist[(8, i // 4)]]
        l1 = p.op("sync", lambda e, i=i, b=b: e.dma_start(
            out=rx1[:, b:b + 1024], in_=d["x1"][i * 128:(i + 1) * 128, :]), after=dep, dma=True, tag="fx1")
        l2 = p.op("sync", lambda e, i=i, b=b: e.dma_start(
            out=rat[:, b:b + 1024], in_=d["attn1"][i * 128:(i + 1) * 128, :]), after=dep, dma=True, tag="fat")
        l3 = p.op("sync", lambda e, i=i, b=b: e.dma_start(
            out=rmo[:, b:b + 1024], in_=d["moe"][i * 128:(i + 1) * 128, :]), after=dep, dma=True, tag="fmo")
        sdep = [fin_st.get(i - 2)]
        v1 = p.op("vector", lambda e, b=b: e.tensor_add(
            fout[:, b:b + 1024], rx1[:, b:b + 1024], rat[:, b:b + 1024]),
            after=[l1, l2] + sdep, tag="fadd1")
        v2 = p.op("vector", lambda e, b=b: e.tensor_add(
            fout[:, b:b + 1024], fout[:, b:b + 1024], rmo[:, b:b + 1024]), after=[l3], tag="fadd2")
        fin_v[i] = v2
        fin_st[i] = p.op("sync", lambda e, i=i, b=b: e.dma_start(
            out=d["out"][i * 128:(i + 1) * 128, :], in_=fout[:, b:b + 1024]),
            after=[v2], dma=True, tag="fst")



def build_one(T, stages, dram_io):
    nc = bass.Bass("TRN2", target_bir_lowering=False, debug=False,
                   detect_race_conditions=False)
    specs = dict(SHARED_SPECS)
    specs.update(inter_specs(T))
    dram = {}
    for name, kind in dram_io.items():
        shape, dt = specs[name]
        k = {"in": "ExternalInput", "out": "ExternalOutput", "internal": "Internal"}[kind]
        dram[name] = nc.dram_tensor(name, shape, dt, kind=k).ap()
    with ExitStack() as ctx:
        c = Ctx(nc, ctx, T)
        p = Prog(nc, ctx)
        for st in stages:
            st(p, c, dram)
        p.emit()
    return nc


def launch_l0pre(T):
    def stages(p, c, d):
        h0 = emit_setup(p, c, d)
        emit_l0pre(p, c, d, h0)
    io = {k: "in" for k in ["x", "Wqkv", "bqkv_pp", "bv_row", "Wqp", "keysT", "ident_bf", "consts"]}
    io.update({k: "out" for k in ["xn_t", "xn_f", "qk0_f", "v0_t", "sims"]})
    return build_one(T, [stages], io)


def launch_l0attn(T):
    def stages(p, c, d):
        h0 = emit_setup(p, c, d)
        emit_l0attn(p, c, d, h0)
    io = {k: "in" for k in ["qk0_f", "v0_t", "Wproj", "bproj_row", "ident_bf", "consts"]}
    io.update({k: "out" for k in ["attn0"]})
    return build_one(T, [stages], io)


def launch_l0mlp(T):
    def stages(p, c, d):
        h0 = emit_setup(p, c, d)
        emit_l0mlp(p, c, d, h0)
    io = {k: "in" for k in ["xn_f", "x", "attn0", "pkm", "Wg0", "Wu0", "Wd0",
                            "bg0_pp", "bu0_pp", "bd0_row", "ident_bf", "consts"]}
    io.update({k: "out" for k in ["x1"]})
    return build_one(T, [stages], io)


def launch_l1pre(T):
    def stages(p, c, d):
        h0 = emit_setup(p, c, d, need_identf=True)
        emit_l1pre(p, c, d, h0)
    io = {k: "in" for k in ["x1", "Wq1", "Wk1", "Wv1", "WrT", "rb_row",
                            "ident_bf", "ident_f32", "consts"]}
    io.update({k: "out" for k in ["xn1_f", "q1_f", "k1_f", "v1_t", "cw_t"]})
    return build_one(T, [stages], io)


def launch_l1moe(T):
    def stages(p, c, d):
        h0 = emit_setup(p, c, d)
        emit_l1moe(p, c, d, h0)
    io = {k: "in" for k in ["xn1_f", "cw_t", "x1", "attn1", "WgTs", "WuTs", "WdTs",
                            "WgT", "WuT", "WdT", "ident_bf", "consts"]}
    io.update({k: "out" for k in ["out"]})
    io.update({k: "internal" for k in ["moe"]})
    return build_one(T, [stages], io)


def launch_l1attn(T):
    def stages(p, c, d):
        h0 = emit_setup(p, c, d)
        emit_l1attn(p, c, d, h0)
    io = {k: "in" for k in ["q1_f", "k1_f", "v1_t", "Wo1", "ident_bf", "consts"]}
    io.update({k: "out" for k in ["attn1"]})
    return build_one(T, [stages], io)


def launch_l0pkm(T):
    def stages(p, c, d):
        h0 = emit_setup(p, c, d, need_iota=True)
        emit_l0pkm(p, c, d, h0)
    io = {k: "in" for k in ["sims", "xn_t", "demb", "uemb", "ident_bf", "consts"]}
    io.update({k: "out" for k in ["pkm"]})
    io.update({k: "internal" for k in ["dembh", "uembh"]})
    return build_one(T, [stages], io)


FUSE_N = 7
FUSE_LIST = None


def emit_probe(p, c, d, h0):
    t1 = c.sbuf("pr_t1", [128, 512], F32)
    t2 = c.sbuf("pr_t2", [128, 512], F32)
    ps = c.psum("pr_ps", [128, 512], F32)
    l1 = p.op("sync", lambda e: e.dma_start(out=t1[:], in_=d["x"][0:128, 0:512]), dma=True, tag="prld")
    v1 = p.op("vector", lambda e: e.tensor_scalar_mul(t2[:], t1[:], 2.0), after=[l1], tag="prv")
    s1 = p.op("scalar", lambda e: e.copy(t1[:], t2[:]), after=[v1], tag="prs")
    g1 = p.op("gpsimd", lambda e: e.memset(t2[:, 0:4], 0.0), after=[s1], tag="prg")
    p.op("sync", lambda e: e.dma_start(out=d["pkm"][0:128, 0:512], in_=t1[:]), after=[s1], dma=True, tag="prst")


def launch_fused(T):
    def stages(p, c, d):
        h0 = emit_setup(p, c, d, need_iota=True)
        ems = FUSE_LIST if FUSE_LIST is not None else (
            emit_l0pre, emit_l0attn, emit_l0pkm, emit_l0mlp,
            emit_l1pre, emit_l1attn, emit_l1moe)[:FUSE_N]
        for em in ems:
            c.push()
            em(p, c, d, h0)
            c.pop()
            p.barrier()
    io = {}
    for k in ["x", "Wqkv", "bqkv_pp", "bv_row", "Wproj", "bproj_row", "Wg0",
              "Wu0", "Wd0", "bg0_pp", "bu0_pp", "bd0_row", "Wqp", "keysT",
              "demb", "uemb", "Wq1", "Wk1", "Wv1", "Wo1", "WgTs", "WuTs",
              "WdTs", "WgT", "WuT", "WdT", "WrT", "rb_row", "ident_bf",
              "consts"]:
        io[k] = "in"
    for k in ["xn_t", "xn_f", "qk0_f", "v0_t", "sims", "attn0", "pkm", "x1",
              "xn1_f", "q1_f", "k1_f", "v1_t", "cw_t", "attn1", "moe",
              "dembh", "uembh"]:
        io[k] = "internal"
    io["out"] = "out"
    return build_one(T, [stages], io)


FUSED = True

STAGE_IO = {
    "l0pre": (["x", "Wqkv", "bqkv_pp", "bv_row", "Wqp", "keysT"],
              ["xn_t", "xn_f", "qk0_f", "v0_t", "sims"]),
    "l0attn": (["qk0_f", "v0_t", "Wproj", "bproj_row"], ["attn0"]),
    "l0pkm": (["sims", "xn_t", "demb", "uemb"], ["pkm", "dembh", "uembh"]),
    "l0mlp": (["xn_f", "x", "attn0", "pkm", "Wg0", "Wu0", "Wd0", "bg0_pp",
               "bu0_pp", "bd0_row"], ["x1"]),
    "l1pre": (["x1", "Wq1", "Wk1", "Wv1", "WrT", "rb_row"],
              ["xn1_f", "q1_f", "k1_f", "v1_t", "cw_t"]),
    "l1attn": (["q1_f", "k1_f", "v1_t", "Wo1"], ["attn1"]),
    "l1moe": (["xn1_f", "cw_t", "x1", "attn1", "WgTs", "WuTs", "WdTs", "WgT",
               "WuT", "WdT"], ["out", "moe"]),
}
SETUP_INS = ["ident_bf", "consts"]
MODULE_INS = set(["x", "Wqkv", "bqkv_pp", "bv_row", "Wproj", "bproj_row",
                  "Wg0", "Wu0", "Wd0", "bg0_pp", "bu0_pp", "bd0_row", "Wqp",
                  "keysT", "demb", "uemb", "Wq1", "Wk1", "Wv1", "Wo1", "WgTs",
                  "WuTs", "WdTs", "WgT", "WuT", "WdT", "WrT", "rb_row",
                  "ident_bf", "consts"])

FUSE_GROUPS = [["l0pre", "l0pkm", "l0attn"], ["l0mlp"], ["l1pre"], ["l1attn"], ["l1moe"]]


def launch_group(T, group):
    ems = {"l0pre": emit_l0pre, "l0attn": emit_l0attn, "l0pkm": emit_l0pkm,
           "l0mlp": emit_l0mlp, "l1pre": emit_l1pre, "l1attn": emit_l1attn,
           "l1moe": emit_l1moe}
    produced = set()
    needs = set(SETUP_INS)
    outs = set()
    for sname in group:
        ins_s, outs_s = STAGE_IO[sname]
        for k in ins_s:
            if k not in produced:
                needs.add(k)
        produced |= set(outs_s)
        outs |= set(outs_s)
    io = {}
    for k in needs:
        io[k] = "in"
    for k in outs:
        io[k] = "internal" if k != "out" else "out"
    gi = FUSE_GROUPS.index(group)
    later_needs = set()
    for g2 in FUSE_GROUPS[gi + 1:]:
        for sname in g2:
            later_needs |= set(STAGE_IO[sname][0])
    for k in outs & later_needs:
        io[k] = "out"

    def stages(p, c, d):
        h0 = emit_setup(p, c, d, need_iota=True)
        for sname in group:
            c.push()
            ems[sname](p, c, d, h0)
            c.pop()
            p.barrier()
    return build_one(T, [stages], io), io


LAUNCHES = [
    ("l0pre", launch_l0pre,
     ["x", "Wqkv", "bqkv_pp", "bv_row", "Wqp", "keysT", "ident_bf", "consts"],
     ["xn_t", "xn_f", "qk0_f", "v0_t", "sims"]),
    ("l0attn", launch_l0attn,
     ["qk0_f", "v0_t", "Wproj", "bproj_row", "ident_bf", "consts"],
     ["attn0"]),
    ("l0pkm", launch_l0pkm,
     ["sims", "xn_t", "demb", "uemb", "ident_bf", "consts"],
     ["pkm"]),
    ("l0mlp", launch_l0mlp,
     ["xn_f", "x", "attn0", "pkm", "Wg0", "Wu0", "Wd0", "bg0_pp", "bu0_pp",
      "bd0_row", "ident_bf", "consts"],
     ["x1"]),
    ("l1pre", launch_l1pre,
     ["x1", "Wq1", "Wk1", "Wv1", "WrT", "rb_row", "ident_bf", "ident_f32", "consts"],
     ["xn1_f", "q1_f", "k1_f", "v1_t", "cw_t"]),
    ("l1attn", launch_l1attn,
     ["q1_f", "k1_f", "v1_t", "Wo1", "ident_bf", "consts"],
     ["attn1"]),
    ("l1moe", launch_l1moe,
     ["xn1_f", "cw_t", "x1", "attn1", "WgTs", "WuTs", "WdTs", "WgT", "WuT",
      "WdT", "ident_bf", "consts"],
     ["out"]),
]

_NC_CACHE = {}


def kernel(**inputs):
    sh, per_core = _prep(inputs)
    T = per_core[0]["x"].shape[0]
    ncores = len(per_core)
    if FUSED:
        state = [dict(pc) for pc in per_core]
        for group in FUSE_GROUPS:
            key = ("grp", tuple(group), T)
            if key not in _NC_CACHE:
                _NC_CACHE[key] = launch_group(T, group)
            nc, io = _NC_CACHE[key]
            in_maps = []
            for b in range(ncores):
                m = {}
                for k, kind in io.items():
                    if kind == "in":
                        m[k] = state[b][k] if k in state[b] else sh[k]
                in_maps.append(m)
            res = run_bass_kernel_spmd(nc, in_maps, core_ids=list(range(ncores)))
            for b in range(ncores):
                for k, kind in io.items():
                    if kind == "out":
                        state[b][k] = res.results[b][k]
        return np.stack([np.asarray(state[b]["out"], dtype=np.float32)
                         for b in range(ncores)], 0)
    state = [dict(pc) for pc in per_core]
    for name, builder, ins, outs in LAUNCHES:
        key = (name, T)
        if key not in _NC_CACHE:
            _NC_CACHE[key] = builder(T)
        nc = _NC_CACHE[key]
        in_maps = []
        for b in range(ncores):
            m = {}
            for k in ins:
                m[k] = state[b][k] if k in state[b] else sh[k]
            in_maps.append(m)
        res = run_bass_kernel_spmd(nc, in_maps, core_ids=list(range(ncores)))
        for b in range(ncores):
            for k in outs:
                state[b][k] = res.results[b][k]
    return np.stack([np.asarray(state[b]["out"], dtype=np.float32) for b in range(ncores)], 0)
